# version 12
# baseline (speedup 1.0000x reference)
import math
import numpy as np
import ml_dtypes
import concourse.bass as bass
import concourse.mybir as mybir
from concourse.bass_utils import run_bass_kernel_spmd

F32 = mybir.dt.float32
BF16 = mybir.dt.bfloat16
AF = mybir.ActivationFunctionType
ALU = mybir.AluOpType

B, T, E, H = 4, 4096, 2048, 16
HID, GUP = 8192, 16384
TOK = 2048
TALL = 4096
DC = E // 128
NTILE = TOK // 512
NTKC = TOK // 128
EPS_LN = 1e-5
EPS_RMS = 1.1920929e-7
ATT_SCALE = 1.0 / math.sqrt(128.0)
STW = 256

ENGS = ("t", "v", "c", "sy", "g")


class Prog:
    def __init__(self, nc):
        self.nc = nc
        self.q = {e: [] for e in ENGS}
        self.sems = {}
        self.cnt = {}
        self._guards = []

    def mksem(self, name):
        g = self.nc.semaphore(name)
        self.sems[name] = g.__enter__()
        self._guards.append(g)
        self.cnt[name] = 0

    def val(self, s):
        return self.cnt[s]

    def wait(self, eng, s, v):
        if v <= 0:
            return
        sem = self.sems[s]
        self.q[eng].append(lambda e, sem=sem, v=v: e.wait_ge(sem, v))

    def op(self, eng, build, inc=None, inc_n=1):
        if inc is None and eng in ("v", "c"):
            inc = eng
        if inc is not None:
            self.cnt[inc] += inc_n
            sem = self.sems[inc]

            def f(e, build=build, sem=sem, n=inc_n):
                build(e).then_inc(sem, n)
        else:
            def f(e, build=build):
                build(e)
        self.q[eng].append(f)

    def finalize(self):
        with self.nc.Block() as blk:
            for eng, meth in (("t", blk.tensor), ("v", blk.vector),
                              ("c", blk.scalar), ("sy", blk.sync), ("g", blk.gpsimd)):
                fns = self.q[eng]

                def body(e, fns=fns):
                    for f in fns:
                        f(e)
                meth(body)


class Slot:

    _n = [0]

    def __init__(self, p, dma=False):
        self.p = p
        self.rel = []
        self.sem = None
        if dma:
            Slot._n[0] += 1
            self.sem = f"sl{Slot._n[0]}"
            p.mksem(self.sem)

    def acquire(self, eng):
        for s, v in self.rel:
            self.p.wait(eng, s, v)
        self.rel = []

    def released(self, sem):
        self.rel.append((sem, self.p.val(sem)))

    def released_dma(self):
        self.rel.append((self.sem, self.p.val(self.sem)))

    def dma(self, eng, build):
        self.p.op(eng, build, inc=self.sem, inc_n=16)

    def ready(self):
        return self.p.val(self.sem)

    def wait_ready(self, eng):
        self.p.wait(eng, self.sem, self.p.val(self.sem))


class SbAlloc:
    def __init__(self, nc, base, limit):
        self.nc = nc
        self.off = base
        self.limit = limit
        self.marks = []

    def alloc(self, name, shape, dt):
        sz = int(np.prod(shape[1:])) * mybir.dt.size(dt)
        off = (self.off + 31) // 32 * 32
        assert off + sz <= self.limit, f"SBUF overflow {name}: {off}+{sz}>{self.limit}"
        h = self.nc.alloc_sbuf_tensor_at(name, list(shape), dt, offset=off)
        self.off = off + sz
        return h

    def push(self):
        self.marks.append(self.off)

    def pop(self):
        self.off = self.marks.pop()


def build_program(dbg=()):
    nc = bass.Bass("TRN2", target_bir_lowering=False)

    x_all = nc.dram_tensor("x_all", [TALL, E], F32, kind="ExternalInput").ap()
    Wq = nc.dram_tensor("Wq", [E, E], F32, kind="ExternalInput").ap()
    Wk_sh = nc.dram_tensor("Wk_sh", [E // 2, E], F32, kind="ExternalInput").ap()
    Wv_sh = nc.dram_tensor("Wv_sh", [E // 2, E], F32, kind="ExternalInput").ap()
    Wo = nc.dram_tensor("Wo", [E, E], F32, kind="ExternalInput").ap()
    bq_d = nc.dram_tensor("bq", [E, 1], F32, kind="ExternalInput").ap()
    bk_d = nc.dram_tensor("bk", [1, E // 2], F32, kind="ExternalInput").ap()
    bv_d = nc.dram_tensor("bv", [1, E // 2], F32, kind="ExternalInput").ap()
    bo_d = nc.dram_tensor("bo", [1, E], F32, kind="ExternalInput").ap()
    wgu_sh = nc.dram_tensor("wgu_sh", [GUP // 8, E], F32, kind="ExternalInput").ap()
    wp_sh = nc.dram_tensor("wp_sh", [E // 8, HID], F32, kind="ExternalInput").ap()
    out_d = nc.dram_tensor("out", [TOK, E], F32, kind="ExternalOutput").ap()

    agu_in = nc.dram_tensor("agu_in", [E, GUP // 8], BF16, kind="Internal").ap()
    agu_out = nc.dram_tensor("agu_out", [8 * E, GUP // 8], BF16, kind="Internal").ap()
    ap_in = nc.dram_tensor("ap_in", [HID, E // 8], BF16, kind="Internal").ap()
    ap_out = nc.dram_tensor("ap_out", [8 * HID, E // 8], BF16, kind="Internal").ap()
    akv_in = nc.dram_tensor("akv_in", [128, E // 2], F32, kind="Internal").ap()
    akv_out = nc.dram_tensor("akv_out", [256, E // 2], F32, kind="Internal").ap()

    wq_bf = nc.dram_tensor("wq_bf", [E, E], BF16, kind="Internal").ap()
    wo_bf = nc.dram_tensor("wo_bf", [E, E], BF16, kind="Internal").ap()
    xn_sp = nc.dram_tensor("xn_sp", [E, TOK], BF16, kind="Internal").ap()
    part_d = nc.dram_tensor("part_d", [TOK, E], F32, kind="Internal").ap()
    hsp = nc.dram_tensor("hsp", [HID, TOK], BF16, kind="Internal").ap()
    ss_d = nc.dram_tensor("ss_d", [4, 512], F32, kind="Internal").ap()

    dbg_specs = {
        "dbg_xn": ([E, TOK], BF16),
        "dbg_kv": ([128, E], F32),
        "dbg_q": ([128, DC * E], BF16),
        "dbg_attn": ([128, DC * E], BF16),
        "dbg_part": ([TOK, E], F32),
        "dbg_h": ([HID, TOK], BF16),
        "dbg_wqg": ([128, E], BF16),
        "dbg_inv": ([128, NTKC], F32),
    }
    dbg_aps = {}
    for name in dbg:
        shape, dt = dbg_specs[name]
        dbg_aps[name] = nc.dram_tensor(name, shape, dt, kind="ExternalOutput").ap()

    ident = nc.inline_tensor(np.eye(128, dtype=ml_dtypes.bfloat16), name="identc").ap()
    identf = nc.inline_tensor(np.eye(128, dtype=np.float32), name="identfc").ap()
    sel4_np = np.zeros((4, 128, 4), dtype=ml_dtypes.bfloat16)
    for i in range(4):
        sel4_np[i, :, i] = 1
    sel4 = [nc.inline_tensor(sel4_np[i], name=f"sel4_{i}").ap() for i in range(4)]
    ok_np = np.zeros((128, 128), dtype=ml_dtypes.bfloat16)
    ok_np[0, :] = 1
    ok_np[2, :] = 1
    ov_np = np.zeros((128, 128), dtype=ml_dtypes.bfloat16)
    ov_np[1, :] = 1
    ov_np[3, :] = 1
    ones_k = nc.inline_tensor(ok_np, name="oneskc").ap()
    ones_v = nc.inline_tensor(ov_np, name="onesvc").ap()

    p = Prog(nc)
    for s in ("t", "v", "c"):
        p.mksem(s)

    KB = 1024
    sb = SbAlloc(nc, base=18 * KB, limit=206 * KB)

    ident_sb = sb.alloc("ident_sb", [128, 128], BF16)
    identf_sb = sb.alloc("identf_sb", [128, 128], F32)
    sel4_sb = sb.alloc("sel4_sb", [128, 16], BF16)
    ones_k_sb = sb.alloc("ones_k_sb", [128, 128], BF16)
    ones_v_sb = sb.alloc("ones_v_sb", [128, 128], BF16)
    bq_sb = sb.alloc("bq_sb", [128, DC], F32)
    kv_acc = sb.alloc("kv_acc", [128, E // 2], F32)
    kvb = sb.alloc("kvb", [128, E], BF16)
    ssq = sb.alloc("ssq", [4, 512], F32)
    inv_raw = sb.alloc("inv_raw", [128, NTKC], F32)
    inv_tm = sb.alloc("inv_tm", [128, NTKC], F32)
    vbub = sb.alloc("vbub", [128, 8], F32)

    ps_guards = []

    def psum(name, shape, dt):
        g = nc.psum_tensor(name, shape, dt)
        h = g.__enter__()
        ps_guards.append(g)
        return h

    pmm = [psum(f"pmm{i}", [128, 512], F32) for i in range(4)]
    pxp = [psum(f"pxp{i}", [128, 1024], BF16) for i in range(2)]
    pkv = psum("pkv", [128, 1024], F32)

    pmm_s = [Slot(p) for _ in range(4)]
    pxp_s = [Slot(p) for _ in range(2)]
    pkvh_s = [Slot(p), Slot(p)]

    def vbubble():
        p.wait("v", "v", p.val("v"))

    def phase_barrier(slots, engines=("v", "c")):
        for eng in engines:
            for s_ in slots:
                if s_ is not None and s_.sem is not None:
                    p.wait(eng, s_.sem, p.val(s_.sem))

    cst = Slot(p, dma=True)
    cst.dma("sy", lambda e: e.dma_start(out=ident_sb[:, :], in_=ident[:, :]))
    cst.dma("sy", lambda e: e.dma_start(out=identf_sb[:, :], in_=identf[:, :]))
    for i in range(4):
        cst.dma("sy", lambda e, i=i: e.dma_start(out=sel4_sb[:, 4 * i:4 * (i + 1)], in_=sel4[i][:, :]))
    cst.dma("sy", lambda e: e.dma_start(out=ones_k_sb[:, :], in_=ones_k[:, :]))
    cst.dma("sy", lambda e: e.dma_start(out=ones_v_sb[:, :], in_=ones_v[:, :]))
    cst.dma("sy", lambda e: e.dma_start(
        out=bq_sb[:, :], in_=bass.AP(bq_d.tensor, 0, [[1, 128], [128, DC]]),
        allow_slow_non_contiguous=True))
    CST_ALL = cst.ready()

    wqbf_s = Slot(p, dma=True)
    wobf_s = Slot(p, dma=True)
    wqbf_s.dma("g", lambda e: e.dma_start(out=wq_bf[:, :], in_=Wq[:, :]))
    wobf_s.dma("g", lambda e: e.dma_start(out=wo_bf[:, :], in_=Wo[:, :]))

    xp_k = [0]

    def transpose_tiles(src_sb, n_ic, dst_sb, dst_off, dst_chunk_stride, f32=False):
        gsz = 4 if f32 else 8
        assert n_ic % gsz == 0
        idn = identf_sb if f32 else ident_sb
        for half in range(n_ic // gsz):
            k = xp_k[0]
            slot = pxp_s[k % 2]
            ps = pxp[k % 2][:, :].bitcast(F32) if f32 else pxp[k % 2][:, :]
            slot.acquire("t")
            for i in range(gsz):
                ic = half * gsz + i
                p.op("t", lambda e, ic=ic, ps=ps, i=i, src_sb=src_sb, idn=idn: e.transpose(
                    ps[:, i * 128:(i + 1) * 128],
                    src_sb[:, ic * 128:(ic + 1) * 128],
                    idn[:, :]), inc=("t" if i == gsz - 1 else None))
            p.wait("c", "t", p.val("t"))
            dst = bass.AP(dst_sb, dst_off + (half * gsz) * dst_chunk_stride,
                          [[dst_sb.shape[1], 128], [dst_chunk_stride, gsz], [1, 128]])
            p.op("c", lambda e, ps=ps, dst=dst, gsz=gsz: e.copy(
                dst, ps[:, 0:gsz * 128]), inc="c")
            slot.released("c")
            xp_k[0] += 1

    def transpose_weight_g(w_ap, wt_sb, O, I, nat_bufs, nat_slots):
        nb = len(nat_bufs)
        for rc in range(O // 128):
            slot = rc % nb
            nat_slots[slot].acquire("g")
            nat_slots[slot].dma("g", lambda e, rc=rc, slot=slot, w_ap=w_ap: e.dma_start(
                out=nat_bufs[slot][:, :], in_=w_ap[rc * 128:(rc + 1) * 128, :]))
            nat_slots[slot].wait_ready("t")
            transpose_tiles(nat_bufs[slot], I // 128, wt_sb, rc * 128, O, f32=False)
            nat_slots[slot].released("t")

    def transpose_weight(w_ap, wt_sb, O, I, nat_bufs, nat_slots, f32=True):
        nb = len(nat_bufs)
        for rc in range(O // 128):
            slot = rc % nb
            nat_slots[slot].acquire("sy")
            nat_slots[slot].dma("sy", lambda e, rc=rc, slot=slot, w_ap=w_ap: e.dma_start(
                out=nat_bufs[slot][:, :], in_=w_ap[rc * 128:(rc + 1) * 128, :]))
            nat_slots[slot].wait_ready("t")
            transpose_tiles(nat_bufs[slot], I // 128, wt_sb, rc * 128, O, f32=f32)
            nat_slots[slot].released("t")

    sb.push()
    qnat = sb.alloc("qnat", [128, HID], BF16)
    q_ub = sb.alloc("q_ub", [128, 2048], BF16)
    q_col = sb.alloc("q_col", [128, 32], F32)
    q_e1 = sb.alloc("q_e1", [128, 512], F32)
    q_e2 = sb.alloc("q_e2", [128, 1536], F32)
    q_w1, q_w2 = q_e2[:, 0:1024], q_e2[:, 1024:1536]
    q_sg = sb.alloc("q_sg", [128, 2048], BF16)
    q_sexpb = sb.alloc("q_sexpb", [128, 2048], BF16)
    q_prb = sb.alloc("q_prb", [128, 2048], BF16)
    q_wq = [sb.alloc(f"q_wq{i}", [128, 2048], BF16) for i in range(2)]
    q_wT = [sb.alloc(f"q_wT{i}", [128, 2048], BF16) for i in range(2)]

    qnat_s = Slot(p, dma=True)
    qst_s = [Slot(p, dma=True), Slot(p, dma=True)]
    qeq_s = Slot(p)
    qsgn_s = Slot(p)
    qwq_s = [Slot(p), Slot(p)]
    qwT_s = [Slot(p), Slot(p)]
    q_pb = [0]

    def quant_piece_q(off):
        width = 2048
        G = width // 64
        qeq_s.acquire("v")
        qnat_s.wait_ready("v")
        srcw = qnat[:, off:off + width]
        p.op("v", lambda e, srcw=srcw: e.scalar_tensor_tensor(q_ub[:, :], srcw, -1.0, srcw, ALU.mult, ALU.max))
        u3 = q_ub[:, :].rearrange("p (g k) -> p g k", k=64)
        w13 = q_w1.rearrange("p (g k) -> p g k", k=32)
        w23 = q_w2.rearrange("p (g k) -> p g k", k=16)
        vbubble()
        p.op("v", lambda e: e.tensor_tensor(w13, u3[:, :, 0:32], u3[:, :, 32:64], ALU.add))
        vbubble()
        p.op("v", lambda e: e.tensor_tensor(w23, w13[:, :, 0:16], w13[:, :, 16:32], ALU.add))
        vbubble()
        p.op("v", lambda e: e.tensor_tensor(w13[:, :, 0:8], w23[:, :, 0:8], w23[:, :, 8:16], ALU.add))
        vbubble()
        p.op("v", lambda e: e.tensor_tensor(w23[:, :, 0:4], w13[:, :, 0:4], w13[:, :, 4:8], ALU.add))
        vbubble()
        p.op("v", lambda e: e.tensor_tensor(w13[:, :, 0:2], w23[:, :, 0:2], w23[:, :, 2:4], ALU.add))
        vbubble()
        p.op("v", lambda e: e.tensor_tensor(q_col[:, :], w13[:, :, 0:1].rearrange("p g k -> p (g k)"),
                                            w13[:, :, 1:2].rearrange("p g k -> p (g k)"), ALU.add))
        vbubble()
        p.op("v", lambda e: e.tensor_scalar(q_e1[:, 0:G], q_col[:, :], 1.0 / 64, 1e-8, ALU.mult, ALU.max))
        src_, dst_ = q_e1, q_e2
        for m in range(5):
            bs = 1 << m
            vbubble()
            s_in = src_[:, 0:G * bs].rearrange("p (g k) -> p g k", k=bs)
            d3 = dst_[:, 0:G * 2 * bs].rearrange("p (g k) -> p g k", k=2 * bs)
            p.op("v", lambda e, d3=d3, s_in=s_in, bs=bs: e.tensor_copy(d3[:, :, 0:bs], s_in))
            p.op("v", lambda e, d3=d3, s_in=s_in, bs=bs: e.tensor_copy(d3[:, :, bs:2 * bs], s_in))
            src_, dst_ = dst_, src_
        bs = 32
        vbubble()
        s_in = src_[:, 0:G * bs].rearrange("p (g k) -> p g k", k=bs)
        d3 = q_sexpb[:, :].rearrange("p (g k) -> p g k", k=2 * bs)
        p.op("v", lambda e, d3=d3, s_in=s_in: e.tensor_copy(d3[:, :, 0:bs], s_in))
        p.op("v", lambda e, d3=d3, s_in=s_in: e.tensor_copy(d3[:, :, bs:2 * bs], s_in))
        vbubble()
        p.op("v", lambda e: e.scalar_tensor_tensor(
            q_ub[:, :], q_ub[:, :], 2.0, q_sexpb[:, :], ALU.mult, ALU.is_gt))
        vbubble()
        p.op("v", lambda e: e.tensor_tensor(q_prb[:, :], q_ub[:, :], q_sexpb[:, :], ALU.mult))
        qsgn_s.acquire("c")
        qnat_s.wait_ready("c")
        p.op("c", lambda e, srcw=srcw: e.activation(q_sg[:, :], srcw, AF.Sign), inc="c")
        c_done = p.val("c")
        p.wait("v", "c", c_done)
        vbubble()
        pb = q_pb[0]
        qwq_s[pb].acquire("v")
        p.op("v", lambda e, pb=pb: e.tensor_tensor(q_wq[pb][:, :], q_prb[:, :], q_sg[:, :], ALU.mult), inc="v")
        qsgn_s.released("v")
        return p.val("v")

    tdefer = []

    def quant_xpose_store(out_dram_ap_fn):
        vv = p.val("v")
        pb = q_pb[0]
        q_pb[0] ^= 1

        def fn(vv=vv, pb=pb, out_dram_ap_fn=out_dram_ap_fn):
            p.wait("t", "v", vv)
            qwT_s[pb].acquire("c")
            transpose_tiles(q_wq[pb], 16, q_wT[pb], 0, 128)
            qwq_s[pb].released("t")
            p.wait("sy", "c", p.val("c"))
            qst_s[pb].acquire("sy")
            qst_s[pb].dma("sy", lambda e, pb=pb: out_dram_ap_fn(e, pb))
            qwT_s[pb].released(qst_s[pb].sem)
        tdefer.append(fn)

    def make_q_emitters():
        ems = []
        for ch in range(GUP // 8 // 128):
            def em(ch=ch):
                qnat_s.acquire("g")
                qnat_s.dma("g", lambda e, ch=ch: e.dma_start(
                    out=qnat[:, 0:E], in_=wgu_sh[ch * 128:(ch + 1) * 128, :]))
                quant_piece_q(0)
                qnat_s.released("v")
                qnat_s.released("c")
                qeq_s.released("v")
                dst = bass.AP(agu_in.tensor, ch * 128,
                              [[GUP // 8, 128], [128 * (GUP // 8), 16], [1, 128]])
                quant_xpose_store(lambda e, pb, dst=dst: e.dma_start(
                    out=dst, in_=q_wT[pb][:, :].rearrange("p (c j) -> p c j", j=128)))
            ems.append(em)
        for ch in range(E // 8 // 128):
            for piece in range(4):
                def em(ch=ch, piece=piece):
                    wpf_s.acquire("sy")
                    wpf_s.dma("sy", lambda e, ch=ch, piece=piece: e.dma_start(
                        out=wpf[0][:, :],
                        in_=wp_sh[ch * 128:(ch + 1) * 128, piece * 2048:(piece + 1) * 2048]))
                    qnat_s.acquire("c")
                    wpf_s.wait_ready("c")
                    p.op("c", lambda e: e.activation(qnat[:, 0:2048], wpf[0][:, :], AF.Identity), inc="c")
                    p.wait("c", "c", p.val("c"))
                    wpf_s.released("c")
                    p.wait("v", "c", p.val("c"))
                    quant_piece_q(0)
                    qnat_s.released("v")
                    qnat_s.released("c")
                    qeq_s.released("v")
                    dst = bass.AP(ap_in.tensor, (piece * 16 * 128) * (E // 8) + ch * 128,
                                  [[E // 8, 128], [128 * (E // 8), 16], [1, 128]])
                    quant_xpose_store(lambda e, pb, dst=dst: e.dma_start(
                        out=dst, in_=q_wT[pb][:, :].rearrange("p (c j) -> p c j", j=128)))
                ems.append(em)
        return ems

    q_emitters = make_q_emitters()
    agq_s = Slot(p, dma=True)
    rg8 = [[0, 1, 2, 3, 4, 5, 6, 7]]

    def emit_gu_trigger():
        p.wait("g", qst_s[0].sem, qst_s[0].ready())
        p.wait("g", qst_s[1].sem, qst_s[1].ready())
        p.op("g", lambda e: e.collective_compute(
            "AllGather", ALU.bypass, ins=[agu_in[:, :]], outs=[agu_out[:, :]],
            replica_groups=rg8), inc=agq_s.sem, inc_n=1)

    def emit_p_trigger():
        p.wait("g", qst_s[0].sem, qst_s[0].ready())
        p.wait("g", qst_s[1].sem, qst_s[1].ready())
        p.op("g", lambda e: e.collective_compute(
            "AllGather", ALU.bypass, ins=[ap_in[:, :]], outs=[ap_out[:, :]],
            replica_groups=rg8), inc=agq_s.sem, inc_n=1)

    sb.push()
    wkT = sb.alloc("wkT", [128, DC * (E // 2)], BF16)
    wvT = sb.alloc("wvT", [128, DC * (E // 2)], BF16)
    xb_off = sb.off
    xbuf = [sb.alloc(f"xbuf{i}", [128, E], F32) for i in range(2)]
    xb_end = sb.off
    sb.off = xb_off
    kvf = sb.alloc("kvf", [128, E], F32)
    sb.off = xb_end
    xn_tm = [sb.alloc(f"xn_tm{i}", [128, E], BF16) for i in range(2)]
    xn_st = sb.alloc("xn_st", [128, DC * STW], BF16)
    kbuf = sb.alloc("kbuf", [128, E // 2], BF16)
    vbuf = sb.alloc("vbuf", [128, E // 2], BF16)
    brow = sb.alloc("brow", [128, E // 2], BF16)
    blo = sb.alloc("blo", [128, E // 2], BF16)
    stats = sb.alloc("stats", [128, 24], F32)
    aggr = sb.alloc("aggr", [128, 2], F32)
    veps = sb.alloc("veps", [128, 1], F32)
    vrec = sb.alloc("vrec", [128, 1], F32)
    vinv = sb.alloc("vinv", [128, 1], F32)
    negm = sb.alloc("negm", [128, 1], F32)
    negmb = sb.alloc("negmb", [128, 1], F32)
    fb_off = sb.off
    fbias = sb.alloc("fbias", [128, E // 2], F32)
    natw = [sb.alloc(f"natw{i}", [128, E], BF16) for i in range(2)]
    fb_end = sb.off
    sb.off = fb_off
    wpf = [sb.alloc("wpf", [128, 2048], F32)]
    sb.off = fb_end
    natw_s = [Slot(p, dma=True), Slot(p, dma=True)]
    wpf_s = Slot(p, dma=True)
    xbuf_s = [Slot(p, dma=True), Slot(p, dma=True)]
    bias_s = Slot(p, dma=True)
    xnsp_s = Slot(p, dma=True)
    xn_tm_s = [Slot(p), Slot(p)]
    xn_st_s = Slot(p)
    kvbuf_s = Slot(p)

    phase_barrier([qnat_s, qst_s[0], qst_s[1]], engines=("v", "c", "sy", "t"))
    bias_s.dma("sy", lambda e: e.dma_start(out=fbias[0:1, :], in_=bk_d[:, :]))
    bias_s.dma("sy", lambda e: e.dma_start(out=fbias[1:2, :], in_=bv_d[:, :]))
    BIAS_ALL = bias_s.ready()

    p.wait("t", cst.sem, CST_ALL)
    transpose_weight_g(Wk_sh, wkT, E // 2, E, natw, natw_s)
    transpose_weight_g(Wv_sh, wvT, E // 2, E, natw, natw_s)
    NATW_T = p.val("t")

    p.op("v", lambda e: e.memset(kv_acc[:, :], 0.0))
    p.wait("v", bias_s.sem, BIAS_ALL)
    p.op("v", lambda e: e.memset(brow[:, :], 0.0))
    vbubble()
    p.op("v", lambda e: e.tensor_copy(brow[0:2, :], fbias[0:2, :]))
    vbubble()
    p.op("v", lambda e: e.tensor_tensor(blo[0:2, :], fbias[0:2, :], brow[0:2, :], ALU.subtract), inc="v")
    BIAS_V = p.val("v")
    p.wait("sy", "v", BIAS_V)
    p.wait("sy", "t", NATW_T)
    p.wait("sy", "v", BIAS_V)
    bias_s.dma("sy", lambda e: e.dma_start(out=brow[2:4, :], in_=blo[0:2, :]))
    p.wait("t", "v", BIAS_V)
    bias_s.wait_ready("t")

    apply_cval = {}
    negmb_cval = {}

    def emit_x_load(tcg):
        xsl = tcg % 2
        xbuf_s[xsl].acquire("sy")
        xbuf_s[xsl].dma("sy", lambda e, tcg=tcg, xsl=xsl: e.dma_start(
            out=xbuf[xsl][:, :], in_=x_all[tcg * 128:(tcg + 1) * 128, :]))

    def emit_ln(tcg):
        xsl = tcg % 2
        xbuf_s[xsl].wait_ready("v")
        for i in range(4):
            p.op("v", lambda e, i=i, xsl=xsl: e.bn_stats(
                stats[:, 6 * i:6 * (i + 1)], xbuf[xsl][:, 512 * i:512 * (i + 1)]))
        vbubble()
        p.op("v", lambda e: e.bn_aggr(aggr[:, :], stats[:, :]))
        vbubble()
        p.op("v", lambda e: e.tensor_scalar(veps[:, :], aggr[:, 1:2], 1.0, EPS_LN, ALU.mult, ALU.add))
        if tcg >= 1:
            p.wait("v", "c", negmb_cval[tcg - 1])
        vbubble()
        p.op("v", lambda e: e.tensor_scalar(negm[:, :], aggr[:, 0:1], -1.0, 0.0, ALU.mult, ALU.add))
        vbubble()
        p.op("v", lambda e: e.reciprocal(vrec[:, :], veps[:, :]), inc="v")
        p.wait("c", "v", p.val("v"))
        p.op("c", lambda e: e.activation(vinv[:, :], vrec[:, :], AF.Sqrt))
        p.wait("c", "c", p.val("c"))
        p.op("c", lambda e: e.activation(negmb[:, :], negm[:, :], AF.Identity,
                                         scale=vinv[:, 0:1]), inc="c")
        negmb_cval[tcg] = p.val("c")
        p.wait("c", "c", p.val("c"))
        xn_tm_s[xsl].acquire("c")
        xbuf_s[xsl].wait_ready("c")
        p.op("c", lambda e, xsl=xsl: e.activation(
            xn_tm[xsl][:, :], xbuf[xsl][:, :], AF.Identity,
            bias=negmb[:, 0:1], scale=vinv[:, 0:1]), inc="c")
        apply_cval[tcg] = p.val("c")
        xbuf_s[xsl].released("c")
        if tcg + 2 < 32:
            emit_x_load(tcg + 2)

    emit_x_load(0)
    emit_x_load(1)
    emit_ln(0)

    q_iter = iter(q_emitters)
    XN_EVAC_C = 0
    vdefer = []
    for st in range(16):
        xn_st_s.acquire("c")
        for tc in range(2):
            tcg = 2 * st + tc
            p.wait("t", "c", apply_cval[tcg])
            transpose_tiles(xn_tm[tcg % 2], DC, xn_st, tc * 128, STW)
            xn_tm_s[tcg % 2].released("t")
            if tc == 1:
                XN_EVAC_C = p.val("c")
            if tcg + 1 < 32:
                emit_ln(tcg + 1)
            if tc == 0:
                for fn in vdefer:
                    fn()
                vdefer = []
                for fn in tdefer:
                    fn()
                tdefer.clear()
                if st == 8:
                    emit_gu_trigger()
            else:
                em = next(q_iter, None)
                if em is not None:
                    em()
                if st < 8:
                    em = next(q_iter, None)
                    if em is not None:
                        em()

        if st < 8:
            p.wait("sy", "c", XN_EVAC_C)
            xnsp_s.acquire("sy")
            for dcc in range(DC):
                xnsp_s.dma("sy", lambda e, dcc=dcc, st=st: e.dma_start(
                    out=xn_sp[dcc * 128:(dcc + 1) * 128, st * STW:(st + 1) * STW],
                    in_=xn_st[:, dcc * STW:(dcc + 1) * STW]))
            xn_st_s.released(xnsp_s.sem)

        p.wait("t", "c", XN_EVAC_C)
        for tc in range(2):
            if tc == 1:
                for fn in vdefer:
                    fn()
                vdefer = []
            kvbuf_s.acquire("c")
            for ft in range(2):
                for which in ("k", "v"):
                    si = (ft % 2) if which == "k" else (2 + ft % 2)
                    psl = pmm[si]
                    pmm_s[si].acquire("t")
                    wt = wkT if which == "k" else wvT
                    osel = ones_k_sb if which == "k" else ones_v_sb
                    for dcc in range(DC):
                        p.op("t", lambda e, dcc=dcc, tc=tc, ft=ft, psl=psl, wt=wt: e.matmul(
                            psl[:, :],
                            lhsT=xn_st[:, dcc * STW + tc * 128: dcc * STW + (tc + 1) * 128],
                            rhs=wt[:, dcc * (E // 2) + ft * 512: dcc * (E // 2) + (ft + 1) * 512],
                            start=(dcc == 0), stop=False))
                    p.op("t", lambda e, psl=psl, osel=osel, ft=ft: e.matmul(
                        psl[:, :], lhsT=osel[:, :],
                        rhs=brow[:, ft * 512:(ft + 1) * 512],
                        start=False, stop=True), inc="t")
                    p.wait("c", "t", p.val("t"))
                    dstb = kbuf if which == "k" else vbuf
                    p.op("c", lambda e, psl=psl, dstb=dstb, ft=ft: e.activation(
                        dstb[:, ft * 512:(ft + 1) * 512], psl[:, :], AF.Identity), inc="c")
                    pmm_s[si].released("c")
            p.wait("t", "c", p.val("c"))
            for hq in range(2):
                pkvh_s[hq].acquire("t")
                for i in range(4):
                    h = hq * 4 + i
                    p.op("t", lambda e, h=h, i=i, hq=hq: e.matmul(
                        pkv[:, hq * 512 + i * 128: hq * 512 + (i + 1) * 128],
                        lhsT=kbuf[:, h * 128:(h + 1) * 128],
                        rhs=vbuf[:, h * 128:(h + 1) * 128],
                        start=True, stop=True), inc=("t" if i == 3 else None))
                tval = p.val("t")

                def kv_add(hq=hq, tval=tval):
                    p.wait("v", "t", tval)
                    vbubble()
                    p.op("v", lambda e, hq=hq: e.tensor_tensor(
                        kv_acc[:, hq * 512:(hq + 1) * 512],
                        kv_acc[:, hq * 512:(hq + 1) * 512],
                        pkv[:, hq * 512:(hq + 1) * 512], ALU.add), inc="v")
                    pkvh_s[hq].released("v")
                vdefer.append(kv_add)
            kvbuf_s.released("t")
        xn_st_s.released("t")

    for fn in vdefer:
        fn()
    vdefer = []
    for em in q_iter:
        em()
    for fn in tdefer:
        fn()
    tdefer.clear()

    AB_V = p.val("v")
    AB_T = p.val("t")
    XNSP_ALL = xnsp_s.ready()

    dbg_s = Slot(p, dma=True) if dbg_aps else None
    if "dbg_xn" in dbg_aps:
        p.wait("sy", xnsp_s.sem, XNSP_ALL)
        dbg_s.dma("sy", lambda e: e.dma_start(out=dbg_aps["dbg_xn"][:, :], in_=xn_sp[:, :]))

    sb.pop()
    sb.pop()

    akvw_s = Slot(p, dma=True)
    akvr_s = Slot(p, dma=True)
    agkv_s = Slot(p, dma=True)
    p.wait("g", "v", AB_V)
    akvw_s.dma("g", lambda e: e.dma_start(out=akv_in[:, :], in_=kv_acc[:, :]))
    p.wait("g", akvw_s.sem, akvw_s.ready())
    rg2 = [[0, 1], [2, 3], [4, 5], [6, 7]]
    p.op("g", lambda e: e.collective_compute(
        "AllGather", ALU.bypass, ins=[akv_in[:, :]], outs=[akv_out[:, :]],
        replica_groups=rg2), inc=agkv_s.sem, inc_n=1)
    p.wait("g", agkv_s.sem, p.val(agkv_s.sem))
    for r in range(2):
        akvr_s.dma("g", lambda e, r=r: e.dma_start(
            out=kvf[:, r * 1024:(r + 1) * 1024],
            in_=akv_out[r * 128:(r + 1) * 128, :]))
    emit_p_trigger()
    AGW_DONE = p.val(agq_s.sem)
    akvr_s.wait_ready("v")
    if "dbg_kv" in dbg_aps:
        p.wait("sy", akvr_s.sem, akvr_s.ready())
        dbg_s.dma("sy", lambda e: e.dma_start(out=dbg_aps["dbg_kv"][:, :], in_=kvf[:, :]))
    p.op("v", lambda e: e.tensor_scalar(kvb[:, :], kvf[:, :], ATT_SCALE, 0.0, ALU.mult, ALU.add), inc="v")
    KVB_V = p.val("v")

    sb.push()
    R1_off = sb.off
    wqT = sb.alloc("wqT", [128, DC * E], BF16)
    R2_off = sb.off
    q_own = sb.alloc("q_own", [128, DC * E], BF16)
    xnt = [sb.alloc(f"xnt{i}", [128, DC * 512], BF16) for i in range(2)]
    natw1 = [sb.alloc(f"natw1_{i}", [128, E], BF16) for i in range(2)]
    natw1_s = [Slot(p, dma=True) for _ in range(2)]
    xnt_s = [Slot(p, dma=True), Slot(p, dma=True)]

    p.wait("sy", "t", AB_T)
    p.wait("sy", "v", KVB_V)
    p.wait("sy", "c", p.val("c"))
    p.wait("sy", wqbf_s.sem, wqbf_s.ready())
    p.wait("c", "t", AB_T)
    p.wait("c", "v", AB_V)
    p.wait("c", cst.sem, CST_ALL)
    phase_barrier([qnat_s, qst_s[0], qst_s[1]])
    phase_barrier([xnsp_s, xbuf_s[0], xbuf_s[1], natw_s[0], natw_s[1], bias_s, wpf_s])

    transpose_weight(wq_bf, wqT, E, E, natw1, natw1_s, f32=False)
    p.wait("c", "v", KVB_V)
    p.wait("t", "c", p.val("c"))

    for tile in range(NTILE):
        slot = tile % 2
        xnt_s[slot].acquire("sy")
        p.wait("sy", xnsp_s.sem, XNSP_ALL)
        for dcc in range(DC):
            xnt_s[slot].dma("sy", lambda e, dcc=dcc, tile=tile, slot=slot: e.dma_start(
                out=xnt[slot][:, dcc * 512:(dcc + 1) * 512],
                in_=xn_sp[dcc * 128:(dcc + 1) * 128, tile * 512:(tile + 1) * 512]))
        xnt_s[slot].wait_ready("t")
        for oc in range(DC):
            si = oc % 4
            psl = pmm[si]
            pmm_s[si].acquire("t")
            for dcc in range(DC):
                p.op("t", lambda e, dcc=dcc, oc=oc, slot=slot, psl=psl: e.matmul(
                    psl[:, :],
                    lhsT=wqT[:, dcc * E + oc * 128: dcc * E + (oc + 1) * 128],
                    rhs=xnt[slot][:, dcc * 512:(dcc + 1) * 512],
                    start=(dcc == 0), stop=(dcc == DC - 1)),
                    inc=("t" if dcc == DC - 1 else None))
            p.wait("c", "t", p.val("t"))
            p.op("c", lambda e, psl=psl, oc=oc, tile=tile: e.activation(
                q_own[:, oc * E + tile * 512: oc * E + (tile + 1) * 512],
                psl[:, :], AF.Identity, bias=bq_sb[:, oc:oc + 1], scale=1.0), inc="c")
            pmm_s[si].released("c")
        xnt_s[slot].released("t")
    C1_C = p.val("c")
    C1_T = p.val("t")

    if "dbg_q" in dbg_aps:
        p.wait("sy", "c", C1_C)
        dbg_s.dma("sy", lambda e: e.dma_start(out=dbg_aps["dbg_q"][:, :], in_=q_own[:, :]))

    sb.pop()

    sb.push()
    assert sb.off == R1_off
    a_own = sb.alloc("a_own", [128, DC * E], BF16)
    assert sb.off == R2_off
    sb.off = R2_off + DC * E * 2
    q_own2 = q_own

    p.wait("t", "v", KVB_V)
    p.wait("t", "c", C1_C)
    for tile in range(NTILE):
        for h in range(H):
            si = (tile * H + h) % 4
            psl = pmm[si]
            pmm_s[si].acquire("t")
            p.op("t", lambda e, h=h, tile=tile, psl=psl: e.matmul(
                psl[:, :],
                lhsT=kvb[:, h * 128:(h + 1) * 128],
                rhs=q_own2[:, h * E + tile * 512: h * E + (tile + 1) * 512],
                start=True, stop=True), inc="t")
            p.wait("c", "t", p.val("t"))
            p.op("c", lambda e, psl=psl, h=h, tile=tile: e.copy(
                a_own[:, h * E + tile * 512: h * E + (tile + 1) * 512], psl[:, :]), inc="c")
            pmm_s[si].released("c")
    C2_V = p.val("c")

    if "dbg_attn" in dbg_aps:
        p.wait("sy", "c", C2_V)
        dbg_s.dma("sy", lambda e: e.dma_start(out=dbg_aps["dbg_attn"][:, :], in_=a_own[:, :]))

    sb.off = R2_off
    woT = sb.alloc("woT", [128, DC * E], BF16)
    bo_b = sb.alloc("bo_b", [128, E], F32)
    xchunk = [sb.alloc("xchunk0", [128, E], F32)]
    stg_a = [sb.alloc(f"stg_a{i}", [128, 512], F32) for i in range(4)]
    pstage = sb.alloc("pstage0", [128, E], F32)
    natw2 = [sb.alloc(f"natw2_{i}", [128, E], BF16) for i in range(2)]
    natw2_s = [Slot(p, dma=True), Slot(p, dma=True)]
    xchunk_s = [Slot(p, dma=True)]
    pstg_s = Slot(p, dma=True)

    p.wait("g", "t", p.val("t"))
    p.wait("sy", "t", C1_T)
    phase_barrier([xnt_s[0], xnt_s[1], natw1_s[0], natw1_s[1]])
    bias_s.dma("sy", lambda e: e.dma_start(out=bo_b[:, :], in_=bo_d[:, :].broadcast_to([128, E])))
    BO_ALL = bias_s.ready()
    p.wait("sy", wobf_s.sem, wobf_s.ready())
    transpose_weight(wo_bf, woT, E, E, natw2, natw2_s, f32=False)
    p.wait("t", "c", p.val("c"))
    p.wait("v", bias_s.sem, BO_ALL)

    for tkc in range(NTKC):
        slot = 0
        xchunk_s[slot].acquire("sy")
        xchunk_s[slot].dma("sy", lambda e, tkc=tkc, slot=slot: e.dma_start(
            out=xchunk[slot][:, :], in_=x_all[tkc * 128:(tkc + 1) * 128, :]))
        for ot in range(4):
            pmm_s[ot].acquire("t")
            for dcc in range(DC):
                p.op("t", lambda e, dcc=dcc, tkc=tkc, ot=ot: e.matmul(
                    pmm[ot][:, :],
                    lhsT=a_own[:, dcc * E + tkc * 128: dcc * E + (tkc + 1) * 128],
                    rhs=woT[:, dcc * E + ot * 512: dcc * E + (ot + 1) * 512],
                    start=(dcc == 0), stop=(dcc == DC - 1)),
                    inc=("t" if dcc == DC - 1 else None))
        p.wait("v", "t", p.val("t"))
        xchunk_s[slot].wait_ready("v")
        pstg_s.acquire("v")
        for ot in range(4):
            p.op("v", lambda e, ot=ot: e.tensor_tensor(
                stg_a[ot][:, :], pmm[ot][:, :], bo_b[:, ot * 512:(ot + 1) * 512], ALU.add),
                inc=("v" if ot == 3 else None))
        for ot in range(4):
            pmm_s[ot].released("v")
        vbubble()
        for ot in range(4):
            p.op("v", lambda e, ot=ot, slot=slot: e.tensor_tensor(
                pstage[:, ot * 512:(ot + 1) * 512], stg_a[ot][:, :],
                xchunk[slot][:, ot * 512:(ot + 1) * 512], ALU.add),
                inc=("v" if ot == 3 else None))
        xchunk_s[slot].released("v")
        p.wait("sy", "v", p.val("v"))
        pstg_s.dma("sy", lambda e, tkc=tkc: e.dma_start(
            out=part_d[tkc * 128:(tkc + 1) * 128, :], in_=pstage[:, :]))
        pstg_s.released_dma()
    PSTG_ALL = pstg_s.ready()
    C3_T = p.val("t")

    if "dbg_part" in dbg_aps:
        p.wait("sy", pstg_s.sem, PSTG_ALL)
        dbg_s.dma("sy", lambda e: e.dma_start(out=dbg_aps["dbg_part"][:, :], in_=part_d[:, :]))

    sb.pop()

    sb.push()
    xn_own = sb.alloc("xn_own", [128, DC * TOK], BF16)
    wgT = [sb.alloc(f"wgT{i}", [128, E], BF16) for i in range(2)]
    wuT = [sb.alloc(f"wuT{i}", [128, E], BF16) for i in range(2)]
    sil = [sb.alloc(f"sil{i}", [128, 512], F32) for i in range(2)]
    h_stb = [sb.alloc(f"h_st{i}", [128, 512], BF16) for i in range(4)]
    hsq_stb = [sb.alloc(f"hsq_st{i}", [128, 512], BF16) for i in range(4)]

    gu_nat_s = [Slot(p, dma=True), Slot(p, dma=True)]
    xno_s = Slot(p, dma=True)
    hst_s = [Slot(p, dma=True) for _ in range(4)]
    wT_s = [Slot(p), Slot(p)]
    sil_s = [Slot(p), Slot(p)]
    hsq_s = [Slot(p) for _ in range(4)]

    p.wait("g", "t", C3_T)
    p.wait("g", "v", p.val("v"))
    phase_barrier([pstg_s, xchunk_s[0], natw2_s[0], natw2_s[1], bias_s])
    p.wait("sy", agq_s.sem, AGW_DONE)
    p.wait("sy", xnsp_s.sem, XNSP_ALL)
    for dcc in range(DC):
        xno_s.dma("sy", lambda e, dcc=dcc: e.dma_start(
            out=xn_own[:, dcc * TOK:(dcc + 1) * TOK],
            in_=xn_sp[dcc * 128:(dcc + 1) * 128, :]))
    xno_s.wait_ready("t")
    p.wait("t", "t", C3_T)

    ss_started = [False]
    ss_last_t = [0]
    WT_READY = {}

    SH = GUP // 8

    def _guT_src(h_global):
        c0, ch = divmod(h_global, 16)
        return bass.AP(agu_out.tensor, (c0 * E) * SH + ch * 128,
                       [[SH, 128], [128 * SH, DC], [1, 128]])

    def emit_gu_load_quant(pc):
        slot = pc % 2
        wT_s[slot].acquire("sy")
        gu_nat_s[slot].acquire("sy")
        gu_nat_s[slot].dma("sy", lambda e, pc=pc, slot=slot: e.dma_start(
            out=wgT[slot][:, :].rearrange("p (c j) -> p c j", j=128),
            in_=_guT_src(pc)))
        gu_nat_s[slot].dma("sy", lambda e, pc=pc, slot=slot: e.dma_start(
            out=wuT[slot][:, :].rearrange("p (c j) -> p c j", j=128),
            in_=_guT_src(64 + pc)))
        WT_READY[pc] = gu_nat_s[slot].ready()

    def emit_gu_mm(qc):
        slot = qc % 2
        hsq_counts = []
        p.wait("t", gu_nat_s[slot].sem, WT_READY[qc])
        for tile in range(4):
            sig, siu = tile % 2, 2 + tile % 2
            psg, psu = pmm[sig], pmm[siu]
            pmm_s[sig].acquire("t")
            pmm_s[siu].acquire("t")
            for dcc in range(DC):
                p.op("t", lambda e, dcc=dcc, tile=tile, psg=psg, slot=slot: e.matmul(
                    psg[:, :],
                    lhsT=wgT[slot][:, dcc * 128:(dcc + 1) * 128],
                    rhs=xn_own[:, dcc * TOK + tile * 512: dcc * TOK + (tile + 1) * 512],
                    start=(dcc == 0), stop=(dcc == DC - 1)),
                    inc=("t" if dcc == DC - 1 else None))
            TG = p.val("t")
            for dcc in range(DC):
                p.op("t", lambda e, dcc=dcc, tile=tile, psu=psu, slot=slot: e.matmul(
                    psu[:, :],
                    lhsT=wuT[slot][:, dcc * 128:(dcc + 1) * 128],
                    rhs=xn_own[:, dcc * TOK + tile * 512: dcc * TOK + (tile + 1) * 512],
                    start=(dcc == 0), stop=(dcc == DC - 1)),
                    inc=("t" if dcc == DC - 1 else None))
            TU = p.val("t")
            p.wait("c", "t", TG)
            sil_s[tile % 2].acquire("c")
            p.op("c", lambda e, psg=psg, tile=tile: e.activation(
                sil[tile % 2][:, :], psg[:, :], AF.Silu), inc="c")
            SIL_C = p.val("c")
            pmm_s[sig].released("c")
            hslot = tile
            p.wait("v", "t", TU)
            p.wait("v", "c", SIL_C)
            h_s_cur = hst_s[hslot]
            h_s_cur.acquire("v")
            p.op("v", lambda e, tile=tile, psu=psu, hslot=hslot: e.tensor_tensor(
                h_stb[hslot][:, :], sil[tile % 2][:, :], psu[:, :], ALU.mult), inc="v")
            HB_V = p.val("v")
            sil_s[tile % 2].released("v")
            pmm_s[siu].released("v")
            vbubble()
            hsq_s[tile].acquire("v")
            p.op("v", lambda e, hslot=hslot, tile=tile: e.tensor_tensor(
                hsq_stb[tile][:, :], h_stb[hslot][:, :], h_stb[hslot][:, :], ALU.mult), inc="v")
            hsq_counts.append(p.val("v"))
            p.wait("sy", "v", HB_V)
            h_s_cur.dma("sy", lambda e, qc=qc, tile=tile, hslot=hslot: e.dma_start(
                out=hsp[qc * 128:(qc + 1) * 128, tile * 512:(tile + 1) * 512],
                in_=h_stb[hslot][:, :]))
            h_s_cur.released_dma()
        for tile in range(4):
            p.wait("t", "v", hsq_counts[tile])
            first = not ss_started[0]
            ss_started[0] = True
            p.op("t", lambda e, tile=tile, first=first, last=(qc == 63 and tile == 3): e.matmul(
                pkv[0:4, 512:1024],
                lhsT=sel4_sb[:, 4 * tile:4 * (tile + 1)],
                rhs=hsq_stb[tile][:, :],
                start=first, stop=last), inc="t")
            ss_last_t[0] = p.val("t")
            hsq_s[tile].released("t")
        wT_s[slot].released("t")

    for pc in range(65):
        if pc < 64:
            emit_gu_load_quant(pc)
        if pc >= 1:
            emit_gu_mm(pc - 1)

    HST_ALL = [s.ready() for s in hst_s]

    if "dbg_h" in dbg_aps:
        for s, vtot in zip(hst_s, HST_ALL):
            p.wait("sy", s.sem, vtot)
        dbg_s.dma("sy", lambda e: e.dma_start(out=dbg_aps["dbg_h"][:, :], in_=hsp[:, :]))

    ssd_s = Slot(p, dma=True)
    invr_s = Slot(p, dma=True)
    p.wait("v", "t", ss_last_t[0])
    p.op("v", lambda e: e.tensor_copy(ssq[:, :], pkv[0:4, 512:1024]), inc="v")
    p.wait("sy", "v", p.val("v"))
    ssd_s.dma("sy", lambda e: e.dma_start(out=ss_d[:, :], in_=ssq[:, :]))
    ssd_s.wait_ready("sy")
    invr_s.dma("sy", lambda e: e.dma_start(
        out=inv_raw[:, :], in_=bass.AP(ss_d.tensor, 0, [[1, 128], [128, NTKC]]),
        allow_slow_non_contiguous=True))
    invr_s.wait_ready("v")
    p.op("v", lambda e: e.tensor_scalar(inv_raw[:, :], inv_raw[:, :], 1.0 / HID, EPS_RMS, ALU.mult, ALU.add))
    vbubble()
    p.op("v", lambda e: e.reciprocal(inv_raw[:, :], inv_raw[:, :]), inc="v")
    p.wait("c", "v", p.val("v"))
    p.op("c", lambda e: e.activation(inv_tm[:, :], inv_raw[:, :], AF.Sqrt), inc="c")
    INV_C = p.val("c")
    if "dbg_inv" in dbg_aps:
        p.wait("sy", "c", INV_C)
        dbg_s.dma("sy", lambda e: e.dma_start(out=dbg_aps["dbg_inv"][:, :], in_=inv_tm[:, :]))

    sb.pop()

    sb.push()
    wpT = [sb.alloc(f"wpT{i}", [128, 64 * 256], BF16) for i in range(2)]
    hpiece = [sb.alloc(f"hpiece{i}", [128, 64 * 256], BF16) for i in range(2)]
    pchunk = [sb.alloc(f"pchunk{i}", [128, 256], F32) for i in range(2)]
    stg1 = [sb.alloc(f"stg1{i}", [128, 256], F32) for i in range(2)]
    ostg = [sb.alloc(f"ostg{i}", [128, 256], F32) for i in range(2)]

    wpTl_s = [Slot(p, dma=True), Slot(p, dma=True)]
    hp_s = [Slot(p, dma=True), Slot(p, dma=True)]
    pch_s = [Slot(p, dma=True), Slot(p, dma=True)]
    ost_s = [Slot(p, dma=True), Slot(p, dma=True)]
    wpT_s = [Slot(p), Slot(p)]
    stg1_s = [Slot(p), Slot(p)]

    D_T_END = p.val("t")
    D_V_END = p.val("v")
    p.wait("g", "t", D_T_END)
    p.wait("g", "v", D_V_END)
    p.wait("c", "t", D_T_END)
    p.wait("c", "v", D_V_END)
    p.wait("sy", "t", D_T_END)
    p.wait("sy", "v", D_V_END)
    phase_barrier([gu_nat_s[0], gu_nat_s[1], hst_s[0], hst_s[1], hst_s[2],
                   hst_s[3], xno_s, ssd_s, invr_s])
    for s_, vtot in zip(hst_s, HST_ALL):
        p.wait("c", s_.sem, vtot)
    p.wait("sy", pstg_s.sem, PSTG_ALL)

    WPT_READY = {}

    def emit_wpT_load(dt_):
        dslot = dt_ % 2
        wpT_s[dslot].acquire("sy")
        wpTl_s[dslot].acquire("sy")
        wpTl_s[dslot].dma("sy", lambda e, dt_=dt_, dslot=dslot: e.dma_start(
            out=wpT[dslot][:, :].rearrange("p (c j) -> p c j", j=256),
            in_=bass.AP(ap_out.tensor, (dt_ * HID) * (E // 8),
                        [[E // 8, 128], [128 * (E // 8), 64], [1, 256]])))
        WPT_READY[dt_] = wpTl_s[dslot].ready()

    def emit_hp_load(pi):
        hsl = pi % 2
        hp_s[hsl].acquire("c")
        hp_s[hsl].dma("c", lambda e, pi=pi, hsl=hsl: e.dma_start(
            out=bass.AP(hpiece[hsl], 0, [[hpiece[hsl].shape[1], 128], [256, 64], [1, 256]]),
            in_=bass.AP(hsp.tensor, pi * 256, [[TOK, 128], [128 * TOK, 64], [1, 256]])))

    def emit_proj_pi(dt_, pi):
        dslot = dt_ % 2
        if pi == 0:
            emit_hp_load(0)
            p.wait("t", wpTl_s[dslot].sem, WPT_READY[dt_])
        if pi < 7:
            emit_hp_load(pi + 1)
        hsl = pi % 2
        for sub in range(2):
            tkc = pi * 2 + sub
            si = tkc % 4
            pmm_s[si].acquire("t")
            hp_s[hsl].wait_ready("t")
            for hc in range(64):
                p.op("t", lambda e, hc=hc, sub=sub, si=si, hsl=hsl, dslot=dslot: e.matmul(
                    pmm[si][:, 0:256],
                    lhsT=hpiece[hsl][:, hc * 256 + sub * 128: hc * 256 + (sub + 1) * 128],
                    rhs=wpT[dslot][:, hc * 256:(hc + 1) * 256],
                    start=(hc == 0), stop=(hc == 63)),
                    inc=("t" if hc == 63 else None))
            T_PE = p.val("t")
            pch_s[sub].acquire("sy")
            pch_s[sub].dma("sy", lambda e, tkc=tkc, dt_=dt_, sub=sub: e.dma_start(
                out=pchunk[sub][:, :],
                in_=part_d[tkc * 128:(tkc + 1) * 128, dt_ * 256:(dt_ + 1) * 256]))
            p.wait("c", "t", T_PE)
            stg1_s[sub].acquire("c")
            p.op("c", lambda e, si=si, tkc=tkc, sub=sub: e.activation(
                stg1[sub][:, :], pmm[si][:, 0:256], AF.Identity,
                bias=0.0, scale=inv_tm[:, tkc:tkc + 1]), inc="c")
            STG1_C = p.val("c")
            pmm_s[si].released("c")
            p.wait("v", "c", STG1_C)
            pch_s[sub].wait_ready("v")
            ost_s[sub].acquire("v")
            p.op("v", lambda e, sub=sub: e.tensor_tensor(
                ostg[sub][:, :], stg1[sub][:, :], pchunk[sub][:, :], ALU.add), inc="v")
            stg1_s[sub].released("v")
            pch_s[sub].released("v")
            p.wait("sy", "v", p.val("v"))
            ost_s[sub].dma("sy", lambda e, tkc=tkc, dt_=dt_, sub=sub: e.dma_start(
                out=out_d[tkc * 128:(tkc + 1) * 128, dt_ * 256:(dt_ + 1) * 256],
                in_=ostg[sub][:, :]))
            ost_s[sub].released_dma()
        hp_s[hsl].released("t")
        if pi == 7:
            wpT_s[dslot].released("t")
            wpTl_s[dslot].released("t")

    for dt_ in range(9):
        if dt_ < 8:
            emit_wpT_load(dt_)
        if dt_ >= 1:
            for pi in range(8):
                emit_proj_pi(dt_ - 1, pi)

    for s_ in ost_s:
        s_.wait_ready("sy")
    if dbg_s is not None:
        dbg_s.wait_ready("sy")
    sb.pop()

    p.finalize()
    return nc


_NC_CACHE = {}


def _get_nc(dbg=()):
    key = tuple(sorted(dbg))
    if key not in _NC_CACHE:
        _NC_CACHE[key] = build_program(dbg)
    return _NC_CACHE[key]


def make_in_maps(inputs):
    x = np.ascontiguousarray(np.asarray(inputs["L0_attn_x"], dtype=np.float32))
    ws = {k: np.ascontiguousarray(np.asarray(inputs[k], dtype=np.float32))
          for k in ("L0_attn_Wq", "L0_attn_Wk", "L0_attn_Wv", "L0_attn_Wo",
                    "L0_ffn_w_gate_up", "L0_ffn_w_proj")}
    bs = {k: np.ascontiguousarray(np.asarray(inputs[k], dtype=np.float32))
          for k in ("L0_attn_bq", "L0_attn_bk", "L0_attn_bv", "L0_attn_bo")}
    in_maps = []
    EH = E // 2
    for c in range(8):
        b, half = c // 2, c % 2
        own = x[b, half * TOK:(half + 1) * TOK, :]
        other = x[b, (1 - half) * TOK:(2 - half) * TOK, :]
        x_core = np.concatenate([own, other], axis=0)
        in_maps.append({
            "x_all": x_core,
            "Wq": ws["L0_attn_Wq"],
            "Wk_sh": np.ascontiguousarray(ws["L0_attn_Wk"][half * EH:(half + 1) * EH]),
            "Wv_sh": np.ascontiguousarray(ws["L0_attn_Wv"][half * EH:(half + 1) * EH]),
            "Wo": ws["L0_attn_Wo"],
            "bq": bs["L0_attn_bq"].reshape(E, 1),
            "bk": np.ascontiguousarray(bs["L0_attn_bk"].reshape(1, E)[:, half * EH:(half + 1) * EH]),
            "bv": np.ascontiguousarray(bs["L0_attn_bv"].reshape(1, E)[:, half * EH:(half + 1) * EH]),
            "bo": bs["L0_attn_bo"].reshape(1, E),
            "wgu_sh": np.ascontiguousarray(ws["L0_ffn_w_gate_up"][c * (GUP // 8):(c + 1) * (GUP // 8)]),
            "wp_sh": np.ascontiguousarray(ws["L0_ffn_w_proj"][c * (E // 8):(c + 1) * (E // 8)]),
        })
    return in_maps


def kernel(**inputs):
    nc = _get_nc()
    in_maps = make_in_maps(inputs)
    res = run_bass_kernel_spmd(nc, in_maps, core_ids=list(range(8)))
    out = np.empty((B, T, E), dtype=np.float32)
    for c in range(8):
        b, half = c // 2, c % 2
        out[b, half * TOK:(half + 1) * TOK, :] = np.asarray(res.results[c]["out"])
    return out


# version 13
# speedup vs baseline: 1.2001x; 1.2001x over previous
import math
import numpy as np
import ml_dtypes
import concourse.bass as bass
import concourse.mybir as mybir
from concourse.bass_utils import run_bass_kernel_spmd

F32 = mybir.dt.float32
BF16 = mybir.dt.bfloat16
AF = mybir.ActivationFunctionType
ALU = mybir.AluOpType

B, T, E, H = 4, 4096, 2048, 16
HID, GUP = 8192, 16384
TOK = 2048
TALL = 4096
DC = E // 128
NTILE = TOK // 512
NTKC = TOK // 128
EPS_LN = 1e-5
EPS_RMS = 1.1920929e-7
ATT_SCALE = 1.0 / math.sqrt(128.0)
STW = 256

ENGS = ("t", "v", "c", "sy", "g")


class Prog:
    def __init__(self, nc):
        self.nc = nc
        self.q = {e: [] for e in ENGS}
        self.sems = {}
        self.cnt = {}
        self._guards = []

    def mksem(self, name):
        g = self.nc.semaphore(name)
        self.sems[name] = g.__enter__()
        self._guards.append(g)
        self.cnt[name] = 0

    def val(self, s):
        return self.cnt[s]

    def wait(self, eng, s, v):
        if v <= 0:
            return
        sem = self.sems[s]
        self.q[eng].append(lambda e, sem=sem, v=v: e.wait_ge(sem, v))

    def op(self, eng, build, inc=None, inc_n=1):
        if inc is None and eng in ("v", "c"):
            inc = eng
        if inc is not None:
            self.cnt[inc] += inc_n
            sem = self.sems[inc]

            def f(e, build=build, sem=sem, n=inc_n):
                build(e).then_inc(sem, n)
        else:
            def f(e, build=build):
                build(e)
        self.q[eng].append(f)

    def finalize(self):
        with self.nc.Block() as blk:
            for eng, meth in (("t", blk.tensor), ("v", blk.vector),
                              ("c", blk.scalar), ("sy", blk.sync), ("g", blk.gpsimd)):
                fns = self.q[eng]

                def body(e, fns=fns):
                    for f in fns:
                        f(e)
                meth(body)


class Slot:

    _n = [0]

    def __init__(self, p, dma=False):
        self.p = p
        self.rel = []
        self.sem = None
        if dma:
            Slot._n[0] += 1
            self.sem = f"sl{Slot._n[0]}"
            p.mksem(self.sem)

    def acquire(self, eng):
        for s, v in self.rel:
            self.p.wait(eng, s, v)
        self.rel = []

    def released(self, sem):
        self.rel.append((sem, self.p.val(sem)))

    def released_dma(self):
        self.rel.append((self.sem, self.p.val(self.sem)))

    def dma(self, eng, build):
        self.p.op(eng, build, inc=self.sem, inc_n=16)

    def ready(self):
        return self.p.val(self.sem)

    def wait_ready(self, eng):
        self.p.wait(eng, self.sem, self.p.val(self.sem))


class SbAlloc:
    def __init__(self, nc, base, limit):
        self.nc = nc
        self.off = base
        self.limit = limit
        self.marks = []

    def alloc(self, name, shape, dt):
        sz = int(np.prod(shape[1:])) * mybir.dt.size(dt)
        off = (self.off + 31) // 32 * 32
        assert off + sz <= self.limit, f"SBUF overflow {name}: {off}+{sz}>{self.limit}"
        h = self.nc.alloc_sbuf_tensor_at(name, list(shape), dt, offset=off)
        self.off = off + sz
        return h

    def push(self):
        self.marks.append(self.off)

    def pop(self):
        self.off = self.marks.pop()


def build_program(dbg=()):
    nc = bass.Bass("TRN2", target_bir_lowering=False)

    x_all = nc.dram_tensor("x_all", [TALL, E], F32, kind="ExternalInput").ap()
    Wq = nc.dram_tensor("Wq", [E, E], F32, kind="ExternalInput").ap()
    Wk_sh = nc.dram_tensor("Wk_sh", [E // 2, E], F32, kind="ExternalInput").ap()
    Wv_sh = nc.dram_tensor("Wv_sh", [E // 2, E], F32, kind="ExternalInput").ap()
    Wo = nc.dram_tensor("Wo", [E, E], F32, kind="ExternalInput").ap()
    bq_d = nc.dram_tensor("bq", [E, 1], F32, kind="ExternalInput").ap()
    bk_d = nc.dram_tensor("bk", [1, E // 2], F32, kind="ExternalInput").ap()
    bv_d = nc.dram_tensor("bv", [1, E // 2], F32, kind="ExternalInput").ap()
    bo_d = nc.dram_tensor("bo", [1, E], F32, kind="ExternalInput").ap()
    wgu_sh = nc.dram_tensor("wgu_sh", [GUP // 8, E], F32, kind="ExternalInput").ap()
    wp_sh = nc.dram_tensor("wp_sh", [E // 8, HID], F32, kind="ExternalInput").ap()
    out_d = nc.dram_tensor("out", [TOK, E], F32, kind="ExternalOutput").ap()

    agu_in = nc.dram_tensor("agu_in", [E, GUP // 8], BF16, kind="Internal").ap()
    agu_out = nc.dram_tensor("agu_out", [8 * E, GUP // 8], BF16, kind="Internal").ap()
    ap_in = nc.dram_tensor("ap_in", [HID, E // 8], BF16, kind="Internal").ap()
    ap_out = nc.dram_tensor("ap_out", [8 * HID, E // 8], BF16, kind="Internal").ap()
    akv_in = nc.dram_tensor("akv_in", [128, E // 2], F32, kind="Internal").ap()
    akv_out = nc.dram_tensor("akv_out", [256, E // 2], F32, kind="Internal").ap()

    wq_bf = nc.dram_tensor("wq_bf", [E, E], BF16, kind="Internal").ap()
    wo_bf = nc.dram_tensor("wo_bf", [E, E], BF16, kind="Internal").ap()
    xn_sp = nc.dram_tensor("xn_sp", [E, TOK], BF16, kind="Internal").ap()
    part_d = nc.dram_tensor("part_d", [TOK, E], F32, kind="Internal").ap()
    hsp = nc.dram_tensor("hsp", [HID, TOK], BF16, kind="Internal").ap()
    ss_d = nc.dram_tensor("ss_d", [4, 512], F32, kind="Internal").ap()

    dbg_specs = {
        "dbg_xn": ([E, TOK], BF16),
        "dbg_kv": ([128, E], F32),
        "dbg_q": ([128, DC * E], BF16),
        "dbg_attn": ([128, DC * E], BF16),
        "dbg_part": ([TOK, E], F32),
        "dbg_h": ([HID, TOK], BF16),
        "dbg_wqg": ([128, E], BF16),
        "dbg_inv": ([128, NTKC], F32),
    }
    dbg_aps = {}
    for name in dbg:
        shape, dt = dbg_specs[name]
        dbg_aps[name] = nc.dram_tensor(name, shape, dt, kind="ExternalOutput").ap()

    ident = nc.inline_tensor(np.eye(128, dtype=ml_dtypes.bfloat16), name="identc").ap()
    identf = nc.inline_tensor(np.eye(128, dtype=np.float32), name="identfc").ap()
    sel4_np = np.zeros((4, 128, 4), dtype=ml_dtypes.bfloat16)
    for i in range(4):
        sel4_np[i, :, i] = 1
    sel4 = [nc.inline_tensor(sel4_np[i], name=f"sel4_{i}").ap() for i in range(4)]
    ok_np = np.zeros((128, 128), dtype=ml_dtypes.bfloat16)
    ok_np[0, :] = 1
    ok_np[2, :] = 1
    ov_np = np.zeros((128, 128), dtype=ml_dtypes.bfloat16)
    ov_np[1, :] = 1
    ov_np[3, :] = 1
    ones_k = nc.inline_tensor(ok_np, name="oneskc").ap()
    ones_v = nc.inline_tensor(ov_np, name="onesvc").ap()

    p = Prog(nc)
    for s in ("t", "v", "c"):
        p.mksem(s)

    KB = 1024
    sb = SbAlloc(nc, base=18 * KB, limit=206 * KB)

    ident_sb = sb.alloc("ident_sb", [128, 128], BF16)
    identf_sb = sb.alloc("identf_sb", [128, 128], F32)
    sel4_sb = sb.alloc("sel4_sb", [128, 16], BF16)
    ones_k_sb = sb.alloc("ones_k_sb", [128, 128], BF16)
    ones_v_sb = sb.alloc("ones_v_sb", [128, 128], BF16)
    bq_sb = sb.alloc("bq_sb", [128, DC], F32)
    kv_acc = sb.alloc("kv_acc", [128, E // 2], F32)
    kvb = sb.alloc("kvb", [128, E], BF16)
    ssq = sb.alloc("ssq", [4, 512], F32)
    inv_raw = sb.alloc("inv_raw", [128, NTKC], F32)
    inv_tm = sb.alloc("inv_tm", [128, NTKC], F32)
    vbub = sb.alloc("vbub", [128, 8], F32)

    ps_guards = []

    def psum(name, shape, dt):
        g = nc.psum_tensor(name, shape, dt)
        h = g.__enter__()
        ps_guards.append(g)
        return h

    pmm = [psum(f"pmm{i}", [128, 512], F32) for i in range(4)]
    pxp = [psum(f"pxp{i}", [128, 1024], BF16) for i in range(2)]
    pkv = psum("pkv", [128, 1024], F32)

    pmm_s = [Slot(p) for _ in range(4)]
    pxp_s = [Slot(p) for _ in range(2)]
    pkvh_s = [Slot(p), Slot(p)]

    def vbubble():
        p.wait("v", "v", p.val("v"))

    def phase_barrier(slots, engines=("v", "c")):
        for eng in engines:
            for s_ in slots:
                if s_ is not None and s_.sem is not None:
                    p.wait(eng, s_.sem, p.val(s_.sem))

    cst = Slot(p, dma=True)
    cst.dma("sy", lambda e: e.dma_start(out=ident_sb[:, :], in_=ident[:, :]))
    cst.dma("sy", lambda e: e.dma_start(out=identf_sb[:, :], in_=identf[:, :]))
    for i in range(4):
        cst.dma("sy", lambda e, i=i: e.dma_start(out=sel4_sb[:, 4 * i:4 * (i + 1)], in_=sel4[i][:, :]))
    cst.dma("sy", lambda e: e.dma_start(out=ones_k_sb[:, :], in_=ones_k[:, :]))
    cst.dma("sy", lambda e: e.dma_start(out=ones_v_sb[:, :], in_=ones_v[:, :]))
    cst.dma("sy", lambda e: e.dma_start(
        out=bq_sb[:, :], in_=bass.AP(bq_d.tensor, 0, [[1, 128], [128, DC]]),
        allow_slow_non_contiguous=True))
    CST_ALL = cst.ready()

    wqbf_s = Slot(p, dma=True)
    wobf_s = Slot(p, dma=True)
    wqbf_s.dma("g", lambda e: e.dma_start(out=wq_bf[:, :], in_=Wq[:, :]))
    wobf_s.dma("g", lambda e: e.dma_start(out=wo_bf[:, :], in_=Wo[:, :]))

    xp_k = [0]

    def transpose_tiles(src_sb, n_ic, dst_sb, dst_off, dst_chunk_stride, f32=False):
        gsz = 4 if f32 else 8
        assert n_ic % gsz == 0
        idn = identf_sb if f32 else ident_sb
        for half in range(n_ic // gsz):
            k = xp_k[0]
            slot = pxp_s[k % 2]
            ps = pxp[k % 2][:, :].bitcast(F32) if f32 else pxp[k % 2][:, :]
            slot.acquire("t")
            for i in range(gsz):
                ic = half * gsz + i
                p.op("t", lambda e, ic=ic, ps=ps, i=i, src_sb=src_sb, idn=idn: e.transpose(
                    ps[:, i * 128:(i + 1) * 128],
                    src_sb[:, ic * 128:(ic + 1) * 128],
                    idn[:, :]), inc=("t" if i == gsz - 1 else None))
            p.wait("c", "t", p.val("t"))
            dst = bass.AP(dst_sb, dst_off + (half * gsz) * dst_chunk_stride,
                          [[dst_sb.shape[1], 128], [dst_chunk_stride, gsz], [1, 128]])
            p.op("c", lambda e, ps=ps, dst=dst, gsz=gsz: e.copy(
                dst, ps[:, 0:gsz * 128]), inc="c")
            slot.released("c")
            xp_k[0] += 1

    def transpose_weight_g(w_ap, wt_sb, O, I, nat_bufs, nat_slots):
        nb = len(nat_bufs)
        for rc in range(O // 128):
            slot = rc % nb
            nat_slots[slot].acquire("g")
            nat_slots[slot].dma("g", lambda e, rc=rc, slot=slot, w_ap=w_ap: e.dma_start(
                out=nat_bufs[slot][:, :], in_=w_ap[rc * 128:(rc + 1) * 128, :]))
            nat_slots[slot].wait_ready("t")
            transpose_tiles(nat_bufs[slot], I // 128, wt_sb, rc * 128, O, f32=False)
            nat_slots[slot].released("t")

    def transpose_weight(w_ap, wt_sb, O, I, nat_bufs, nat_slots, f32=True):
        nb = len(nat_bufs)
        for rc in range(O // 128):
            slot = rc % nb
            nat_slots[slot].acquire("sy")
            nat_slots[slot].dma("sy", lambda e, rc=rc, slot=slot, w_ap=w_ap: e.dma_start(
                out=nat_bufs[slot][:, :], in_=w_ap[rc * 128:(rc + 1) * 128, :]))
            nat_slots[slot].wait_ready("t")
            transpose_tiles(nat_bufs[slot], I // 128, wt_sb, rc * 128, O, f32=f32)
            nat_slots[slot].released("t")

    sb.push()
    qnat = sb.alloc("qnat", [128, HID], BF16)
    q_ub = sb.alloc("q_ub", [128, 2048], BF16)
    q_col = sb.alloc("q_col", [128, 32], F32)
    q_e1 = sb.alloc("q_e1", [128, 512], F32)
    q_e2 = sb.alloc("q_e2", [128, 1536], F32)
    q_w1, q_w2 = q_e2[:, 0:1024], q_e2[:, 1024:1536]
    q_sg = sb.alloc("q_sg", [128, 2048], BF16)
    q_sexpb = sb.alloc("q_sexpb", [128, 2048], BF16)
    q_prb = sb.alloc("q_prb", [128, 2048], BF16)
    q_wq = [sb.alloc(f"q_wq{i}", [128, 2048], BF16) for i in range(2)]
    q_wT = [sb.alloc(f"q_wT{i}", [128, 2048], BF16) for i in range(2)]

    qnat_s = Slot(p, dma=True)
    qst_s = [Slot(p, dma=True), Slot(p, dma=True)]
    qeq_s = Slot(p)
    qsgn_s = Slot(p)
    qwq_s = [Slot(p), Slot(p)]
    qwT_s = [Slot(p), Slot(p)]
    q_pb = [0]

    def quant_piece_q(off):
        width = 2048
        G = width // 64
        qeq_s.acquire("v")
        qnat_s.wait_ready("v")
        srcw = qnat[:, off:off + width]
        p.op("v", lambda e, srcw=srcw: e.scalar_tensor_tensor(q_ub[:, :], srcw, -1.0, srcw, ALU.mult, ALU.max))
        u3 = q_ub[:, :].rearrange("p (g k) -> p g k", k=64)
        w13 = q_w1.rearrange("p (g k) -> p g k", k=32)
        w23 = q_w2.rearrange("p (g k) -> p g k", k=16)
        vbubble()
        p.op("v", lambda e: e.tensor_tensor(w13, u3[:, :, 0:32], u3[:, :, 32:64], ALU.add))
        vbubble()
        p.op("v", lambda e: e.tensor_tensor(w23, w13[:, :, 0:16], w13[:, :, 16:32], ALU.add))
        vbubble()
        p.op("v", lambda e: e.tensor_tensor(w13[:, :, 0:8], w23[:, :, 0:8], w23[:, :, 8:16], ALU.add))
        vbubble()
        p.op("v", lambda e: e.tensor_tensor(w23[:, :, 0:4], w13[:, :, 0:4], w13[:, :, 4:8], ALU.add))
        vbubble()
        p.op("v", lambda e: e.tensor_tensor(w13[:, :, 0:2], w23[:, :, 0:2], w23[:, :, 2:4], ALU.add))
        vbubble()
        p.op("v", lambda e: e.tensor_tensor(q_col[:, :], w13[:, :, 0:1].rearrange("p g k -> p (g k)"),
                                            w13[:, :, 1:2].rearrange("p g k -> p (g k)"), ALU.add))
        vbubble()
        p.op("v", lambda e: e.tensor_scalar(q_e1[:, 0:G], q_col[:, :], 1.0 / 64, 1e-8, ALU.mult, ALU.max))
        src_, dst_ = q_e1, q_e2
        for m in range(5):
            bs = 1 << m
            vbubble()
            s_in = src_[:, 0:G * bs].rearrange("p (g k) -> p g k", k=bs)
            d3 = dst_[:, 0:G * 2 * bs].rearrange("p (g k) -> p g k", k=2 * bs)
            p.op("v", lambda e, d3=d3, s_in=s_in, bs=bs: e.tensor_copy(d3[:, :, 0:bs], s_in))
            p.op("v", lambda e, d3=d3, s_in=s_in, bs=bs: e.tensor_copy(d3[:, :, bs:2 * bs], s_in))
            src_, dst_ = dst_, src_
        bs = 32
        vbubble()
        s_in = src_[:, 0:G * bs].rearrange("p (g k) -> p g k", k=bs)
        d3 = q_sexpb[:, :].rearrange("p (g k) -> p g k", k=2 * bs)
        p.op("v", lambda e, d3=d3, s_in=s_in: e.tensor_copy(d3[:, :, 0:bs], s_in))
        p.op("v", lambda e, d3=d3, s_in=s_in: e.tensor_copy(d3[:, :, bs:2 * bs], s_in))
        vbubble()
        p.op("v", lambda e: e.scalar_tensor_tensor(
            q_ub[:, :], q_ub[:, :], 2.0, q_sexpb[:, :], ALU.mult, ALU.is_gt))
        vbubble()
        p.op("v", lambda e: e.tensor_tensor(q_prb[:, :], q_ub[:, :], q_sexpb[:, :], ALU.mult))
        qsgn_s.acquire("c")
        qnat_s.wait_ready("c")
        p.op("c", lambda e, srcw=srcw: e.activation(q_sg[:, :], srcw, AF.Sign), inc="c")
        c_done = p.val("c")
        p.wait("v", "c", c_done)
        vbubble()
        pb = q_pb[0]
        qwq_s[pb].acquire("v")
        p.op("v", lambda e, pb=pb: e.tensor_tensor(q_wq[pb][:, :], q_prb[:, :], q_sg[:, :], ALU.mult), inc="v")
        qsgn_s.released("v")
        return p.val("v")

    tdefer = []

    def quant_xpose_store(out_dram_ap_fn):
        vv = p.val("v")
        pb = q_pb[0]
        q_pb[0] ^= 1

        def fn(vv=vv, pb=pb, out_dram_ap_fn=out_dram_ap_fn):
            p.wait("t", "v", vv)
            qwT_s[pb].acquire("c")
            transpose_tiles(q_wq[pb], 16, q_wT[pb], 0, 128)
            qwq_s[pb].released("t")
            p.wait("sy", "c", p.val("c"))
            qst_s[pb].acquire("sy")
            qst_s[pb].dma("sy", lambda e, pb=pb: out_dram_ap_fn(e, pb))
            qwT_s[pb].released(qst_s[pb].sem)
        tdefer.append(fn)

    def make_q_emitters():
        ems = []
        for ch in range(GUP // 8 // 128):
            def em(ch=ch):
                qnat_s.acquire("g")
                qnat_s.dma("g", lambda e, ch=ch: e.dma_start(
                    out=qnat[:, 0:E], in_=wgu_sh[ch * 128:(ch + 1) * 128, :]))
                quant_piece_q(0)
                qnat_s.released("v")
                qnat_s.released("c")
                qeq_s.released("v")
                dst = bass.AP(agu_in.tensor, ch * 128,
                              [[GUP // 8, 128], [128 * (GUP // 8), 16], [1, 128]])
                quant_xpose_store(lambda e, pb, dst=dst: e.dma_start(
                    out=dst, in_=q_wT[pb][:, :].rearrange("p (c j) -> p c j", j=128)))
            ems.append(em)
        for ch in range(E // 8 // 128):
            for piece in range(4):
                def em(ch=ch, piece=piece):
                    wpf_s.acquire("sy")
                    wpf_s.dma("sy", lambda e, ch=ch, piece=piece: e.dma_start(
                        out=wpf[0][:, :],
                        in_=wp_sh[ch * 128:(ch + 1) * 128, piece * 2048:(piece + 1) * 2048]))
                    qnat_s.acquire("c")
                    wpf_s.wait_ready("c")
                    p.op("c", lambda e: e.activation(qnat[:, 0:2048], wpf[0][:, :], AF.Identity), inc="c")
                    p.wait("c", "c", p.val("c"))
                    wpf_s.released("c")
                    p.wait("v", "c", p.val("c"))
                    quant_piece_q(0)
                    qnat_s.released("v")
                    qnat_s.released("c")
                    qeq_s.released("v")
                    dst = bass.AP(ap_in.tensor, (piece * 16 * 128) * (E // 8) + ch * 128,
                                  [[E // 8, 128], [128 * (E // 8), 16], [1, 128]])
                    quant_xpose_store(lambda e, pb, dst=dst: e.dma_start(
                        out=dst, in_=q_wT[pb][:, :].rearrange("p (c j) -> p c j", j=128)))
                ems.append(em)
        return ems

    q_emitters = make_q_emitters()
    agq_s = Slot(p, dma=True)
    rg8 = [[0, 1, 2, 3, 4, 5, 6, 7]]

    def emit_gu_trigger():
        p.wait("g", qst_s[0].sem, qst_s[0].ready())
        p.wait("g", qst_s[1].sem, qst_s[1].ready())
        p.op("g", lambda e: e.collective_compute(
            "AllGather", ALU.bypass, ins=[agu_in[:, :]], outs=[agu_out[:, :]],
            replica_groups=rg8), inc=agq_s.sem, inc_n=1)

    def emit_p_trigger():
        p.wait("g", qst_s[0].sem, qst_s[0].ready())
        p.wait("g", qst_s[1].sem, qst_s[1].ready())
        p.op("g", lambda e: e.collective_compute(
            "AllGather", ALU.bypass, ins=[ap_in[:, :]], outs=[ap_out[:, :]],
            replica_groups=rg8), inc=agq_s.sem, inc_n=1)

    sb.push()
    wkT = sb.alloc("wkT", [128, DC * (E // 2)], BF16)
    wvT = sb.alloc("wvT", [128, DC * (E // 2)], BF16)
    xb_off = sb.off
    xbuf = [sb.alloc(f"xbuf{i}", [128, E], F32) for i in range(2)]
    xb_end = sb.off
    sb.off = xb_off
    kvf = sb.alloc("kvf", [128, E], F32)
    sb.off = xb_end
    xn_tm = [sb.alloc(f"xn_tm{i}", [128, E], BF16) for i in range(2)]
    xn_st = sb.alloc("xn_st", [128, DC * STW], BF16)
    kbuf = sb.alloc("kbuf", [128, E // 2], BF16)
    vbuf = sb.alloc("vbuf", [128, E // 2], BF16)
    brow = sb.alloc("brow", [128, E // 2], BF16)
    blo = sb.alloc("blo", [128, E // 2], BF16)
    stats = sb.alloc("stats", [128, 24], F32)
    aggr = sb.alloc("aggr", [128, 2], F32)
    veps = sb.alloc("veps", [128, 1], F32)
    vrec = sb.alloc("vrec", [128, 1], F32)
    vinv = sb.alloc("vinv", [128, 1], F32)
    negm = sb.alloc("negm", [128, 1], F32)
    negmb = sb.alloc("negmb", [128, 1], F32)
    fb_off = sb.off
    fbias = sb.alloc("fbias", [128, E // 2], F32)
    natw = [sb.alloc(f"natw{i}", [128, E], BF16) for i in range(2)]
    fb_end = sb.off
    sb.off = fb_off
    wpf = [sb.alloc("wpf", [128, 2048], F32)]
    sb.off = fb_end
    natw_s = [Slot(p, dma=True), Slot(p, dma=True)]
    wpf_s = Slot(p, dma=True)
    xbuf_s = [Slot(p, dma=True), Slot(p, dma=True)]
    bias_s = Slot(p, dma=True)
    xnsp_s = Slot(p, dma=True)
    xn_tm_s = [Slot(p), Slot(p)]
    xn_st_s = Slot(p)
    kvbuf_s = Slot(p)

    phase_barrier([qnat_s, qst_s[0], qst_s[1]], engines=("v", "c", "sy", "t"))
    bias_s.dma("sy", lambda e: e.dma_start(out=fbias[0:1, :], in_=bk_d[:, :]))
    bias_s.dma("sy", lambda e: e.dma_start(out=fbias[1:2, :], in_=bv_d[:, :]))
    BIAS_ALL = bias_s.ready()

    p.wait("t", cst.sem, CST_ALL)
    transpose_weight_g(Wk_sh, wkT, E // 2, E, natw, natw_s)
    transpose_weight_g(Wv_sh, wvT, E // 2, E, natw, natw_s)
    NATW_T = p.val("t")

    p.op("v", lambda e: e.memset(kv_acc[:, :], 0.0))
    p.wait("v", bias_s.sem, BIAS_ALL)
    p.op("v", lambda e: e.memset(brow[:, :], 0.0))
    vbubble()
    p.op("v", lambda e: e.tensor_copy(brow[0:2, :], fbias[0:2, :]))
    vbubble()
    p.op("v", lambda e: e.tensor_tensor(blo[0:2, :], fbias[0:2, :], brow[0:2, :], ALU.subtract), inc="v")
    BIAS_V = p.val("v")
    p.wait("sy", "v", BIAS_V)
    p.wait("sy", "t", NATW_T)
    p.wait("sy", "v", BIAS_V)
    bias_s.dma("sy", lambda e: e.dma_start(out=brow[2:4, :], in_=blo[0:2, :]))
    p.wait("t", "v", BIAS_V)
    bias_s.wait_ready("t")

    apply_cval = {}
    negmb_cval = {}

    def emit_x_load(tcg):
        xsl = tcg % 2
        xbuf_s[xsl].acquire("sy")
        xbuf_s[xsl].dma("sy", lambda e, tcg=tcg, xsl=xsl: e.dma_start(
            out=xbuf[xsl][:, :], in_=x_all[tcg * 128:(tcg + 1) * 128, :]))

    def emit_ln(tcg):
        xsl = tcg % 2
        xbuf_s[xsl].wait_ready("v")
        for i in range(4):
            p.op("v", lambda e, i=i, xsl=xsl: e.bn_stats(
                stats[:, 6 * i:6 * (i + 1)], xbuf[xsl][:, 512 * i:512 * (i + 1)]))
        vbubble()
        p.op("v", lambda e: e.bn_aggr(aggr[:, :], stats[:, :]))
        vbubble()
        p.op("v", lambda e: e.tensor_scalar(veps[:, :], aggr[:, 1:2], 1.0, EPS_LN, ALU.mult, ALU.add))
        if tcg >= 1:
            p.wait("v", "c", negmb_cval[tcg - 1])
        vbubble()
        p.op("v", lambda e: e.tensor_scalar(negm[:, :], aggr[:, 0:1], -1.0, 0.0, ALU.mult, ALU.add))
        vbubble()
        p.op("v", lambda e: e.reciprocal(vrec[:, :], veps[:, :]), inc="v")
        p.wait("c", "v", p.val("v"))
        p.op("c", lambda e: e.activation(vinv[:, :], vrec[:, :], AF.Sqrt))
        p.wait("c", "c", p.val("c"))
        p.op("c", lambda e: e.activation(negmb[:, :], negm[:, :], AF.Identity,
                                         scale=vinv[:, 0:1]), inc="c")
        negmb_cval[tcg] = p.val("c")
        p.wait("c", "c", p.val("c"))
        xn_tm_s[xsl].acquire("c")
        xbuf_s[xsl].wait_ready("c")
        p.op("c", lambda e, xsl=xsl: e.activation(
            xn_tm[xsl][:, :], xbuf[xsl][:, :], AF.Identity,
            bias=negmb[:, 0:1], scale=vinv[:, 0:1]), inc="c")
        apply_cval[tcg] = p.val("c")
        xbuf_s[xsl].released("c")
        if tcg + 2 < 32:
            emit_x_load(tcg + 2)

    emit_x_load(0)
    emit_x_load(1)
    emit_ln(0)

    q_iter = iter(q_emitters)
    XN_EVAC_C = 0
    vdefer = []
    for st in range(16):
        xn_st_s.acquire("c")
        for tc in range(2):
            tcg = 2 * st + tc
            p.wait("t", "c", apply_cval[tcg])
            transpose_tiles(xn_tm[tcg % 2], DC, xn_st, tc * 128, STW)
            xn_tm_s[tcg % 2].released("t")
            if tc == 1:
                XN_EVAC_C = p.val("c")
            if tcg + 1 < 32:
                emit_ln(tcg + 1)
            if tc == 0:
                for fn in vdefer:
                    fn()
                vdefer = []
                for fn in tdefer:
                    fn()
                tdefer.clear()
            else:
                em = next(q_iter, None)
                if em is not None:
                    em()
                if st < 8:
                    em = next(q_iter, None)
                    if em is not None:
                        em()

        if st < 8:
            p.wait("sy", "c", XN_EVAC_C)
            xnsp_s.acquire("sy")
            for dcc in range(DC):
                xnsp_s.dma("sy", lambda e, dcc=dcc, st=st: e.dma_start(
                    out=xn_sp[dcc * 128:(dcc + 1) * 128, st * STW:(st + 1) * STW],
                    in_=xn_st[:, dcc * STW:(dcc + 1) * STW]))
            xn_st_s.released(xnsp_s.sem)

        p.wait("t", "c", XN_EVAC_C)
        for tc in range(2):
            if tc == 1:
                for fn in vdefer:
                    fn()
                vdefer = []
            kvbuf_s.acquire("c")
            for ft in range(2):
                for which in ("k", "v"):
                    si = (ft % 2) if which == "k" else (2 + ft % 2)
                    psl = pmm[si]
                    pmm_s[si].acquire("t")
                    wt = wkT if which == "k" else wvT
                    osel = ones_k_sb if which == "k" else ones_v_sb
                    for dcc in range(DC):
                        p.op("t", lambda e, dcc=dcc, tc=tc, ft=ft, psl=psl, wt=wt: e.matmul(
                            psl[:, :],
                            lhsT=xn_st[:, dcc * STW + tc * 128: dcc * STW + (tc + 1) * 128],
                            rhs=wt[:, dcc * (E // 2) + ft * 512: dcc * (E // 2) + (ft + 1) * 512],
                            start=(dcc == 0), stop=False))
                    p.op("t", lambda e, psl=psl, osel=osel, ft=ft: e.matmul(
                        psl[:, :], lhsT=osel[:, :],
                        rhs=brow[:, ft * 512:(ft + 1) * 512],
                        start=False, stop=True), inc="t")
                    p.wait("c", "t", p.val("t"))
                    dstb = kbuf if which == "k" else vbuf
                    p.op("c", lambda e, psl=psl, dstb=dstb, ft=ft: e.activation(
                        dstb[:, ft * 512:(ft + 1) * 512], psl[:, :], AF.Identity), inc="c")
                    pmm_s[si].released("c")
            p.wait("t", "c", p.val("c"))
            for hq in range(2):
                pkvh_s[hq].acquire("t")
                for i in range(4):
                    h = hq * 4 + i
                    p.op("t", lambda e, h=h, i=i, hq=hq: e.matmul(
                        pkv[:, hq * 512 + i * 128: hq * 512 + (i + 1) * 128],
                        lhsT=kbuf[:, h * 128:(h + 1) * 128],
                        rhs=vbuf[:, h * 128:(h + 1) * 128],
                        start=True, stop=True), inc=("t" if i == 3 else None))
                tval = p.val("t")

                def kv_add(hq=hq, tval=tval):
                    p.wait("v", "t", tval)
                    vbubble()
                    p.op("v", lambda e, hq=hq: e.tensor_tensor(
                        kv_acc[:, hq * 512:(hq + 1) * 512],
                        kv_acc[:, hq * 512:(hq + 1) * 512],
                        pkv[:, hq * 512:(hq + 1) * 512], ALU.add), inc="v")
                    pkvh_s[hq].released("v")
                vdefer.append(kv_add)
            kvbuf_s.released("t")
        xn_st_s.released("t")

    for fn in vdefer:
        fn()
    vdefer = []
    for em in q_iter:
        em()
    for fn in tdefer:
        fn()
    tdefer.clear()

    AB_V = p.val("v")
    AB_T = p.val("t")
    XNSP_ALL = xnsp_s.ready()

    dbg_s = Slot(p, dma=True) if dbg_aps else None
    if "dbg_xn" in dbg_aps:
        p.wait("sy", xnsp_s.sem, XNSP_ALL)
        dbg_s.dma("sy", lambda e: e.dma_start(out=dbg_aps["dbg_xn"][:, :], in_=xn_sp[:, :]))

    sb.pop()
    sb.pop()

    akvw_s = Slot(p, dma=True)
    akvr_s = Slot(p, dma=True)
    agkv_s = Slot(p, dma=True)
    p.wait("g", "v", AB_V)
    akvw_s.dma("g", lambda e: e.dma_start(out=akv_in[:, :], in_=kv_acc[:, :]))
    p.wait("g", akvw_s.sem, akvw_s.ready())
    rg2 = [[0, 1], [2, 3], [4, 5], [6, 7]]
    p.op("g", lambda e: e.collective_compute(
        "AllGather", ALU.bypass, ins=[akv_in[:, :]], outs=[akv_out[:, :]],
        replica_groups=rg2), inc=agkv_s.sem, inc_n=1)
    p.wait("g", agkv_s.sem, p.val(agkv_s.sem))
    for r in range(2):
        akvr_s.dma("g", lambda e, r=r: e.dma_start(
            out=kvf[:, r * 1024:(r + 1) * 1024],
            in_=akv_out[r * 128:(r + 1) * 128, :]))
    emit_gu_trigger()
    GU_DONE = p.val(agq_s.sem)
    akvr_s.wait_ready("v")
    if "dbg_kv" in dbg_aps:
        p.wait("sy", akvr_s.sem, akvr_s.ready())
        dbg_s.dma("sy", lambda e: e.dma_start(out=dbg_aps["dbg_kv"][:, :], in_=kvf[:, :]))
    p.op("v", lambda e: e.tensor_scalar(kvb[:, :], kvf[:, :], ATT_SCALE, 0.0, ALU.mult, ALU.add), inc="v")
    KVB_V = p.val("v")

    sb.push()
    R1_off = sb.off
    wqT = sb.alloc("wqT", [128, DC * E], BF16)
    R2_off = sb.off
    q_own = sb.alloc("q_own", [128, DC * E], BF16)
    xnt = [sb.alloc(f"xnt{i}", [128, DC * 512], BF16) for i in range(2)]
    natw1 = [sb.alloc(f"natw1_{i}", [128, E], BF16) for i in range(2)]
    natw1_s = [Slot(p, dma=True) for _ in range(2)]
    xnt_s = [Slot(p, dma=True), Slot(p, dma=True)]

    p.wait("sy", "t", AB_T)
    p.wait("sy", "v", KVB_V)
    p.wait("sy", "c", p.val("c"))
    p.wait("sy", wqbf_s.sem, wqbf_s.ready())
    p.wait("c", "t", AB_T)
    p.wait("c", "v", AB_V)
    p.wait("c", cst.sem, CST_ALL)
    phase_barrier([qnat_s, qst_s[0], qst_s[1]])
    phase_barrier([xnsp_s, xbuf_s[0], xbuf_s[1], natw_s[0], natw_s[1], bias_s, wpf_s])

    transpose_weight(wq_bf, wqT, E, E, natw1, natw1_s, f32=False)
    p.wait("c", "v", KVB_V)
    p.wait("t", "c", p.val("c"))

    for tile in range(NTILE):
        slot = tile % 2
        xnt_s[slot].acquire("sy")
        p.wait("sy", xnsp_s.sem, XNSP_ALL)
        for dcc in range(DC):
            xnt_s[slot].dma("sy", lambda e, dcc=dcc, tile=tile, slot=slot: e.dma_start(
                out=xnt[slot][:, dcc * 512:(dcc + 1) * 512],
                in_=xn_sp[dcc * 128:(dcc + 1) * 128, tile * 512:(tile + 1) * 512]))
        xnt_s[slot].wait_ready("t")
        for oc in range(DC):
            si = oc % 4
            psl = pmm[si]
            pmm_s[si].acquire("t")
            for dcc in range(DC):
                p.op("t", lambda e, dcc=dcc, oc=oc, slot=slot, psl=psl: e.matmul(
                    psl[:, :],
                    lhsT=wqT[:, dcc * E + oc * 128: dcc * E + (oc + 1) * 128],
                    rhs=xnt[slot][:, dcc * 512:(dcc + 1) * 512],
                    start=(dcc == 0), stop=(dcc == DC - 1)),
                    inc=("t" if dcc == DC - 1 else None))
            p.wait("c", "t", p.val("t"))
            p.op("c", lambda e, psl=psl, oc=oc, tile=tile: e.activation(
                q_own[:, oc * E + tile * 512: oc * E + (tile + 1) * 512],
                psl[:, :], AF.Identity, bias=bq_sb[:, oc:oc + 1], scale=1.0), inc="c")
            pmm_s[si].released("c")
        xnt_s[slot].released("t")
    C1_C = p.val("c")
    C1_T = p.val("t")

    if "dbg_q" in dbg_aps:
        p.wait("sy", "c", C1_C)
        dbg_s.dma("sy", lambda e: e.dma_start(out=dbg_aps["dbg_q"][:, :], in_=q_own[:, :]))

    sb.pop()

    sb.push()
    assert sb.off == R1_off
    a_own = sb.alloc("a_own", [128, DC * E], BF16)
    assert sb.off == R2_off
    sb.off = R2_off + DC * E * 2
    q_own2 = q_own

    p.wait("t", "v", KVB_V)
    p.wait("t", "c", C1_C)
    for tile in range(NTILE):
        for h in range(H):
            si = (tile * H + h) % 4
            psl = pmm[si]
            pmm_s[si].acquire("t")
            p.op("t", lambda e, h=h, tile=tile, psl=psl: e.matmul(
                psl[:, :],
                lhsT=kvb[:, h * 128:(h + 1) * 128],
                rhs=q_own2[:, h * E + tile * 512: h * E + (tile + 1) * 512],
                start=True, stop=True), inc="t")
            p.wait("c", "t", p.val("t"))
            p.op("c", lambda e, psl=psl, h=h, tile=tile: e.copy(
                a_own[:, h * E + tile * 512: h * E + (tile + 1) * 512], psl[:, :]), inc="c")
            pmm_s[si].released("c")
    C2_V = p.val("c")

    if "dbg_attn" in dbg_aps:
        p.wait("sy", "c", C2_V)
        dbg_s.dma("sy", lambda e: e.dma_start(out=dbg_aps["dbg_attn"][:, :], in_=a_own[:, :]))

    sb.off = R2_off
    woT = sb.alloc("woT", [128, DC * E], BF16)
    bo_b = sb.alloc("bo_b", [128, E], F32)
    xchunk = [sb.alloc("xchunk0", [128, E], F32)]
    stg_a = [sb.alloc(f"stg_a{i}", [128, 512], F32) for i in range(4)]
    pstage = sb.alloc("pstage0", [128, E], F32)
    natw2 = [sb.alloc(f"natw2_{i}", [128, E], BF16) for i in range(2)]
    natw2_s = [Slot(p, dma=True), Slot(p, dma=True)]
    xchunk_s = [Slot(p, dma=True)]
    pstg_s = Slot(p, dma=True)

    p.wait("g", "t", p.val("t"))
    p.wait("sy", "t", C1_T)
    phase_barrier([xnt_s[0], xnt_s[1], natw1_s[0], natw1_s[1]])
    bias_s.dma("sy", lambda e: e.dma_start(out=bo_b[:, :], in_=bo_d[:, :].broadcast_to([128, E])))
    BO_ALL = bias_s.ready()
    p.wait("sy", wobf_s.sem, wobf_s.ready())
    transpose_weight(wo_bf, woT, E, E, natw2, natw2_s, f32=False)
    p.wait("g", natw2_s[0].sem, natw2_s[0].ready())
    p.wait("g", natw2_s[1].sem, natw2_s[1].ready())
    emit_p_trigger()
    AGW_DONE = p.val(agq_s.sem)
    p.wait("t", "c", p.val("c"))
    p.wait("v", bias_s.sem, BO_ALL)

    for tkc in range(NTKC):
        slot = 0
        xchunk_s[slot].acquire("sy")
        xchunk_s[slot].dma("sy", lambda e, tkc=tkc, slot=slot: e.dma_start(
            out=xchunk[slot][:, :], in_=x_all[tkc * 128:(tkc + 1) * 128, :]))
        for ot in range(4):
            pmm_s[ot].acquire("t")
            for dcc in range(DC):
                p.op("t", lambda e, dcc=dcc, tkc=tkc, ot=ot: e.matmul(
                    pmm[ot][:, :],
                    lhsT=a_own[:, dcc * E + tkc * 128: dcc * E + (tkc + 1) * 128],
                    rhs=woT[:, dcc * E + ot * 512: dcc * E + (ot + 1) * 512],
                    start=(dcc == 0), stop=(dcc == DC - 1)),
                    inc=("t" if dcc == DC - 1 else None))
        p.wait("v", "t", p.val("t"))
        xchunk_s[slot].wait_ready("v")
        pstg_s.acquire("v")
        for ot in range(4):
            p.op("v", lambda e, ot=ot: e.tensor_tensor(
                stg_a[ot][:, :], pmm[ot][:, :], bo_b[:, ot * 512:(ot + 1) * 512], ALU.add),
                inc=("v" if ot == 3 else None))
        for ot in range(4):
            pmm_s[ot].released("v")
        vbubble()
        for ot in range(4):
            p.op("v", lambda e, ot=ot, slot=slot: e.tensor_tensor(
                pstage[:, ot * 512:(ot + 1) * 512], stg_a[ot][:, :],
                xchunk[slot][:, ot * 512:(ot + 1) * 512], ALU.add),
                inc=("v" if ot == 3 else None))
        xchunk_s[slot].released("v")
        p.wait("sy", "v", p.val("v"))
        pstg_s.dma("sy", lambda e, tkc=tkc: e.dma_start(
            out=part_d[tkc * 128:(tkc + 1) * 128, :], in_=pstage[:, :]))
        pstg_s.released_dma()
    PSTG_ALL = pstg_s.ready()
    C3_T = p.val("t")

    if "dbg_part" in dbg_aps:
        p.wait("sy", pstg_s.sem, PSTG_ALL)
        dbg_s.dma("sy", lambda e: e.dma_start(out=dbg_aps["dbg_part"][:, :], in_=part_d[:, :]))

    sb.pop()

    sb.push()
    xn_own = sb.alloc("xn_own", [128, DC * TOK], BF16)
    wgT = [sb.alloc(f"wgT{i}", [128, E], BF16) for i in range(2)]
    wuT = [sb.alloc(f"wuT{i}", [128, E], BF16) for i in range(2)]
    sil = [sb.alloc(f"sil{i}", [128, 512], F32) for i in range(2)]
    h_stb = [sb.alloc(f"h_st{i}", [128, 512], BF16) for i in range(4)]
    hsq_stb = [sb.alloc(f"hsq_st{i}", [128, 512], BF16) for i in range(4)]

    gu_nat_s = [Slot(p, dma=True), Slot(p, dma=True)]
    xno_s = Slot(p, dma=True)
    hst_s = [Slot(p, dma=True) for _ in range(4)]
    wT_s = [Slot(p), Slot(p)]
    sil_s = [Slot(p), Slot(p)]
    hsq_s = [Slot(p) for _ in range(4)]

    p.wait("g", "t", C3_T)
    p.wait("g", "v", p.val("v"))
    phase_barrier([pstg_s, xchunk_s[0], natw2_s[0], natw2_s[1], bias_s])
    p.wait("sy", agq_s.sem, GU_DONE)
    p.wait("sy", xnsp_s.sem, XNSP_ALL)
    for dcc in range(DC):
        xno_s.dma("sy", lambda e, dcc=dcc: e.dma_start(
            out=xn_own[:, dcc * TOK:(dcc + 1) * TOK],
            in_=xn_sp[dcc * 128:(dcc + 1) * 128, :]))
    xno_s.wait_ready("t")
    p.wait("t", "t", C3_T)

    ss_started = [False]
    ss_last_t = [0]
    WT_READY = {}

    SH = GUP // 8

    def _guT_src(h_global):
        c0, ch = divmod(h_global, 16)
        return bass.AP(agu_out.tensor, (c0 * E) * SH + ch * 128,
                       [[SH, 128], [128 * SH, DC], [1, 128]])

    def emit_gu_load_quant(pc):
        slot = pc % 2
        wT_s[slot].acquire("sy")
        gu_nat_s[slot].acquire("sy")
        gu_nat_s[slot].dma("sy", lambda e, pc=pc, slot=slot: e.dma_start(
            out=wgT[slot][:, :].rearrange("p (c j) -> p c j", j=128),
            in_=_guT_src(pc)))
        gu_nat_s[slot].dma("sy", lambda e, pc=pc, slot=slot: e.dma_start(
            out=wuT[slot][:, :].rearrange("p (c j) -> p c j", j=128),
            in_=_guT_src(64 + pc)))
        WT_READY[pc] = gu_nat_s[slot].ready()

    def emit_gu_mm(qc):
        slot = qc % 2
        hsq_counts = []
        p.wait("t", gu_nat_s[slot].sem, WT_READY[qc])
        for tile in range(4):
            sig, siu = tile % 2, 2 + tile % 2
            psg, psu = pmm[sig], pmm[siu]
            pmm_s[sig].acquire("t")
            pmm_s[siu].acquire("t")
            for dcc in range(DC):
                p.op("t", lambda e, dcc=dcc, tile=tile, psg=psg, slot=slot: e.matmul(
                    psg[:, :],
                    lhsT=wgT[slot][:, dcc * 128:(dcc + 1) * 128],
                    rhs=xn_own[:, dcc * TOK + tile * 512: dcc * TOK + (tile + 1) * 512],
                    start=(dcc == 0), stop=(dcc == DC - 1)),
                    inc=("t" if dcc == DC - 1 else None))
            TG = p.val("t")
            for dcc in range(DC):
                p.op("t", lambda e, dcc=dcc, tile=tile, psu=psu, slot=slot: e.matmul(
                    psu[:, :],
                    lhsT=wuT[slot][:, dcc * 128:(dcc + 1) * 128],
                    rhs=xn_own[:, dcc * TOK + tile * 512: dcc * TOK + (tile + 1) * 512],
                    start=(dcc == 0), stop=(dcc == DC - 1)),
                    inc=("t" if dcc == DC - 1 else None))
            TU = p.val("t")
            p.wait("c", "t", TG)
            sil_s[tile % 2].acquire("c")
            p.op("c", lambda e, psg=psg, tile=tile: e.activation(
                sil[tile % 2][:, :], psg[:, :], AF.Silu), inc="c")
            SIL_C = p.val("c")
            pmm_s[sig].released("c")
            hslot = tile
            p.wait("v", "t", TU)
            p.wait("v", "c", SIL_C)
            h_s_cur = hst_s[hslot]
            h_s_cur.acquire("v")
            p.op("v", lambda e, tile=tile, psu=psu, hslot=hslot: e.tensor_tensor(
                h_stb[hslot][:, :], sil[tile % 2][:, :], psu[:, :], ALU.mult), inc="v")
            HB_V = p.val("v")
            sil_s[tile % 2].released("v")
            pmm_s[siu].released("v")
            vbubble()
            hsq_s[tile].acquire("v")
            p.op("v", lambda e, hslot=hslot, tile=tile: e.tensor_tensor(
                hsq_stb[tile][:, :], h_stb[hslot][:, :], h_stb[hslot][:, :], ALU.mult), inc="v")
            hsq_counts.append(p.val("v"))
            p.wait("sy", "v", HB_V)
            h_s_cur.dma("sy", lambda e, qc=qc, tile=tile, hslot=hslot: e.dma_start(
                out=hsp[qc * 128:(qc + 1) * 128, tile * 512:(tile + 1) * 512],
                in_=h_stb[hslot][:, :]))
            h_s_cur.released_dma()
        for tile in range(4):
            p.wait("t", "v", hsq_counts[tile])
            first = not ss_started[0]
            ss_started[0] = True
            p.op("t", lambda e, tile=tile, first=first, last=(qc == 63 and tile == 3): e.matmul(
                pkv[0:4, 512:1024],
                lhsT=sel4_sb[:, 4 * tile:4 * (tile + 1)],
                rhs=hsq_stb[tile][:, :],
                start=first, stop=last), inc="t")
            ss_last_t[0] = p.val("t")
            hsq_s[tile].released("t")
        wT_s[slot].released("t")

    for pc in range(65):
        if pc < 64:
            emit_gu_load_quant(pc)
        if pc >= 1:
            emit_gu_mm(pc - 1)

    HST_ALL = [s.ready() for s in hst_s]

    if "dbg_h" in dbg_aps:
        for s, vtot in zip(hst_s, HST_ALL):
            p.wait("sy", s.sem, vtot)
        dbg_s.dma("sy", lambda e: e.dma_start(out=dbg_aps["dbg_h"][:, :], in_=hsp[:, :]))

    ssd_s = Slot(p, dma=True)
    invr_s = Slot(p, dma=True)
    p.wait("v", "t", ss_last_t[0])
    p.op("v", lambda e: e.tensor_copy(ssq[:, :], pkv[0:4, 512:1024]), inc="v")
    p.wait("sy", "v", p.val("v"))
    ssd_s.dma("sy", lambda e: e.dma_start(out=ss_d[:, :], in_=ssq[:, :]))
    ssd_s.wait_ready("sy")
    invr_s.dma("sy", lambda e: e.dma_start(
        out=inv_raw[:, :], in_=bass.AP(ss_d.tensor, 0, [[1, 128], [128, NTKC]]),
        allow_slow_non_contiguous=True))
    invr_s.wait_ready("v")
    p.op("v", lambda e: e.tensor_scalar(inv_raw[:, :], inv_raw[:, :], 1.0 / HID, EPS_RMS, ALU.mult, ALU.add))
    vbubble()
    p.op("v", lambda e: e.reciprocal(inv_raw[:, :], inv_raw[:, :]), inc="v")
    p.wait("c", "v", p.val("v"))
    p.op("c", lambda e: e.activation(inv_tm[:, :], inv_raw[:, :], AF.Sqrt), inc="c")
    INV_C = p.val("c")
    if "dbg_inv" in dbg_aps:
        p.wait("sy", "c", INV_C)
        dbg_s.dma("sy", lambda e: e.dma_start(out=dbg_aps["dbg_inv"][:, :], in_=inv_tm[:, :]))

    sb.pop()

    sb.push()
    wpT = [sb.alloc(f"wpT{i}", [128, 64 * 256], BF16) for i in range(2)]
    hpiece = [sb.alloc(f"hpiece{i}", [128, 64 * 256], BF16) for i in range(2)]
    pchunk = [sb.alloc(f"pchunk{i}", [128, 256], F32) for i in range(2)]
    stg1 = [sb.alloc(f"stg1{i}", [128, 256], F32) for i in range(2)]
    ostg = [sb.alloc(f"ostg{i}", [128, 256], F32) for i in range(2)]

    wpTl_s = [Slot(p, dma=True), Slot(p, dma=True)]
    hp_s = [Slot(p, dma=True), Slot(p, dma=True)]
    pch_s = [Slot(p, dma=True), Slot(p, dma=True)]
    ost_s = [Slot(p, dma=True), Slot(p, dma=True)]
    wpT_s = [Slot(p), Slot(p)]
    stg1_s = [Slot(p), Slot(p)]

    D_T_END = p.val("t")
    D_V_END = p.val("v")
    p.wait("g", "t", D_T_END)
    p.wait("g", "v", D_V_END)
    p.wait("c", "t", D_T_END)
    p.wait("c", "v", D_V_END)
    p.wait("sy", "t", D_T_END)
    p.wait("sy", "v", D_V_END)
    phase_barrier([gu_nat_s[0], gu_nat_s[1], hst_s[0], hst_s[1], hst_s[2],
                   hst_s[3], xno_s, ssd_s, invr_s])
    for s_, vtot in zip(hst_s, HST_ALL):
        p.wait("c", s_.sem, vtot)
    p.wait("sy", pstg_s.sem, PSTG_ALL)

    p.wait("sy", agq_s.sem, AGW_DONE)
    WPT_READY = {}

    def emit_wpT_load(dt_):
        dslot = dt_ % 2
        wpT_s[dslot].acquire("sy")
        wpTl_s[dslot].acquire("sy")
        wpTl_s[dslot].dma("sy", lambda e, dt_=dt_, dslot=dslot: e.dma_start(
            out=wpT[dslot][:, :].rearrange("p (c j) -> p c j", j=256),
            in_=bass.AP(ap_out.tensor, (dt_ * HID) * (E // 8),
                        [[E // 8, 128], [128 * (E // 8), 64], [1, 256]])))
        WPT_READY[dt_] = wpTl_s[dslot].ready()

    hp_loaded = [-1, -1]

    def emit_hp_load(pi):
        hsl = pi % 2
        if hp_loaded[hsl] == pi:
            return
        hp_loaded[hsl] = pi
        hp_s[hsl].acquire("c")
        hp_s[hsl].dma("c", lambda e, pi=pi, hsl=hsl: e.dma_start(
            out=bass.AP(hpiece[hsl], 0, [[hpiece[hsl].shape[1], 128], [256, 64], [1, 256]]),
            in_=bass.AP(hsp.tensor, pi * 256, [[TOK, 128], [128 * TOK, 64], [1, 256]])))

    def emit_proj_pi(dt_, k):
        order = list(range(8)) if dt_ % 2 == 0 else list(range(7, -1, -1))
        pi = order[k]
        dslot = dt_ % 2
        if k == 0:
            emit_hp_load(pi)
            p.wait("t", wpTl_s[dslot].sem, WPT_READY[dt_])
        if k < 7:
            emit_hp_load(order[k + 1])
        hsl = pi % 2
        for sub in range(2):
            tkc = pi * 2 + sub
            si = tkc % 4
            pmm_s[si].acquire("t")
            hp_s[hsl].wait_ready("t")
            for hc in range(64):
                p.op("t", lambda e, hc=hc, sub=sub, si=si, hsl=hsl, dslot=dslot: e.matmul(
                    pmm[si][:, 0:256],
                    lhsT=hpiece[hsl][:, hc * 256 + sub * 128: hc * 256 + (sub + 1) * 128],
                    rhs=wpT[dslot][:, hc * 256:(hc + 1) * 256],
                    start=(hc == 0), stop=(hc == 63)),
                    inc=("t" if hc == 63 else None))
            T_PE = p.val("t")
            pch_s[sub].acquire("sy")
            pch_s[sub].dma("sy", lambda e, tkc=tkc, dt_=dt_, sub=sub: e.dma_start(
                out=pchunk[sub][:, :],
                in_=part_d[tkc * 128:(tkc + 1) * 128, dt_ * 256:(dt_ + 1) * 256]))
            p.wait("c", "t", T_PE)
            stg1_s[sub].acquire("c")
            p.op("c", lambda e, si=si, tkc=tkc, sub=sub: e.activation(
                stg1[sub][:, :], pmm[si][:, 0:256], AF.Identity,
                bias=0.0, scale=inv_tm[:, tkc:tkc + 1]), inc="c")
            STG1_C = p.val("c")
            pmm_s[si].released("c")
            p.wait("v", "c", STG1_C)
            pch_s[sub].wait_ready("v")
            ost_s[sub].acquire("v")
            p.op("v", lambda e, sub=sub: e.tensor_tensor(
                ostg[sub][:, :], stg1[sub][:, :], pchunk[sub][:, :], ALU.add), inc="v")
            stg1_s[sub].released("v")
            pch_s[sub].released("v")
            p.wait("sy", "v", p.val("v"))
            ost_s[sub].dma("sy", lambda e, tkc=tkc, dt_=dt_, sub=sub: e.dma_start(
                out=out_d[tkc * 128:(tkc + 1) * 128, dt_ * 256:(dt_ + 1) * 256],
                in_=ostg[sub][:, :]))
            ost_s[sub].released_dma()
        hp_s[hsl].released("t")
        if k == 7:
            wpT_s[dslot].released("t")
            wpTl_s[dslot].released("t")

    for dt_ in range(9):
        if dt_ < 8:
            emit_wpT_load(dt_)
        if dt_ >= 1:
            for k in range(8):
                emit_proj_pi(dt_ - 1, k)

    for s_ in ost_s:
        s_.wait_ready("sy")
    if dbg_s is not None:
        dbg_s.wait_ready("sy")
    sb.pop()

    p.finalize()
    return nc


_NC_CACHE = {}


def _get_nc(dbg=()):
    key = tuple(sorted(dbg))
    if key not in _NC_CACHE:
        _NC_CACHE[key] = build_program(dbg)
    return _NC_CACHE[key]


def make_in_maps(inputs):
    x = np.ascontiguousarray(np.asarray(inputs["L0_attn_x"], dtype=np.float32))
    ws = {k: np.ascontiguousarray(np.asarray(inputs[k], dtype=np.float32))
          for k in ("L0_attn_Wq", "L0_attn_Wk", "L0_attn_Wv", "L0_attn_Wo",
                    "L0_ffn_w_gate_up", "L0_ffn_w_proj")}
    bs = {k: np.ascontiguousarray(np.asarray(inputs[k], dtype=np.float32))
          for k in ("L0_attn_bq", "L0_attn_bk", "L0_attn_bv", "L0_attn_bo")}
    in_maps = []
    EH = E // 2
    for c in range(8):
        b, half = c // 2, c % 2
        own = x[b, half * TOK:(half + 1) * TOK, :]
        other = x[b, (1 - half) * TOK:(2 - half) * TOK, :]
        x_core = np.concatenate([own, other], axis=0)
        in_maps.append({
            "x_all": x_core,
            "Wq": ws["L0_attn_Wq"],
            "Wk_sh": np.ascontiguousarray(ws["L0_attn_Wk"][half * EH:(half + 1) * EH]),
            "Wv_sh": np.ascontiguousarray(ws["L0_attn_Wv"][half * EH:(half + 1) * EH]),
            "Wo": ws["L0_attn_Wo"],
            "bq": bs["L0_attn_bq"].reshape(E, 1),
            "bk": np.ascontiguousarray(bs["L0_attn_bk"].reshape(1, E)[:, half * EH:(half + 1) * EH]),
            "bv": np.ascontiguousarray(bs["L0_attn_bv"].reshape(1, E)[:, half * EH:(half + 1) * EH]),
            "bo": bs["L0_attn_bo"].reshape(1, E),
            "wgu_sh": np.ascontiguousarray(ws["L0_ffn_w_gate_up"][c * (GUP // 8):(c + 1) * (GUP // 8)]),
            "wp_sh": np.ascontiguousarray(ws["L0_ffn_w_proj"][c * (E // 8):(c + 1) * (E // 8)]),
        })
    return in_maps


def kernel(**inputs):
    nc = _get_nc()
    in_maps = make_in_maps(inputs)
    res = run_bass_kernel_spmd(nc, in_maps, core_ids=list(range(8)))
    out = np.empty((B, T, E), dtype=np.float32)
    for c in range(8):
        b, half = c // 2, c % 2
        out[b, half * TOK:(half + 1) * TOK, :] = np.asarray(res.results[c]["out"])
    return out


# version 14
# speedup vs baseline: 1.2244x; 1.0202x over previous
import math
import numpy as np
import ml_dtypes
import concourse.bass as bass
import concourse.mybir as mybir
from concourse.bass_utils import run_bass_kernel_spmd

F32 = mybir.dt.float32
BF16 = mybir.dt.bfloat16
AF = mybir.ActivationFunctionType
ALU = mybir.AluOpType

B, T, E, H = 4, 4096, 2048, 16
HID, GUP = 8192, 16384
TOK = 2048
TALL = 4096
DC = E // 128
NTILE = TOK // 512
NTKC = TOK // 128
EPS_LN = 1e-5
EPS_RMS = 1.1920929e-7
ATT_SCALE = 1.0 / math.sqrt(128.0)
STW = 256

ENGS = ("t", "v", "c", "sy", "g")


class Prog:
    def __init__(self, nc):
        self.nc = nc
        self.q = {e: [] for e in ENGS}
        self.sems = {}
        self.cnt = {}
        self._guards = []

    def mksem(self, name):
        g = self.nc.semaphore(name)
        self.sems[name] = g.__enter__()
        self._guards.append(g)
        self.cnt[name] = 0

    def val(self, s):
        return self.cnt[s]

    def wait(self, eng, s, v):
        if v <= 0:
            return
        sem = self.sems[s]
        self.q[eng].append(lambda e, sem=sem, v=v: e.wait_ge(sem, v))

    def op(self, eng, build, inc=None, inc_n=1):
        if inc is None and eng in ("v", "c"):
            inc = eng
        if inc is not None:
            self.cnt[inc] += inc_n
            sem = self.sems[inc]

            def f(e, build=build, sem=sem, n=inc_n):
                build(e).then_inc(sem, n)
        else:
            def f(e, build=build):
                build(e)
        self.q[eng].append(f)

    def finalize(self):
        with self.nc.Block() as blk:
            for eng, meth in (("t", blk.tensor), ("v", blk.vector),
                              ("c", blk.scalar), ("sy", blk.sync), ("g", blk.gpsimd)):
                fns = self.q[eng]

                def body(e, fns=fns):
                    for f in fns:
                        f(e)
                meth(body)


class Slot:

    _n = [0]

    def __init__(self, p, dma=False):
        self.p = p
        self.rel = []
        self.sem = None
        if dma:
            Slot._n[0] += 1
            self.sem = f"sl{Slot._n[0]}"
            p.mksem(self.sem)

    def acquire(self, eng):
        for s, v in self.rel:
            self.p.wait(eng, s, v)
        self.rel = []

    def released(self, sem):
        self.rel.append((sem, self.p.val(sem)))

    def released_dma(self):
        self.rel.append((self.sem, self.p.val(self.sem)))

    def dma(self, eng, build):
        self.p.op(eng, build, inc=self.sem, inc_n=16)

    def ready(self):
        return self.p.val(self.sem)

    def wait_ready(self, eng):
        self.p.wait(eng, self.sem, self.p.val(self.sem))


class SbAlloc:
    def __init__(self, nc, base, limit):
        self.nc = nc
        self.off = base
        self.limit = limit
        self.marks = []

    def alloc(self, name, shape, dt):
        sz = int(np.prod(shape[1:])) * mybir.dt.size(dt)
        off = (self.off + 31) // 32 * 32
        assert off + sz <= self.limit, f"SBUF overflow {name}: {off}+{sz}>{self.limit}"
        h = self.nc.alloc_sbuf_tensor_at(name, list(shape), dt, offset=off)
        self.off = off + sz
        return h

    def push(self):
        self.marks.append(self.off)

    def pop(self):
        self.off = self.marks.pop()


def build_program(dbg=()):
    nc = bass.Bass("TRN2", target_bir_lowering=False)

    x_all = nc.dram_tensor("x_all", [TALL, E], F32, kind="ExternalInput").ap()
    Wq = nc.dram_tensor("Wq", [E, E], F32, kind="ExternalInput").ap()
    Wk_sh = nc.dram_tensor("Wk_sh", [E // 2, E], F32, kind="ExternalInput").ap()
    Wv_sh = nc.dram_tensor("Wv_sh", [E // 2, E], F32, kind="ExternalInput").ap()
    Wo = nc.dram_tensor("Wo", [E, E], F32, kind="ExternalInput").ap()
    bq_d = nc.dram_tensor("bq", [E, 1], F32, kind="ExternalInput").ap()
    bk_d = nc.dram_tensor("bk", [1, E // 2], F32, kind="ExternalInput").ap()
    bv_d = nc.dram_tensor("bv", [1, E // 2], F32, kind="ExternalInput").ap()
    bo_d = nc.dram_tensor("bo", [1, E], F32, kind="ExternalInput").ap()
    wgu_sh = nc.dram_tensor("wgu_sh", [GUP // 8, E], F32, kind="ExternalInput").ap()
    wp_sh = nc.dram_tensor("wp_sh", [E // 8, HID], F32, kind="ExternalInput").ap()
    out_d = nc.dram_tensor("out", [TOK, E], F32, kind="ExternalOutput").ap()

    agu_in = nc.dram_tensor("agu_in", [E, GUP // 8], BF16, kind="Internal").ap()
    agu_out = nc.dram_tensor("agu_out", [8 * E, GUP // 8], BF16, kind="Internal").ap()
    ap_in = nc.dram_tensor("ap_in", [HID, E // 8], BF16, kind="Internal").ap()
    ap_out = nc.dram_tensor("ap_out", [8 * HID, E // 8], BF16, kind="Internal").ap()
    akv_in = nc.dram_tensor("akv_in", [128, E // 2], F32, kind="Internal").ap()
    akv_out = nc.dram_tensor("akv_out", [256, E // 2], F32, kind="Internal").ap()

    wq_bf = nc.dram_tensor("wq_bf", [E, E], BF16, kind="Internal").ap()
    wo_bf = nc.dram_tensor("wo_bf", [E, E], BF16, kind="Internal").ap()
    xn_sp = nc.dram_tensor("xn_sp", [E, TOK], BF16, kind="Internal").ap()
    part_d = nc.dram_tensor("part_d", [TOK, E], F32, kind="Internal").ap()
    hsp = nc.dram_tensor("hsp", [HID, TOK], BF16, kind="Internal").ap()
    ss_d = nc.dram_tensor("ss_d", [4, 512], F32, kind="Internal").ap()

    dbg_specs = {
        "dbg_xn": ([E, TOK], BF16),
        "dbg_kv": ([128, E], F32),
        "dbg_q": ([128, DC * E], BF16),
        "dbg_attn": ([128, DC * E], BF16),
        "dbg_part": ([TOK, E], F32),
        "dbg_h": ([HID, TOK], BF16),
        "dbg_wqg": ([128, E], BF16),
        "dbg_inv": ([128, NTKC], F32),
    }
    dbg_aps = {}
    for name in dbg:
        shape, dt = dbg_specs[name]
        dbg_aps[name] = nc.dram_tensor(name, shape, dt, kind="ExternalOutput").ap()

    ident = nc.inline_tensor(np.eye(128, dtype=ml_dtypes.bfloat16), name="identc").ap()
    identf = nc.inline_tensor(np.eye(128, dtype=np.float32), name="identfc").ap()
    sel4_np = np.zeros((4, 128, 4), dtype=ml_dtypes.bfloat16)
    for i in range(4):
        sel4_np[i, :, i] = 1
    sel4 = [nc.inline_tensor(sel4_np[i], name=f"sel4_{i}").ap() for i in range(4)]
    ok_np = np.zeros((128, 128), dtype=ml_dtypes.bfloat16)
    ok_np[0, :] = 1
    ok_np[2, :] = 1
    ov_np = np.zeros((128, 128), dtype=ml_dtypes.bfloat16)
    ov_np[1, :] = 1
    ov_np[3, :] = 1
    ones_k = nc.inline_tensor(ok_np, name="oneskc").ap()
    ones_v = nc.inline_tensor(ov_np, name="onesvc").ap()

    p = Prog(nc)
    for s in ("t", "v", "c"):
        p.mksem(s)

    KB = 1024
    sb = SbAlloc(nc, base=18 * KB, limit=206 * KB)

    ident_sb = sb.alloc("ident_sb", [128, 128], BF16)
    identf_sb = sb.alloc("identf_sb", [128, 128], F32)
    sel4_sb = sb.alloc("sel4_sb", [128, 16], BF16)
    ones_k_sb = sb.alloc("ones_k_sb", [128, 128], BF16)
    ones_v_sb = sb.alloc("ones_v_sb", [128, 128], BF16)
    bq_sb = sb.alloc("bq_sb", [128, DC], F32)
    kv_acc = sb.alloc("kv_acc", [128, E // 2], F32)
    kvb = sb.alloc("kvb", [128, E], BF16)
    ssq = sb.alloc("ssq", [4, 512], F32)
    inv_raw = sb.alloc("inv_raw", [128, NTKC], F32)
    inv_tm = sb.alloc("inv_tm", [128, NTKC], F32)
    vbub = sb.alloc("vbub", [128, 8], F32)

    ps_guards = []

    def psum(name, shape, dt):
        g = nc.psum_tensor(name, shape, dt)
        h = g.__enter__()
        ps_guards.append(g)
        return h

    pmm = [psum(f"pmm{i}", [128, 512], F32) for i in range(4)]
    pxp = [psum(f"pxp{i}", [128, 1024], BF16) for i in range(2)]
    pkv = psum("pkv", [128, 1024], F32)

    pmm_s = [Slot(p) for _ in range(4)]
    pxp_s = [Slot(p) for _ in range(2)]
    pkvh_s = [Slot(p), Slot(p)]

    def vbubble():
        p.wait("v", "v", p.val("v"))

    def phase_barrier(slots, engines=("v", "c")):
        for eng in engines:
            for s_ in slots:
                if s_ is not None and s_.sem is not None:
                    p.wait(eng, s_.sem, p.val(s_.sem))

    cst = Slot(p, dma=True)
    cst.dma("sy", lambda e: e.dma_start(out=ident_sb[:, :], in_=ident[:, :]))
    cst.dma("sy", lambda e: e.dma_start(out=identf_sb[:, :], in_=identf[:, :]))
    for i in range(4):
        cst.dma("sy", lambda e, i=i: e.dma_start(out=sel4_sb[:, 4 * i:4 * (i + 1)], in_=sel4[i][:, :]))
    cst.dma("sy", lambda e: e.dma_start(out=ones_k_sb[:, :], in_=ones_k[:, :]))
    cst.dma("sy", lambda e: e.dma_start(out=ones_v_sb[:, :], in_=ones_v[:, :]))
    cst.dma("sy", lambda e: e.dma_start(
        out=bq_sb[:, :], in_=bass.AP(bq_d.tensor, 0, [[1, 128], [128, DC]]),
        allow_slow_non_contiguous=True))
    CST_ALL = cst.ready()

    wqbf_s = Slot(p, dma=True)
    wobf_s = Slot(p, dma=True)
    wqbf_s.dma("g", lambda e: e.dma_start(out=wq_bf[:, :], in_=Wq[:, :]))
    wobf_s.dma("g", lambda e: e.dma_start(out=wo_bf[:, :], in_=Wo[:, :]))

    xp_k = [0]

    def transpose_tiles(src_sb, n_ic, dst_sb, dst_off, dst_chunk_stride, f32=False):
        gsz = 4 if f32 else 8
        assert n_ic % gsz == 0
        idn = identf_sb if f32 else ident_sb
        for half in range(n_ic // gsz):
            k = xp_k[0]
            slot = pxp_s[k % 2]
            ps = pxp[k % 2][:, :].bitcast(F32) if f32 else pxp[k % 2][:, :]
            slot.acquire("t")
            for i in range(gsz):
                ic = half * gsz + i
                p.op("t", lambda e, ic=ic, ps=ps, i=i, src_sb=src_sb, idn=idn: e.transpose(
                    ps[:, i * 128:(i + 1) * 128],
                    src_sb[:, ic * 128:(ic + 1) * 128],
                    idn[:, :]), inc=("t" if i == gsz - 1 else None))
            p.wait("c", "t", p.val("t"))
            dst = bass.AP(dst_sb, dst_off + (half * gsz) * dst_chunk_stride,
                          [[dst_sb.shape[1], 128], [dst_chunk_stride, gsz], [1, 128]])
            p.op("c", lambda e, ps=ps, dst=dst, gsz=gsz: e.copy(
                dst, ps[:, 0:gsz * 128]), inc="c")
            slot.released("c")
            xp_k[0] += 1

    def transpose_weight_g(w_ap, wt_sb, O, I, nat_bufs, nat_slots):
        nb = len(nat_bufs)
        for rc in range(O // 128):
            slot = rc % nb
            nat_slots[slot].acquire("g")
            nat_slots[slot].dma("g", lambda e, rc=rc, slot=slot, w_ap=w_ap: e.dma_start(
                out=nat_bufs[slot][:, :], in_=w_ap[rc * 128:(rc + 1) * 128, :]))
            nat_slots[slot].wait_ready("t")
            transpose_tiles(nat_bufs[slot], I // 128, wt_sb, rc * 128, O, f32=False)
            nat_slots[slot].released("t")

    def transpose_weight(w_ap, wt_sb, O, I, nat_bufs, nat_slots, f32=True):
        nb = len(nat_bufs)
        for rc in range(O // 128):
            slot = rc % nb
            nat_slots[slot].acquire("sy")
            nat_slots[slot].dma("sy", lambda e, rc=rc, slot=slot, w_ap=w_ap: e.dma_start(
                out=nat_bufs[slot][:, :], in_=w_ap[rc * 128:(rc + 1) * 128, :]))
            nat_slots[slot].wait_ready("t")
            transpose_tiles(nat_bufs[slot], I // 128, wt_sb, rc * 128, O, f32=f32)
            nat_slots[slot].released("t")

    sb.push()
    qnat = sb.alloc("qnat", [128, HID], BF16)
    q_ub = sb.alloc("q_ub", [128, 2048], BF16)
    q_col = sb.alloc("q_col", [128, 32], F32)
    q_e1 = sb.alloc("q_e1", [128, 512], F32)
    q_e2 = sb.alloc("q_e2", [128, 1536], F32)
    q_w1, q_w2 = q_e2[:, 0:1024], q_e2[:, 1024:1536]
    q_sg = sb.alloc("q_sg", [128, 2048], BF16)
    q_sexpb = sb.alloc("q_sexpb", [128, 2048], BF16)
    q_prb = sb.alloc("q_prb", [128, 2048], BF16)
    q_wq = [sb.alloc(f"q_wq{i}", [128, 2048], BF16) for i in range(2)]
    q_wT = [sb.alloc(f"q_wT{i}", [128, 2048], BF16) for i in range(2)]

    qnat_s = Slot(p, dma=True)
    qst_s = [Slot(p, dma=True), Slot(p, dma=True)]
    qeq_s = Slot(p)
    qsgn_s = Slot(p)
    qwq_s = [Slot(p), Slot(p)]
    qwT_s = [Slot(p), Slot(p)]
    q_pb = [0]

    def quant_piece_q(off):
        width = 2048
        G = width // 64
        qeq_s.acquire("v")
        qnat_s.wait_ready("v")
        srcw = qnat[:, off:off + width]
        p.op("v", lambda e, srcw=srcw: e.scalar_tensor_tensor(q_ub[:, :], srcw, -1.0, srcw, ALU.mult, ALU.max))
        u3 = q_ub[:, :].rearrange("p (g k) -> p g k", k=64)
        w13 = q_w1.rearrange("p (g k) -> p g k", k=32)
        w23 = q_w2.rearrange("p (g k) -> p g k", k=16)
        vbubble()
        p.op("v", lambda e: e.tensor_tensor(w13, u3[:, :, 0:32], u3[:, :, 32:64], ALU.add))
        vbubble()
        p.op("v", lambda e: e.tensor_tensor(w23, w13[:, :, 0:16], w13[:, :, 16:32], ALU.add))
        vbubble()
        p.op("v", lambda e: e.tensor_tensor(w13[:, :, 0:8], w23[:, :, 0:8], w23[:, :, 8:16], ALU.add))
        vbubble()
        p.op("v", lambda e: e.tensor_tensor(w23[:, :, 0:4], w13[:, :, 0:4], w13[:, :, 4:8], ALU.add))
        vbubble()
        p.op("v", lambda e: e.tensor_tensor(w13[:, :, 0:2], w23[:, :, 0:2], w23[:, :, 2:4], ALU.add))
        vbubble()
        p.op("v", lambda e: e.tensor_tensor(q_col[:, :], w13[:, :, 0:1].rearrange("p g k -> p (g k)"),
                                            w13[:, :, 1:2].rearrange("p g k -> p (g k)"), ALU.add))
        vbubble()
        p.op("v", lambda e: e.tensor_scalar(q_e1[:, 0:G], q_col[:, :], 1.0 / 64, 1e-8, ALU.mult, ALU.max))
        src_, dst_ = q_e1, q_e2
        for m in range(5):
            bs = 1 << m
            vbubble()
            s_in = src_[:, 0:G * bs].rearrange("p (g k) -> p g k", k=bs)
            d3 = dst_[:, 0:G * 2 * bs].rearrange("p (g k) -> p g k", k=2 * bs)
            p.op("v", lambda e, d3=d3, s_in=s_in, bs=bs: e.tensor_copy(d3[:, :, 0:bs], s_in))
            p.op("v", lambda e, d3=d3, s_in=s_in, bs=bs: e.tensor_copy(d3[:, :, bs:2 * bs], s_in))
            src_, dst_ = dst_, src_
        bs = 32
        vbubble()
        s_in = src_[:, 0:G * bs].rearrange("p (g k) -> p g k", k=bs)
        d3 = q_sexpb[:, :].rearrange("p (g k) -> p g k", k=2 * bs)
        p.op("v", lambda e, d3=d3, s_in=s_in: e.tensor_copy(d3[:, :, 0:bs], s_in))
        p.op("v", lambda e, d3=d3, s_in=s_in: e.tensor_copy(d3[:, :, bs:2 * bs], s_in))
        vbubble()
        p.op("v", lambda e: e.scalar_tensor_tensor(
            q_ub[:, :], q_ub[:, :], 2.0, q_sexpb[:, :], ALU.mult, ALU.is_gt))
        vbubble()
        p.op("v", lambda e: e.tensor_tensor(q_prb[:, :], q_ub[:, :], q_sexpb[:, :], ALU.mult))
        qsgn_s.acquire("c")
        qnat_s.wait_ready("c")
        p.op("c", lambda e, srcw=srcw: e.activation(q_sg[:, :], srcw, AF.Sign), inc="c")
        c_done = p.val("c")
        p.wait("v", "c", c_done)
        vbubble()
        pb = q_pb[0]
        qwq_s[pb].acquire("v")
        p.op("v", lambda e, pb=pb: e.tensor_tensor(q_wq[pb][:, :], q_prb[:, :], q_sg[:, :], ALU.mult), inc="v")
        qsgn_s.released("v")
        return p.val("v")

    tdefer = []

    def quant_xpose_store(out_dram_ap_fn):
        vv = p.val("v")
        pb = q_pb[0]
        q_pb[0] ^= 1

        def fn(vv=vv, pb=pb, out_dram_ap_fn=out_dram_ap_fn):
            p.wait("t", "v", vv)
            qwT_s[pb].acquire("c")
            transpose_tiles(q_wq[pb], 16, q_wT[pb], 0, 128)
            qwq_s[pb].released("t")
            p.wait("sy", "c", p.val("c"))
            qst_s[pb].acquire("sy")
            qst_s[pb].dma("sy", lambda e, pb=pb: out_dram_ap_fn(e, pb))
            qwT_s[pb].released(qst_s[pb].sem)
        tdefer.append(fn)

    def make_q_emitters():
        ems = []
        for ch in range(GUP // 8 // 128):
            def em(ch=ch):
                qnat_s.acquire("g")
                qnat_s.dma("g", lambda e, ch=ch: e.dma_start(
                    out=qnat[:, 0:E], in_=wgu_sh[ch * 128:(ch + 1) * 128, :]))
                quant_piece_q(0)
                qnat_s.released("v")
                qnat_s.released("c")
                qeq_s.released("v")
                dst = bass.AP(agu_in.tensor, (ch // 4) * (E * 512) + (ch % 4) * 128,
                              [[512, 128], [128 * 512, 16], [1, 128]])
                quant_xpose_store(lambda e, pb, dst=dst: e.dma_start(
                    out=dst, in_=q_wT[pb][:, :].rearrange("p (c j) -> p c j", j=128)))
            ems.append(em)
        for ch in range(E // 8 // 128):
            for piece in range(4):
                def em(ch=ch, piece=piece):
                    wpf_s.acquire("sy")
                    wpf_s.dma("sy", lambda e, ch=ch, piece=piece: e.dma_start(
                        out=wpf[0][:, :],
                        in_=wp_sh[ch * 128:(ch + 1) * 128, piece * 2048:(piece + 1) * 2048]))
                    qnat_s.acquire("c")
                    wpf_s.wait_ready("c")
                    p.op("c", lambda e: e.activation(qnat[:, 0:2048], wpf[0][:, :], AF.Identity), inc="c")
                    p.wait("c", "c", p.val("c"))
                    wpf_s.released("c")
                    p.wait("v", "c", p.val("c"))
                    quant_piece_q(0)
                    qnat_s.released("v")
                    qnat_s.released("c")
                    qeq_s.released("v")
                    dst = bass.AP(ap_in.tensor, (piece * 16 * 128) * (E // 8) + ch * 128,
                                  [[E // 8, 128], [128 * (E // 8), 16], [1, 128]])
                    quant_xpose_store(lambda e, pb, dst=dst: e.dma_start(
                        out=dst, in_=q_wT[pb][:, :].rearrange("p (c j) -> p c j", j=128)))
                ems.append(em)
        return ems

    q_emitters = make_q_emitters()
    agq_s = Slot(p, dma=True)
    rg8 = [[0, 1, 2, 3, 4, 5, 6, 7]]

    def emit_gu_slice_trigger(s):
        p.wait("g", qst_s[0].sem, qst_s[0].ready())
        p.wait("g", qst_s[1].sem, qst_s[1].ready())
        ins = bass.AP(agu_in.tensor, s * (E * 512), [[512, E], [1, 512]])
        outs = bass.AP(agu_out.tensor, s * (8 * E * 512), [[512, 8 * E], [1, 512]])
        p.op("g", lambda e, ins=ins, outs=outs: e.collective_compute(
            "AllGather", ALU.bypass, ins=[ins], outs=[outs],
            replica_groups=rg8), inc=agq_s.sem, inc_n=1)

    def emit_p_trigger():
        p.wait("g", qst_s[0].sem, qst_s[0].ready())
        p.wait("g", qst_s[1].sem, qst_s[1].ready())
        p.op("g", lambda e: e.collective_compute(
            "AllGather", ALU.bypass, ins=[ap_in[:, :]], outs=[ap_out[:, :]],
            replica_groups=rg8), inc=agq_s.sem, inc_n=1)

    sb.push()
    wkT = sb.alloc("wkT", [128, DC * (E // 2)], BF16)
    wvT = sb.alloc("wvT", [128, DC * (E // 2)], BF16)
    xb_off = sb.off
    xbuf = [sb.alloc(f"xbuf{i}", [128, E], F32) for i in range(2)]
    xb_end = sb.off
    sb.off = xb_off
    kvf = sb.alloc("kvf", [128, E], F32)
    sb.off = xb_end
    xn_tm = [sb.alloc(f"xn_tm{i}", [128, E], BF16) for i in range(2)]
    xn_st = sb.alloc("xn_st", [128, DC * STW], BF16)
    kbuf = sb.alloc("kbuf", [128, E // 2], BF16)
    vbuf = sb.alloc("vbuf", [128, E // 2], BF16)
    brow = sb.alloc("brow", [128, E // 2], BF16)
    blo = sb.alloc("blo", [128, E // 2], BF16)
    stats = sb.alloc("stats", [128, 24], F32)
    aggr = sb.alloc("aggr", [128, 2], F32)
    veps = sb.alloc("veps", [128, 1], F32)
    vrec = sb.alloc("vrec", [128, 1], F32)
    vinv = sb.alloc("vinv", [128, 1], F32)
    negm = sb.alloc("negm", [128, 1], F32)
    negmb = sb.alloc("negmb", [128, 1], F32)
    fb_off = sb.off
    fbias = sb.alloc("fbias", [128, E // 2], F32)
    natw = [sb.alloc(f"natw{i}", [128, E], BF16) for i in range(2)]
    fb_end = sb.off
    sb.off = fb_off
    wpf = [sb.alloc("wpf", [128, 2048], F32)]
    sb.off = fb_end
    natw_s = [Slot(p, dma=True), Slot(p, dma=True)]
    wpf_s = Slot(p, dma=True)
    xbuf_s = [Slot(p, dma=True), Slot(p, dma=True)]
    bias_s = Slot(p, dma=True)
    xnsp_s = Slot(p, dma=True)
    xn_tm_s = [Slot(p), Slot(p)]
    xn_st_s = Slot(p)
    kvbuf_s = Slot(p)

    phase_barrier([qnat_s, qst_s[0], qst_s[1]], engines=("v", "c", "sy", "t"))
    bias_s.dma("sy", lambda e: e.dma_start(out=fbias[0:1, :], in_=bk_d[:, :]))
    bias_s.dma("sy", lambda e: e.dma_start(out=fbias[1:2, :], in_=bv_d[:, :]))
    BIAS_ALL = bias_s.ready()

    p.wait("t", cst.sem, CST_ALL)
    transpose_weight_g(Wk_sh, wkT, E // 2, E, natw, natw_s)
    transpose_weight_g(Wv_sh, wvT, E // 2, E, natw, natw_s)
    NATW_T = p.val("t")

    p.op("v", lambda e: e.memset(kv_acc[:, :], 0.0))
    p.wait("v", bias_s.sem, BIAS_ALL)
    p.op("v", lambda e: e.memset(brow[:, :], 0.0))
    vbubble()
    p.op("v", lambda e: e.tensor_copy(brow[0:2, :], fbias[0:2, :]))
    vbubble()
    p.op("v", lambda e: e.tensor_tensor(blo[0:2, :], fbias[0:2, :], brow[0:2, :], ALU.subtract), inc="v")
    BIAS_V = p.val("v")
    p.wait("sy", "v", BIAS_V)
    p.wait("sy", "t", NATW_T)
    p.wait("sy", "v", BIAS_V)
    bias_s.dma("sy", lambda e: e.dma_start(out=brow[2:4, :], in_=blo[0:2, :]))
    p.wait("t", "v", BIAS_V)
    bias_s.wait_ready("t")

    apply_cval = {}
    negmb_cval = {}

    def emit_x_load(tcg):
        xsl = tcg % 2
        xbuf_s[xsl].acquire("sy")
        xbuf_s[xsl].dma("sy", lambda e, tcg=tcg, xsl=xsl: e.dma_start(
            out=xbuf[xsl][:, :], in_=x_all[tcg * 128:(tcg + 1) * 128, :]))

    def emit_ln(tcg):
        xsl = tcg % 2
        xbuf_s[xsl].wait_ready("v")
        for i in range(4):
            p.op("v", lambda e, i=i, xsl=xsl: e.bn_stats(
                stats[:, 6 * i:6 * (i + 1)], xbuf[xsl][:, 512 * i:512 * (i + 1)]))
        vbubble()
        p.op("v", lambda e: e.bn_aggr(aggr[:, :], stats[:, :]))
        vbubble()
        p.op("v", lambda e: e.tensor_scalar(veps[:, :], aggr[:, 1:2], 1.0, EPS_LN, ALU.mult, ALU.add))
        if tcg >= 1:
            p.wait("v", "c", negmb_cval[tcg - 1])
        vbubble()
        p.op("v", lambda e: e.tensor_scalar(negm[:, :], aggr[:, 0:1], -1.0, 0.0, ALU.mult, ALU.add))
        vbubble()
        p.op("v", lambda e: e.reciprocal(vrec[:, :], veps[:, :]), inc="v")
        p.wait("c", "v", p.val("v"))
        p.op("c", lambda e: e.activation(vinv[:, :], vrec[:, :], AF.Sqrt))
        p.wait("c", "c", p.val("c"))
        p.op("c", lambda e: e.activation(negmb[:, :], negm[:, :], AF.Identity,
                                         scale=vinv[:, 0:1]), inc="c")
        negmb_cval[tcg] = p.val("c")
        p.wait("c", "c", p.val("c"))
        xn_tm_s[xsl].acquire("c")
        xbuf_s[xsl].wait_ready("c")
        p.op("c", lambda e, xsl=xsl: e.activation(
            xn_tm[xsl][:, :], xbuf[xsl][:, :], AF.Identity,
            bias=negmb[:, 0:1], scale=vinv[:, 0:1]), inc="c")
        apply_cval[tcg] = p.val("c")
        xbuf_s[xsl].released("c")
        if tcg + 2 < 32:
            emit_x_load(tcg + 2)

    emit_x_load(0)
    emit_x_load(1)
    emit_ln(0)

    q_iter = iter(q_emitters)
    XN_EVAC_C = 0
    vdefer = []
    for st in range(16):
        xn_st_s.acquire("c")
        for tc in range(2):
            tcg = 2 * st + tc
            p.wait("t", "c", apply_cval[tcg])
            transpose_tiles(xn_tm[tcg % 2], DC, xn_st, tc * 128, STW)
            xn_tm_s[tcg % 2].released("t")
            if tc == 1:
                XN_EVAC_C = p.val("c")
            if tcg + 1 < 32:
                emit_ln(tcg + 1)
            if tc == 0:
                for fn in vdefer:
                    fn()
                vdefer = []
                for fn in tdefer:
                    fn()
                tdefer.clear()
            else:
                em = next(q_iter, None)
                if em is not None:
                    em()
                if st < 8:
                    em = next(q_iter, None)
                    if em is not None:
                        em()

        if st < 8:
            p.wait("sy", "c", XN_EVAC_C)
            xnsp_s.acquire("sy")
            for dcc in range(DC):
                xnsp_s.dma("sy", lambda e, dcc=dcc, st=st: e.dma_start(
                    out=xn_sp[dcc * 128:(dcc + 1) * 128, st * STW:(st + 1) * STW],
                    in_=xn_st[:, dcc * STW:(dcc + 1) * STW]))
            xn_st_s.released(xnsp_s.sem)

        p.wait("t", "c", XN_EVAC_C)
        for tc in range(2):
            if tc == 1:
                for fn in vdefer:
                    fn()
                vdefer = []
            kvbuf_s.acquire("c")
            for ft in range(2):
                for which in ("k", "v"):
                    si = (ft % 2) if which == "k" else (2 + ft % 2)
                    psl = pmm[si]
                    pmm_s[si].acquire("t")
                    wt = wkT if which == "k" else wvT
                    osel = ones_k_sb if which == "k" else ones_v_sb
                    for dcc in range(DC):
                        p.op("t", lambda e, dcc=dcc, tc=tc, ft=ft, psl=psl, wt=wt: e.matmul(
                            psl[:, :],
                            lhsT=xn_st[:, dcc * STW + tc * 128: dcc * STW + (tc + 1) * 128],
                            rhs=wt[:, dcc * (E // 2) + ft * 512: dcc * (E // 2) + (ft + 1) * 512],
                            start=(dcc == 0), stop=False))
                    p.op("t", lambda e, psl=psl, osel=osel, ft=ft: e.matmul(
                        psl[:, :], lhsT=osel[:, :],
                        rhs=brow[:, ft * 512:(ft + 1) * 512],
                        start=False, stop=True), inc="t")
                    p.wait("c", "t", p.val("t"))
                    dstb = kbuf if which == "k" else vbuf
                    p.op("c", lambda e, psl=psl, dstb=dstb, ft=ft: e.activation(
                        dstb[:, ft * 512:(ft + 1) * 512], psl[:, :], AF.Identity), inc="c")
                    pmm_s[si].released("c")
            p.wait("t", "c", p.val("c"))
            for hq in range(2):
                pkvh_s[hq].acquire("t")
                for i in range(4):
                    h = hq * 4 + i
                    p.op("t", lambda e, h=h, i=i, hq=hq: e.matmul(
                        pkv[:, hq * 512 + i * 128: hq * 512 + (i + 1) * 128],
                        lhsT=kbuf[:, h * 128:(h + 1) * 128],
                        rhs=vbuf[:, h * 128:(h + 1) * 128],
                        start=True, stop=True), inc=("t" if i == 3 else None))
                tval = p.val("t")

                def kv_add(hq=hq, tval=tval):
                    p.wait("v", "t", tval)
                    vbubble()
                    p.op("v", lambda e, hq=hq: e.tensor_tensor(
                        kv_acc[:, hq * 512:(hq + 1) * 512],
                        kv_acc[:, hq * 512:(hq + 1) * 512],
                        pkv[:, hq * 512:(hq + 1) * 512], ALU.add), inc="v")
                    pkvh_s[hq].released("v")
                vdefer.append(kv_add)
            kvbuf_s.released("t")
        xn_st_s.released("t")

    for fn in vdefer:
        fn()
    vdefer = []
    for em in q_iter:
        em()
    for fn in tdefer:
        fn()
    tdefer.clear()

    AB_V = p.val("v")
    AB_T = p.val("t")
    XNSP_ALL = xnsp_s.ready()

    dbg_s = Slot(p, dma=True) if dbg_aps else None
    if "dbg_xn" in dbg_aps:
        p.wait("sy", xnsp_s.sem, XNSP_ALL)
        dbg_s.dma("sy", lambda e: e.dma_start(out=dbg_aps["dbg_xn"][:, :], in_=xn_sp[:, :]))

    sb.pop()
    sb.pop()

    akvw_s = Slot(p, dma=True)
    akvr_s = Slot(p, dma=True)
    agkv_s = Slot(p, dma=True)
    p.wait("g", "v", AB_V)
    akvw_s.dma("g", lambda e: e.dma_start(out=akv_in[:, :], in_=kv_acc[:, :]))
    p.wait("g", akvw_s.sem, akvw_s.ready())
    rg2 = [[0, 1], [2, 3], [4, 5], [6, 7]]
    p.op("g", lambda e: e.collective_compute(
        "AllGather", ALU.bypass, ins=[akv_in[:, :]], outs=[akv_out[:, :]],
        replica_groups=rg2), inc=agkv_s.sem, inc_n=1)
    p.wait("g", agkv_s.sem, p.val(agkv_s.sem))
    for r in range(2):
        akvr_s.dma("g", lambda e, r=r: e.dma_start(
            out=kvf[:, r * 1024:(r + 1) * 1024],
            in_=akv_out[r * 128:(r + 1) * 128, :]))
    akvr_s.wait_ready("v")
    if "dbg_kv" in dbg_aps:
        p.wait("sy", akvr_s.sem, akvr_s.ready())
        dbg_s.dma("sy", lambda e: e.dma_start(out=dbg_aps["dbg_kv"][:, :], in_=kvf[:, :]))
    p.op("v", lambda e: e.tensor_scalar(kvb[:, :], kvf[:, :], ATT_SCALE, 0.0, ALU.mult, ALU.add), inc="v")
    KVB_V = p.val("v")

    sb.push()
    R1_off = sb.off
    wqT = sb.alloc("wqT", [128, DC * E], BF16)
    R2_off = sb.off
    q_own = sb.alloc("q_own", [128, DC * E], BF16)
    xnt = [sb.alloc(f"xnt{i}", [128, DC * 512], BF16) for i in range(2)]
    natw1 = [sb.alloc(f"natw1_{i}", [128, E], BF16) for i in range(2)]
    natw1_s = [Slot(p, dma=True) for _ in range(2)]
    xnt_s = [Slot(p, dma=True), Slot(p, dma=True)]

    p.wait("sy", "t", AB_T)
    p.wait("sy", "v", KVB_V)
    p.wait("sy", "c", p.val("c"))
    p.wait("sy", wqbf_s.sem, wqbf_s.ready())
    p.wait("c", "t", AB_T)
    p.wait("c", "v", AB_V)
    p.wait("c", cst.sem, CST_ALL)
    phase_barrier([qnat_s, qst_s[0], qst_s[1]])
    phase_barrier([xnsp_s, xbuf_s[0], xbuf_s[1], natw_s[0], natw_s[1], bias_s, wpf_s])

    transpose_weight(wq_bf, wqT, E, E, natw1, natw1_s, f32=False)
    p.wait("c", "v", KVB_V)
    p.wait("t", "c", p.val("c"))

    for tile in range(NTILE):
        slot = tile % 2
        xnt_s[slot].acquire("sy")
        p.wait("sy", xnsp_s.sem, XNSP_ALL)
        for dcc in range(DC):
            xnt_s[slot].dma("sy", lambda e, dcc=dcc, tile=tile, slot=slot: e.dma_start(
                out=xnt[slot][:, dcc * 512:(dcc + 1) * 512],
                in_=xn_sp[dcc * 128:(dcc + 1) * 128, tile * 512:(tile + 1) * 512]))
        xnt_s[slot].wait_ready("t")
        for oc in range(DC):
            si = oc % 4
            psl = pmm[si]
            pmm_s[si].acquire("t")
            for dcc in range(DC):
                p.op("t", lambda e, dcc=dcc, oc=oc, slot=slot, psl=psl: e.matmul(
                    psl[:, :],
                    lhsT=wqT[:, dcc * E + oc * 128: dcc * E + (oc + 1) * 128],
                    rhs=xnt[slot][:, dcc * 512:(dcc + 1) * 512],
                    start=(dcc == 0), stop=(dcc == DC - 1)),
                    inc=("t" if dcc == DC - 1 else None))
            p.wait("c", "t", p.val("t"))
            p.op("c", lambda e, psl=psl, oc=oc, tile=tile: e.activation(
                q_own[:, oc * E + tile * 512: oc * E + (tile + 1) * 512],
                psl[:, :], AF.Identity, bias=bq_sb[:, oc:oc + 1], scale=1.0), inc="c")
            pmm_s[si].released("c")
        xnt_s[slot].released("t")
    C1_C = p.val("c")
    C1_T = p.val("t")

    if "dbg_q" in dbg_aps:
        p.wait("sy", "c", C1_C)
        dbg_s.dma("sy", lambda e: e.dma_start(out=dbg_aps["dbg_q"][:, :], in_=q_own[:, :]))

    sb.pop()

    sb.push()
    assert sb.off == R1_off
    a_own = sb.alloc("a_own", [128, DC * E], BF16)
    assert sb.off == R2_off
    sb.off = R2_off + DC * E * 2
    q_own2 = q_own

    p.wait("t", "v", KVB_V)
    p.wait("t", "c", C1_C)
    for tile in range(NTILE):
        for h in range(H):
            si = (tile * H + h) % 4
            psl = pmm[si]
            pmm_s[si].acquire("t")
            p.op("t", lambda e, h=h, tile=tile, psl=psl: e.matmul(
                psl[:, :],
                lhsT=kvb[:, h * 128:(h + 1) * 128],
                rhs=q_own2[:, h * E + tile * 512: h * E + (tile + 1) * 512],
                start=True, stop=True), inc="t")
            p.wait("c", "t", p.val("t"))
            p.op("c", lambda e, psl=psl, h=h, tile=tile: e.copy(
                a_own[:, h * E + tile * 512: h * E + (tile + 1) * 512], psl[:, :]), inc="c")
            pmm_s[si].released("c")
    C2_V = p.val("c")

    if "dbg_attn" in dbg_aps:
        p.wait("sy", "c", C2_V)
        dbg_s.dma("sy", lambda e: e.dma_start(out=dbg_aps["dbg_attn"][:, :], in_=a_own[:, :]))

    sb.off = R2_off
    woT = sb.alloc("woT", [128, DC * E], BF16)
    bo_b = sb.alloc("bo_b", [128, E], F32)
    xchunk = [sb.alloc("xchunk0", [128, E], F32)]
    stg_a = [sb.alloc(f"stg_a{i}", [128, 512], F32) for i in range(4)]
    pstage = sb.alloc("pstage0", [128, E], F32)
    natw2 = [sb.alloc(f"natw2_{i}", [128, E], BF16) for i in range(2)]
    natw2_s = [Slot(p, dma=True), Slot(p, dma=True)]
    xchunk_s = [Slot(p, dma=True)]
    pstg_s = Slot(p, dma=True)

    p.wait("g", "t", p.val("t"))
    p.wait("sy", "t", C1_T)
    phase_barrier([xnt_s[0], xnt_s[1], natw1_s[0], natw1_s[1]])
    bias_s.dma("sy", lambda e: e.dma_start(out=bo_b[:, :], in_=bo_d[:, :].broadcast_to([128, E])))
    BO_ALL = bias_s.ready()
    p.wait("sy", wobf_s.sem, wobf_s.ready())
    transpose_weight(wo_bf, woT, E, E, natw2, natw2_s, f32=False)
    p.wait("g", natw2_s[0].sem, natw2_s[0].ready())
    p.wait("g", natw2_s[1].sem, natw2_s[1].ready())
    AGBASE = p.val(agq_s.sem)
    emit_gu_slice_trigger(0)
    p.wait("t", "c", p.val("c"))
    p.wait("v", bias_s.sem, BO_ALL)

    for tkc in range(NTKC):
        slot = 0
        xchunk_s[slot].acquire("sy")
        xchunk_s[slot].dma("sy", lambda e, tkc=tkc, slot=slot: e.dma_start(
            out=xchunk[slot][:, :], in_=x_all[tkc * 128:(tkc + 1) * 128, :]))
        for ot in range(4):
            pmm_s[ot].acquire("t")
            for dcc in range(DC):
                p.op("t", lambda e, dcc=dcc, tkc=tkc, ot=ot: e.matmul(
                    pmm[ot][:, :],
                    lhsT=a_own[:, dcc * E + tkc * 128: dcc * E + (tkc + 1) * 128],
                    rhs=woT[:, dcc * E + ot * 512: dcc * E + (ot + 1) * 512],
                    start=(dcc == 0), stop=(dcc == DC - 1)),
                    inc=("t" if dcc == DC - 1 else None))
        p.wait("v", "t", p.val("t"))
        xchunk_s[slot].wait_ready("v")
        pstg_s.acquire("v")
        for ot in range(4):
            p.op("v", lambda e, ot=ot: e.tensor_tensor(
                stg_a[ot][:, :], pmm[ot][:, :], bo_b[:, ot * 512:(ot + 1) * 512], ALU.add),
                inc=("v" if ot == 3 else None))
        for ot in range(4):
            pmm_s[ot].released("v")
        vbubble()
        for ot in range(4):
            p.op("v", lambda e, ot=ot, slot=slot: e.tensor_tensor(
                pstage[:, ot * 512:(ot + 1) * 512], stg_a[ot][:, :],
                xchunk[slot][:, ot * 512:(ot + 1) * 512], ALU.add),
                inc=("v" if ot == 3 else None))
        xchunk_s[slot].released("v")
        p.wait("sy", "v", p.val("v"))
        pstg_s.dma("sy", lambda e, tkc=tkc: e.dma_start(
            out=part_d[tkc * 128:(tkc + 1) * 128, :], in_=pstage[:, :]))
        pstg_s.released_dma()
    PSTG_ALL = pstg_s.ready()
    C3_T = p.val("t")

    if "dbg_part" in dbg_aps:
        p.wait("sy", pstg_s.sem, PSTG_ALL)
        dbg_s.dma("sy", lambda e: e.dma_start(out=dbg_aps["dbg_part"][:, :], in_=part_d[:, :]))

    sb.pop()

    sb.push()
    xn_own = sb.alloc("xn_own", [128, DC * TOK], BF16)
    wgT = [sb.alloc(f"wgT{i}", [128, E], BF16) for i in range(2)]
    wuT = [sb.alloc(f"wuT{i}", [128, E], BF16) for i in range(2)]
    sil = [sb.alloc(f"sil{i}", [128, 512], F32) for i in range(2)]
    h_stb = [sb.alloc(f"h_st{i}", [128, 512], BF16) for i in range(4)]
    hsq_stb = [sb.alloc(f"hsq_st{i}", [128, 512], BF16) for i in range(4)]

    gu_nat_s = [Slot(p, dma=True), Slot(p, dma=True)]
    xno_s = Slot(p, dma=True)
    hst_s = [Slot(p, dma=True) for _ in range(4)]
    wT_s = [Slot(p), Slot(p)]
    sil_s = [Slot(p), Slot(p)]
    hsq_s = [Slot(p) for _ in range(4)]

    p.wait("g", "t", C3_T)
    p.wait("g", "v", p.val("v"))
    phase_barrier([pstg_s, xchunk_s[0], natw2_s[0], natw2_s[1], bias_s])
    p.wait("sy", xnsp_s.sem, XNSP_ALL)
    for dcc in range(DC):
        xno_s.dma("sy", lambda e, dcc=dcc: e.dma_start(
            out=xn_own[:, dcc * TOK:(dcc + 1) * TOK],
            in_=xn_sp[dcc * 128:(dcc + 1) * 128, :]))
    xno_s.wait_ready("t")
    p.wait("t", "t", C3_T)

    ss_started = [False]
    ss_last_t = [0]
    WT_READY = {}

    SH = GUP // 8

    def _guT_src(h_global):
        core, ch = divmod(h_global, 16)
        return bass.AP(agu_out.tensor,
                       (ch // 4) * (8 * E * 512) + core * (E * 512) + (ch % 4) * 128,
                       [[512, 128], [128 * 512, DC], [1, 128]])

    def emit_gu_load_quant(pc):
        slot = pc % 2
        wT_s[slot].acquire("sy")
        gu_nat_s[slot].acquire("sy")
        gu_nat_s[slot].dma("sy", lambda e, pc=pc, slot=slot: e.dma_start(
            out=wgT[slot][:, :].rearrange("p (c j) -> p c j", j=128),
            in_=_guT_src(pc)))
        gu_nat_s[slot].dma("sy", lambda e, pc=pc, slot=slot: e.dma_start(
            out=wuT[slot][:, :].rearrange("p (c j) -> p c j", j=128),
            in_=_guT_src(64 + pc)))
        WT_READY[pc] = gu_nat_s[slot].ready()

    def emit_gu_mm(qc, is_last=False):
        slot = qc % 2
        hsq_counts = []
        p.wait("t", gu_nat_s[slot].sem, WT_READY[qc])
        for tile in range(4):
            sig, siu = tile % 2, 2 + tile % 2
            psg, psu = pmm[sig], pmm[siu]
            pmm_s[sig].acquire("t")
            pmm_s[siu].acquire("t")
            for dcc in range(DC):
                p.op("t", lambda e, dcc=dcc, tile=tile, psg=psg, slot=slot: e.matmul(
                    psg[:, :],
                    lhsT=wgT[slot][:, dcc * 128:(dcc + 1) * 128],
                    rhs=xn_own[:, dcc * TOK + tile * 512: dcc * TOK + (tile + 1) * 512],
                    start=(dcc == 0), stop=(dcc == DC - 1)),
                    inc=("t" if dcc == DC - 1 else None))
            TG = p.val("t")
            for dcc in range(DC):
                p.op("t", lambda e, dcc=dcc, tile=tile, psu=psu, slot=slot: e.matmul(
                    psu[:, :],
                    lhsT=wuT[slot][:, dcc * 128:(dcc + 1) * 128],
                    rhs=xn_own[:, dcc * TOK + tile * 512: dcc * TOK + (tile + 1) * 512],
                    start=(dcc == 0), stop=(dcc == DC - 1)),
                    inc=("t" if dcc == DC - 1 else None))
            TU = p.val("t")
            p.wait("c", "t", TG)
            sil_s[tile % 2].acquire("c")
            p.op("c", lambda e, psg=psg, tile=tile: e.activation(
                sil[tile % 2][:, :], psg[:, :], AF.Silu), inc="c")
            SIL_C = p.val("c")
            pmm_s[sig].released("c")
            hslot = tile
            p.wait("v", "t", TU)
            p.wait("v", "c", SIL_C)
            h_s_cur = hst_s[hslot]
            h_s_cur.acquire("v")
            p.op("v", lambda e, tile=tile, psu=psu, hslot=hslot: e.tensor_tensor(
                h_stb[hslot][:, :], sil[tile % 2][:, :], psu[:, :], ALU.mult), inc="v")
            HB_V = p.val("v")
            sil_s[tile % 2].released("v")
            pmm_s[siu].released("v")
            vbubble()
            hsq_s[tile].acquire("v")
            p.op("v", lambda e, hslot=hslot, tile=tile: e.tensor_tensor(
                hsq_stb[tile][:, :], h_stb[hslot][:, :], h_stb[hslot][:, :], ALU.mult), inc="v")
            hsq_counts.append(p.val("v"))
            p.wait("sy", "v", HB_V)
            h_s_cur.dma("sy", lambda e, qc=qc, tile=tile, hslot=hslot: e.dma_start(
                out=hsp[qc * 128:(qc + 1) * 128, tile * 512:(tile + 1) * 512],
                in_=h_stb[hslot][:, :]))
            h_s_cur.released_dma()
        for tile in range(4):
            p.wait("t", "v", hsq_counts[tile])
            first = not ss_started[0]
            ss_started[0] = True
            p.op("t", lambda e, tile=tile, first=first, last=(is_last and tile == 3): e.matmul(
                pkv[0:4, 512:1024],
                lhsT=sel4_sb[:, 4 * tile:4 * (tile + 1)],
                rhs=hsq_stb[tile][:, :],
                start=first, stop=last), inc="t")
            ss_last_t[0] = p.val("t")
            hsq_s[tile].released("t")
        wT_s[slot].released("t")

    qorder = [16 * bp + 4 * s + i for s in range(4) for bp in range(4) for i in range(4)]
    for k in range(65):
        if k < 64:
            if k % 16 == 0:
                p.wait("sy", agq_s.sem, AGBASE + k // 16 + 1)
            emit_gu_load_quant(qorder[k])
        if k in (2, 18, 34):
            p.wait("g", "t", p.val("t"))
            emit_gu_slice_trigger(k // 16 + 1)
        if k == 40:
            p.wait("g", "t", p.val("t"))
            emit_p_trigger()
            AGW_DONE = p.val(agq_s.sem)
        if k >= 1:
            emit_gu_mm(qorder[k - 1], is_last=(k == 64))

    HST_ALL = [s.ready() for s in hst_s]

    if "dbg_h" in dbg_aps:
        for s, vtot in zip(hst_s, HST_ALL):
            p.wait("sy", s.sem, vtot)
        dbg_s.dma("sy", lambda e: e.dma_start(out=dbg_aps["dbg_h"][:, :], in_=hsp[:, :]))

    ssd_s = Slot(p, dma=True)
    invr_s = Slot(p, dma=True)
    p.wait("v", "t", ss_last_t[0])
    p.op("v", lambda e: e.tensor_copy(ssq[:, :], pkv[0:4, 512:1024]), inc="v")
    p.wait("sy", "v", p.val("v"))
    ssd_s.dma("sy", lambda e: e.dma_start(out=ss_d[:, :], in_=ssq[:, :]))
    ssd_s.wait_ready("sy")
    invr_s.dma("sy", lambda e: e.dma_start(
        out=inv_raw[:, :], in_=bass.AP(ss_d.tensor, 0, [[1, 128], [128, NTKC]]),
        allow_slow_non_contiguous=True))
    invr_s.wait_ready("v")
    p.op("v", lambda e: e.tensor_scalar(inv_raw[:, :], inv_raw[:, :], 1.0 / HID, EPS_RMS, ALU.mult, ALU.add))
    vbubble()
    p.op("v", lambda e: e.reciprocal(inv_raw[:, :], inv_raw[:, :]), inc="v")
    p.wait("c", "v", p.val("v"))
    p.op("c", lambda e: e.activation(inv_tm[:, :], inv_raw[:, :], AF.Sqrt), inc="c")
    INV_C = p.val("c")
    if "dbg_inv" in dbg_aps:
        p.wait("sy", "c", INV_C)
        dbg_s.dma("sy", lambda e: e.dma_start(out=dbg_aps["dbg_inv"][:, :], in_=inv_tm[:, :]))

    sb.pop()

    sb.push()
    wpT = [sb.alloc(f"wpT{i}", [128, 64 * 256], BF16) for i in range(2)]
    hpiece = [sb.alloc(f"hpiece{i}", [128, 64 * 256], BF16) for i in range(2)]
    pchunk = [sb.alloc(f"pchunk{i}", [128, 256], F32) for i in range(2)]
    stg1 = [sb.alloc(f"stg1{i}", [128, 256], F32) for i in range(2)]
    ostg = [sb.alloc(f"ostg{i}", [128, 256], F32) for i in range(2)]

    wpTl_s = [Slot(p, dma=True), Slot(p, dma=True)]
    hp_s = [Slot(p, dma=True), Slot(p, dma=True)]
    pch_s = [Slot(p, dma=True), Slot(p, dma=True)]
    ost_s = [Slot(p, dma=True), Slot(p, dma=True)]
    wpT_s = [Slot(p), Slot(p)]
    stg1_s = [Slot(p), Slot(p)]

    D_T_END = p.val("t")
    D_V_END = p.val("v")
    p.wait("g", "t", D_T_END)
    p.wait("g", "v", D_V_END)
    p.wait("c", "t", D_T_END)
    p.wait("c", "v", D_V_END)
    p.wait("sy", "t", D_T_END)
    p.wait("sy", "v", D_V_END)
    phase_barrier([gu_nat_s[0], gu_nat_s[1], hst_s[0], hst_s[1], hst_s[2],
                   hst_s[3], xno_s, ssd_s, invr_s])
    for s_, vtot in zip(hst_s, HST_ALL):
        p.wait("c", s_.sem, vtot)
    p.wait("sy", pstg_s.sem, PSTG_ALL)

    p.wait("sy", agq_s.sem, AGW_DONE)
    WPT_READY = {}

    def emit_wpT_load(dt_):
        dslot = dt_ % 2
        wpT_s[dslot].acquire("sy")
        wpTl_s[dslot].acquire("sy")
        wpTl_s[dslot].dma("sy", lambda e, dt_=dt_, dslot=dslot: e.dma_start(
            out=wpT[dslot][:, :].rearrange("p (c j) -> p c j", j=256),
            in_=bass.AP(ap_out.tensor, (dt_ * HID) * (E // 8),
                        [[E // 8, 128], [128 * (E // 8), 64], [1, 256]])))
        WPT_READY[dt_] = wpTl_s[dslot].ready()

    hp_loaded = [-1, -1]

    def emit_hp_load(pi):
        hsl = pi % 2
        if hp_loaded[hsl] == pi:
            return
        hp_loaded[hsl] = pi
        hp_s[hsl].acquire("c")
        hp_s[hsl].dma("c", lambda e, pi=pi, hsl=hsl: e.dma_start(
            out=bass.AP(hpiece[hsl], 0, [[hpiece[hsl].shape[1], 128], [256, 64], [1, 256]]),
            in_=bass.AP(hsp.tensor, pi * 256, [[TOK, 128], [128 * TOK, 64], [1, 256]])))

    def emit_proj_pi(dt_, k):
        order = list(range(8)) if dt_ % 2 == 0 else list(range(7, -1, -1))
        pi = order[k]
        dslot = dt_ % 2
        if k == 0:
            emit_hp_load(pi)
            p.wait("t", wpTl_s[dslot].sem, WPT_READY[dt_])
        if k < 7:
            emit_hp_load(order[k + 1])
        hsl = pi % 2
        for sub in range(2):
            tkc = pi * 2 + sub
            si = tkc % 4
            pmm_s[si].acquire("t")
            hp_s[hsl].wait_ready("t")
            for hc in range(64):
                p.op("t", lambda e, hc=hc, sub=sub, si=si, hsl=hsl, dslot=dslot: e.matmul(
                    pmm[si][:, 0:256],
                    lhsT=hpiece[hsl][:, hc * 256 + sub * 128: hc * 256 + (sub + 1) * 128],
                    rhs=wpT[dslot][:, hc * 256:(hc + 1) * 256],
                    start=(hc == 0), stop=(hc == 63)),
                    inc=("t" if hc == 63 else None))
            T_PE = p.val("t")
            pch_s[sub].acquire("sy")
            pch_s[sub].dma("sy", lambda e, tkc=tkc, dt_=dt_, sub=sub: e.dma_start(
                out=pchunk[sub][:, :],
                in_=part_d[tkc * 128:(tkc + 1) * 128, dt_ * 256:(dt_ + 1) * 256]))
            p.wait("c", "t", T_PE)
            stg1_s[sub].acquire("c")
            p.op("c", lambda e, si=si, tkc=tkc, sub=sub: e.activation(
                stg1[sub][:, :], pmm[si][:, 0:256], AF.Identity,
                bias=0.0, scale=inv_tm[:, tkc:tkc + 1]), inc="c")
            STG1_C = p.val("c")
            pmm_s[si].released("c")
            p.wait("v", "c", STG1_C)
            pch_s[sub].wait_ready("v")
            ost_s[sub].acquire("v")
            p.op("v", lambda e, sub=sub: e.tensor_tensor(
                ostg[sub][:, :], stg1[sub][:, :], pchunk[sub][:, :], ALU.add), inc="v")
            stg1_s[sub].released("v")
            pch_s[sub].released("v")
            p.wait("sy", "v", p.val("v"))
            ost_s[sub].dma("sy", lambda e, tkc=tkc, dt_=dt_, sub=sub: e.dma_start(
                out=out_d[tkc * 128:(tkc + 1) * 128, dt_ * 256:(dt_ + 1) * 256],
                in_=ostg[sub][:, :]))
            ost_s[sub].released_dma()
        hp_s[hsl].released("t")
        if k == 7:
            wpT_s[dslot].released("t")
            wpTl_s[dslot].released("t")

    for dt_ in range(9):
        if dt_ < 8:
            emit_wpT_load(dt_)
        if dt_ >= 1:
            for k in range(8):
                emit_proj_pi(dt_ - 1, k)

    for s_ in ost_s:
        s_.wait_ready("sy")
    if dbg_s is not None:
        dbg_s.wait_ready("sy")
    sb.pop()

    p.finalize()
    return nc


_NC_CACHE = {}


def _get_nc(dbg=()):
    key = tuple(sorted(dbg))
    if key not in _NC_CACHE:
        _NC_CACHE[key] = build_program(dbg)
    return _NC_CACHE[key]


def make_in_maps(inputs):
    x = np.ascontiguousarray(np.asarray(inputs["L0_attn_x"], dtype=np.float32))
    ws = {k: np.ascontiguousarray(np.asarray(inputs[k], dtype=np.float32))
          for k in ("L0_attn_Wq", "L0_attn_Wk", "L0_attn_Wv", "L0_attn_Wo",
                    "L0_ffn_w_gate_up", "L0_ffn_w_proj")}
    bs = {k: np.ascontiguousarray(np.asarray(inputs[k], dtype=np.float32))
          for k in ("L0_attn_bq", "L0_attn_bk", "L0_attn_bv", "L0_attn_bo")}
    in_maps = []
    EH = E // 2
    for c in range(8):
        b, half = c // 2, c % 2
        own = x[b, half * TOK:(half + 1) * TOK, :]
        other = x[b, (1 - half) * TOK:(2 - half) * TOK, :]
        x_core = np.concatenate([own, other], axis=0)
        in_maps.append({
            "x_all": x_core,
            "Wq": ws["L0_attn_Wq"],
            "Wk_sh": np.ascontiguousarray(ws["L0_attn_Wk"][half * EH:(half + 1) * EH]),
            "Wv_sh": np.ascontiguousarray(ws["L0_attn_Wv"][half * EH:(half + 1) * EH]),
            "Wo": ws["L0_attn_Wo"],
            "bq": bs["L0_attn_bq"].reshape(E, 1),
            "bk": np.ascontiguousarray(bs["L0_attn_bk"].reshape(1, E)[:, half * EH:(half + 1) * EH]),
            "bv": np.ascontiguousarray(bs["L0_attn_bv"].reshape(1, E)[:, half * EH:(half + 1) * EH]),
            "bo": bs["L0_attn_bo"].reshape(1, E),
            "wgu_sh": np.ascontiguousarray(ws["L0_ffn_w_gate_up"][c * (GUP // 8):(c + 1) * (GUP // 8)]),
            "wp_sh": np.ascontiguousarray(ws["L0_ffn_w_proj"][c * (E // 8):(c + 1) * (E // 8)]),
        })
    return in_maps


def kernel(**inputs):
    nc = _get_nc()
    in_maps = make_in_maps(inputs)
    res = run_bass_kernel_spmd(nc, in_maps, core_ids=list(range(8)))
    out = np.empty((B, T, E), dtype=np.float32)
    for c in range(8):
        b, half = c // 2, c % 2
        out[b, half * TOK:(half + 1) * TOK, :] = np.asarray(res.results[c]["out"])
    return out


# version 15
# speedup vs baseline: 1.2425x; 1.0148x over previous
import math
import numpy as np
import ml_dtypes
import concourse.bass as bass
import concourse.mybir as mybir
from concourse.bass_utils import run_bass_kernel_spmd

F32 = mybir.dt.float32
BF16 = mybir.dt.bfloat16
AF = mybir.ActivationFunctionType
ALU = mybir.AluOpType

B, T, E, H = 4, 4096, 2048, 16
HID, GUP = 8192, 16384
TOK = 2048
TALL = 4096
DC = E // 128
NTILE = TOK // 512
NTKC = TOK // 128
EPS_LN = 1e-5
EPS_RMS = 1.1920929e-7
ATT_SCALE = 1.0 / math.sqrt(128.0)
STW = 256

ENGS = ("t", "v", "c", "sy", "g")


class Prog:
    def __init__(self, nc):
        self.nc = nc
        self.q = {e: [] for e in ENGS}
        self.sems = {}
        self.cnt = {}
        self._guards = []

    def mksem(self, name):
        g = self.nc.semaphore(name)
        self.sems[name] = g.__enter__()
        self._guards.append(g)
        self.cnt[name] = 0

    def val(self, s):
        return self.cnt[s]

    def wait(self, eng, s, v):
        if v <= 0:
            return
        sem = self.sems[s]
        self.q[eng].append(lambda e, sem=sem, v=v: e.wait_ge(sem, v))

    def op(self, eng, build, inc=None, inc_n=1):
        if inc is None and eng in ("v", "c"):
            inc = eng
        if inc is not None:
            self.cnt[inc] += inc_n
            sem = self.sems[inc]

            def f(e, build=build, sem=sem, n=inc_n):
                build(e).then_inc(sem, n)
        else:
            def f(e, build=build):
                build(e)
        self.q[eng].append(f)

    def finalize(self):
        with self.nc.Block() as blk:
            for eng, meth in (("t", blk.tensor), ("v", blk.vector),
                              ("c", blk.scalar), ("sy", blk.sync), ("g", blk.gpsimd)):
                fns = self.q[eng]

                def body(e, fns=fns):
                    for f in fns:
                        f(e)
                meth(body)


class Slot:

    _n = [0]

    def __init__(self, p, dma=False):
        self.p = p
        self.rel = []
        self.sem = None
        if dma:
            Slot._n[0] += 1
            self.sem = f"sl{Slot._n[0]}"
            p.mksem(self.sem)

    def acquire(self, eng):
        for s, v in self.rel:
            self.p.wait(eng, s, v)
        self.rel = []

    def released(self, sem):
        self.rel.append((sem, self.p.val(sem)))

    def released_dma(self):
        self.rel.append((self.sem, self.p.val(self.sem)))

    def dma(self, eng, build):
        self.p.op(eng, build, inc=self.sem, inc_n=16)

    def ready(self):
        return self.p.val(self.sem)

    def wait_ready(self, eng):
        self.p.wait(eng, self.sem, self.p.val(self.sem))


class SbAlloc:
    def __init__(self, nc, base, limit):
        self.nc = nc
        self.off = base
        self.limit = limit
        self.marks = []

    def alloc(self, name, shape, dt):
        sz = int(np.prod(shape[1:])) * mybir.dt.size(dt)
        off = (self.off + 31) // 32 * 32
        assert off + sz <= self.limit, f"SBUF overflow {name}: {off}+{sz}>{self.limit}"
        h = self.nc.alloc_sbuf_tensor_at(name, list(shape), dt, offset=off)
        self.off = off + sz
        return h

    def push(self):
        self.marks.append(self.off)

    def pop(self):
        self.off = self.marks.pop()


def build_program(dbg=()):
    nc = bass.Bass("TRN2", target_bir_lowering=False)

    x_all = nc.dram_tensor("x_all", [TALL, E], F32, kind="ExternalInput").ap()
    Wq = nc.dram_tensor("Wq", [E, E], F32, kind="ExternalInput").ap()
    Wk_sh = nc.dram_tensor("Wk_sh", [E // 2, E], F32, kind="ExternalInput").ap()
    Wv_sh = nc.dram_tensor("Wv_sh", [E // 2, E], F32, kind="ExternalInput").ap()
    Wo = nc.dram_tensor("Wo", [E, E], F32, kind="ExternalInput").ap()
    bq_d = nc.dram_tensor("bq", [E, 1], F32, kind="ExternalInput").ap()
    bk_d = nc.dram_tensor("bk", [1, E // 2], F32, kind="ExternalInput").ap()
    bv_d = nc.dram_tensor("bv", [1, E // 2], F32, kind="ExternalInput").ap()
    bo_d = nc.dram_tensor("bo", [1, E], F32, kind="ExternalInput").ap()
    wgu_sh = nc.dram_tensor("wgu_sh", [GUP // 8, E], F32, kind="ExternalInput").ap()
    wp_sh = nc.dram_tensor("wp_sh", [E // 8, HID], F32, kind="ExternalInput").ap()
    out_d = nc.dram_tensor("out", [TOK, E], F32, kind="ExternalOutput").ap()

    agu_in = nc.dram_tensor("agu_in", [E, GUP // 8], BF16, kind="Internal").ap()
    agu_out = nc.dram_tensor("agu_out", [8 * E, GUP // 8], BF16, kind="Internal").ap()
    ap_in = nc.dram_tensor("ap_in", [HID, E // 8], BF16, kind="Internal").ap()
    ap_out = nc.dram_tensor("ap_out", [8 * HID, E // 8], BF16, kind="Internal").ap()
    akv_in = nc.dram_tensor("akv_in", [128, E // 2], F32, kind="Internal").ap()
    akv_out = nc.dram_tensor("akv_out", [256, E // 2], F32, kind="Internal").ap()

    wq_bf = nc.dram_tensor("wq_bf", [E, E], BF16, kind="Internal").ap()
    wo_bf = nc.dram_tensor("wo_bf", [E, E], BF16, kind="Internal").ap()
    xn_sp = nc.dram_tensor("xn_sp", [E, TOK], BF16, kind="Internal").ap()
    part_d = nc.dram_tensor("part_d", [TOK, E], F32, kind="Internal").ap()
    hsp = nc.dram_tensor("hsp", [HID, TOK], BF16, kind="Internal").ap()
    ss_d = nc.dram_tensor("ss_d", [4, 512], F32, kind="Internal").ap()

    dbg_specs = {
        "dbg_xn": ([E, TOK], BF16),
        "dbg_kv": ([128, E], F32),
        "dbg_q": ([128, DC * E], BF16),
        "dbg_attn": ([128, DC * E], BF16),
        "dbg_part": ([TOK, E], F32),
        "dbg_h": ([HID, TOK], BF16),
        "dbg_wqg": ([128, E], BF16),
        "dbg_inv": ([128, NTKC], F32),
    }
    dbg_aps = {}
    for name in dbg:
        shape, dt = dbg_specs[name]
        dbg_aps[name] = nc.dram_tensor(name, shape, dt, kind="ExternalOutput").ap()

    ident = nc.inline_tensor(np.eye(128, dtype=ml_dtypes.bfloat16), name="identc").ap()
    identf = nc.inline_tensor(np.eye(128, dtype=np.float32), name="identfc").ap()
    sel4_np = np.zeros((4, 128, 4), dtype=ml_dtypes.bfloat16)
    for i in range(4):
        sel4_np[i, :, i] = 1
    sel4 = [nc.inline_tensor(sel4_np[i], name=f"sel4_{i}").ap() for i in range(4)]
    ok_np = np.zeros((128, 128), dtype=ml_dtypes.bfloat16)
    ok_np[0, :] = 1
    ok_np[2, :] = 1
    ov_np = np.zeros((128, 128), dtype=ml_dtypes.bfloat16)
    ov_np[1, :] = 1
    ov_np[3, :] = 1
    ones_k = nc.inline_tensor(ok_np, name="oneskc").ap()
    ones_v = nc.inline_tensor(ov_np, name="onesvc").ap()

    p = Prog(nc)
    for s in ("t", "v", "c"):
        p.mksem(s)

    KB = 1024
    sb = SbAlloc(nc, base=18 * KB, limit=206 * KB)

    ident_sb = sb.alloc("ident_sb", [128, 128], BF16)
    identf_sb = sb.alloc("identf_sb", [128, 128], F32)
    sel4_sb = sb.alloc("sel4_sb", [128, 16], BF16)
    ones_k_sb = sb.alloc("ones_k_sb", [128, 128], BF16)
    ones_v_sb = sb.alloc("ones_v_sb", [128, 128], BF16)
    bq_sb = sb.alloc("bq_sb", [128, DC], F32)
    kv_acc = sb.alloc("kv_acc", [128, E // 2], F32)
    kvb = sb.alloc("kvb", [128, E], BF16)
    ssq = sb.alloc("ssq", [4, 512], F32)
    inv_raw = sb.alloc("inv_raw", [128, NTKC], F32)
    inv_tm = sb.alloc("inv_tm", [128, NTKC], F32)
    vbub = sb.alloc("vbub", [128, 8], F32)

    ps_guards = []

    def psum(name, shape, dt):
        g = nc.psum_tensor(name, shape, dt)
        h = g.__enter__()
        ps_guards.append(g)
        return h

    pmm = [psum(f"pmm{i}", [128, 512], F32) for i in range(4)]
    pxp = [psum(f"pxp{i}", [128, 1024], BF16) for i in range(2)]
    pkv = psum("pkv", [128, 1024], F32)

    pmm_s = [Slot(p) for _ in range(4)]
    pxp_s = [Slot(p) for _ in range(2)]
    pkvh_s = [Slot(p), Slot(p)]

    def vbubble():
        p.wait("v", "v", p.val("v"))

    def phase_barrier(slots, engines=("v", "c")):
        for eng in engines:
            for s_ in slots:
                if s_ is not None and s_.sem is not None:
                    p.wait(eng, s_.sem, p.val(s_.sem))

    cst = Slot(p, dma=True)
    cst.dma("sy", lambda e: e.dma_start(out=ident_sb[:, :], in_=ident[:, :]))
    cst.dma("sy", lambda e: e.dma_start(out=identf_sb[:, :], in_=identf[:, :]))
    for i in range(4):
        cst.dma("sy", lambda e, i=i: e.dma_start(out=sel4_sb[:, 4 * i:4 * (i + 1)], in_=sel4[i][:, :]))
    cst.dma("sy", lambda e: e.dma_start(out=ones_k_sb[:, :], in_=ones_k[:, :]))
    cst.dma("sy", lambda e: e.dma_start(out=ones_v_sb[:, :], in_=ones_v[:, :]))
    cst.dma("sy", lambda e: e.dma_start(
        out=bq_sb[:, :], in_=bass.AP(bq_d.tensor, 0, [[1, 128], [128, DC]]),
        allow_slow_non_contiguous=True))
    CST_ALL = cst.ready()

    wqbf_s = Slot(p, dma=True)
    wobf_s = Slot(p, dma=True)
    wqbf_s.dma("g", lambda e: e.dma_start(out=wq_bf[:, :], in_=Wq[:, :]))
    wobf_s.dma("g", lambda e: e.dma_start(out=wo_bf[:, :], in_=Wo[:, :]))

    xp_k = [0]

    def transpose_tiles(src_sb, n_ic, dst_sb, dst_off, dst_chunk_stride, f32=False):
        gsz = 4 if f32 else 8
        assert n_ic % gsz == 0
        idn = identf_sb if f32 else ident_sb
        for half in range(n_ic // gsz):
            k = xp_k[0]
            slot = pxp_s[k % 2]
            ps = pxp[k % 2][:, :].bitcast(F32) if f32 else pxp[k % 2][:, :]
            slot.acquire("t")
            for i in range(gsz):
                ic = half * gsz + i
                p.op("t", lambda e, ic=ic, ps=ps, i=i, src_sb=src_sb, idn=idn: e.transpose(
                    ps[:, i * 128:(i + 1) * 128],
                    src_sb[:, ic * 128:(ic + 1) * 128],
                    idn[:, :]), inc=("t" if i == gsz - 1 else None))
            p.wait("c", "t", p.val("t"))
            dst = bass.AP(dst_sb, dst_off + (half * gsz) * dst_chunk_stride,
                          [[dst_sb.shape[1], 128], [dst_chunk_stride, gsz], [1, 128]])
            p.op("c", lambda e, ps=ps, dst=dst, gsz=gsz: e.copy(
                dst, ps[:, 0:gsz * 128]), inc="c")
            slot.released("c")
            xp_k[0] += 1

    def transpose_weight_g(w_ap, wt_sb, O, I, nat_bufs, nat_slots):
        nb = len(nat_bufs)
        for rc in range(O // 128):
            slot = rc % nb
            nat_slots[slot].acquire("g")
            nat_slots[slot].dma("g", lambda e, rc=rc, slot=slot, w_ap=w_ap: e.dma_start(
                out=nat_bufs[slot][:, :], in_=w_ap[rc * 128:(rc + 1) * 128, :]))
            nat_slots[slot].wait_ready("t")
            transpose_tiles(nat_bufs[slot], I // 128, wt_sb, rc * 128, O, f32=False)
            nat_slots[slot].released("t")

    def transpose_weight(w_ap, wt_sb, O, I, nat_bufs, nat_slots, f32=True):
        nb = len(nat_bufs)
        for rc in range(O // 128):
            slot = rc % nb
            nat_slots[slot].acquire("sy")
            nat_slots[slot].dma("sy", lambda e, rc=rc, slot=slot, w_ap=w_ap: e.dma_start(
                out=nat_bufs[slot][:, :], in_=w_ap[rc * 128:(rc + 1) * 128, :]))
            nat_slots[slot].wait_ready("t")
            transpose_tiles(nat_bufs[slot], I // 128, wt_sb, rc * 128, O, f32=f32)
            nat_slots[slot].released("t")

    sb.push()
    qnat = sb.alloc("qnat", [128, HID], BF16)
    q_ub = sb.alloc("q_ub", [128, 2048], BF16)
    q_col = sb.alloc("q_col", [128, 32], F32)
    q_e1 = sb.alloc("q_e1", [128, 512], F32)
    q_e2 = sb.alloc("q_e2", [128, 1536], F32)
    q_w1, q_w2 = q_e2[:, 0:1024], q_e2[:, 1024:1536]
    q_sg = sb.alloc("q_sg", [128, 2048], BF16)
    q_sexpb = sb.alloc("q_sexpb", [128, 2048], BF16)
    q_prb = sb.alloc("q_prb", [128, 2048], BF16)
    q_wq = [sb.alloc(f"q_wq{i}", [128, 2048], BF16) for i in range(2)]
    q_wT = [sb.alloc(f"q_wT{i}", [128, 2048], BF16) for i in range(2)]

    qnat_s = Slot(p, dma=True)
    qst_s = [Slot(p, dma=True), Slot(p, dma=True)]
    qeq_s = Slot(p)
    qsgn_s = Slot(p)
    qwq_s = [Slot(p), Slot(p)]
    qwT_s = [Slot(p), Slot(p)]
    q_pb = [0]

    def quant_piece_q(off):
        width = 2048
        G = width // 64
        qeq_s.acquire("v")
        qnat_s.wait_ready("v")
        srcw = qnat[:, off:off + width]
        p.op("v", lambda e, srcw=srcw: e.scalar_tensor_tensor(q_ub[:, :], srcw, -1.0, srcw, ALU.mult, ALU.max))
        u3 = q_ub[:, :].rearrange("p (g k) -> p g k", k=64)
        w13 = q_w1.rearrange("p (g k) -> p g k", k=32)
        w23 = q_w2.rearrange("p (g k) -> p g k", k=16)
        vbubble()
        p.op("v", lambda e: e.tensor_tensor(w13, u3[:, :, 0:32], u3[:, :, 32:64], ALU.add))
        vbubble()
        p.op("v", lambda e: e.tensor_tensor(w23, w13[:, :, 0:16], w13[:, :, 16:32], ALU.add))
        vbubble()
        p.op("v", lambda e: e.tensor_tensor(w13[:, :, 0:8], w23[:, :, 0:8], w23[:, :, 8:16], ALU.add))
        vbubble()
        p.op("v", lambda e: e.tensor_tensor(w23[:, :, 0:4], w13[:, :, 0:4], w13[:, :, 4:8], ALU.add))
        vbubble()
        p.op("v", lambda e: e.tensor_tensor(w13[:, :, 0:2], w23[:, :, 0:2], w23[:, :, 2:4], ALU.add))
        vbubble()
        p.op("v", lambda e: e.tensor_tensor(q_col[:, :], w13[:, :, 0:1].rearrange("p g k -> p (g k)"),
                                            w13[:, :, 1:2].rearrange("p g k -> p (g k)"), ALU.add))
        vbubble()
        p.op("v", lambda e: e.tensor_scalar(q_e1[:, 0:G], q_col[:, :], 1.0 / 64, 1e-8, ALU.mult, ALU.max))
        src_, dst_ = q_e1, q_e2
        for m in range(5):
            bs = 1 << m
            vbubble()
            s_in = src_[:, 0:G * bs].rearrange("p (g k) -> p g k", k=bs)
            d3 = dst_[:, 0:G * 2 * bs].rearrange("p (g k) -> p g k", k=2 * bs)
            p.op("v", lambda e, d3=d3, s_in=s_in, bs=bs: e.tensor_copy(d3[:, :, 0:bs], s_in))
            p.op("v", lambda e, d3=d3, s_in=s_in, bs=bs: e.tensor_copy(d3[:, :, bs:2 * bs], s_in))
            src_, dst_ = dst_, src_
        bs = 32
        vbubble()
        s_in = src_[:, 0:G * bs].rearrange("p (g k) -> p g k", k=bs)
        d3 = q_sexpb[:, :].rearrange("p (g k) -> p g k", k=2 * bs)
        p.op("v", lambda e, d3=d3, s_in=s_in: e.tensor_copy(d3[:, :, 0:bs], s_in))
        p.op("v", lambda e, d3=d3, s_in=s_in: e.tensor_copy(d3[:, :, bs:2 * bs], s_in))
        vbubble()
        p.op("v", lambda e: e.scalar_tensor_tensor(
            q_ub[:, :], q_ub[:, :], 2.0, q_sexpb[:, :], ALU.mult, ALU.is_gt))
        vbubble()
        p.op("v", lambda e: e.tensor_tensor(q_prb[:, :], q_ub[:, :], q_sexpb[:, :], ALU.mult))
        qsgn_s.acquire("c")
        qnat_s.wait_ready("c")
        p.op("c", lambda e, srcw=srcw: e.activation(q_sg[:, :], srcw, AF.Sign), inc="c")
        c_done = p.val("c")
        p.wait("v", "c", c_done)
        vbubble()
        pb = q_pb[0]
        qwq_s[pb].acquire("v")
        p.op("v", lambda e, pb=pb: e.tensor_tensor(q_wq[pb][:, :], q_prb[:, :], q_sg[:, :], ALU.mult), inc="v")
        qsgn_s.released("v")
        return p.val("v")

    tdefer = []

    def quant_xpose_store(out_dram_ap_fn):
        vv = p.val("v")
        pb = q_pb[0]
        q_pb[0] ^= 1

        def fn(vv=vv, pb=pb, out_dram_ap_fn=out_dram_ap_fn):
            p.wait("t", "v", vv)
            qwT_s[pb].acquire("c")
            transpose_tiles(q_wq[pb], 16, q_wT[pb], 0, 128)
            qwq_s[pb].released("t")
            p.wait("sy", "c", p.val("c"))
            qst_s[pb].acquire("sy")
            qst_s[pb].dma("sy", lambda e, pb=pb: out_dram_ap_fn(e, pb))
            qwT_s[pb].released(qst_s[pb].sem)
        tdefer.append(fn)

    def make_q_emitters():
        ems = []
        for ch in range(GUP // 8 // 128):
            def em(ch=ch):
                qnat_s.acquire("g")
                qnat_s.dma("g", lambda e, ch=ch: e.dma_start(
                    out=qnat[:, 0:E], in_=wgu_sh[ch * 128:(ch + 1) * 128, :]))
                quant_piece_q(0)
                qnat_s.released("v")
                qnat_s.released("c")
                qeq_s.released("v")
                dst = bass.AP(agu_in.tensor, (ch // 4) * (E * 512) + (ch % 4) * 128,
                              [[512, 128], [128 * 512, 16], [1, 128]])
                quant_xpose_store(lambda e, pb, dst=dst: e.dma_start(
                    out=dst, in_=q_wT[pb][:, :].rearrange("p (c j) -> p c j", j=128)))
            ems.append(em)
        for ch in range(E // 8 // 128):
            for piece in range(4):
                def em(ch=ch, piece=piece):
                    wpf_s.acquire("sy")
                    wpf_s.dma("sy", lambda e, ch=ch, piece=piece: e.dma_start(
                        out=wpf[0][:, :],
                        in_=wp_sh[ch * 128:(ch + 1) * 128, piece * 2048:(piece + 1) * 2048]))
                    qnat_s.acquire("c")
                    wpf_s.wait_ready("c")
                    p.op("c", lambda e: e.activation(qnat[:, 0:2048], wpf[0][:, :], AF.Identity), inc="c")
                    p.wait("c", "c", p.val("c"))
                    wpf_s.released("c")
                    p.wait("v", "c", p.val("c"))
                    quant_piece_q(0)
                    qnat_s.released("v")
                    qnat_s.released("c")
                    qeq_s.released("v")
                    dst = bass.AP(ap_in.tensor, (piece * 16 * 128) * (E // 8) + ch * 128,
                                  [[E // 8, 128], [128 * (E // 8), 16], [1, 128]])
                    quant_xpose_store(lambda e, pb, dst=dst: e.dma_start(
                        out=dst, in_=q_wT[pb][:, :].rearrange("p (c j) -> p c j", j=128)))
                ems.append(em)
        return ems

    q_emitters = make_q_emitters()
    agq_s = Slot(p, dma=True)
    rg8 = [[0, 1, 2, 3, 4, 5, 6, 7]]

    def emit_gu_slice_trigger(s):
        p.wait("g", qst_s[0].sem, qst_s[0].ready())
        p.wait("g", qst_s[1].sem, qst_s[1].ready())
        ins = bass.AP(agu_in.tensor, s * (E * 512), [[512, E], [1, 512]])
        outs = bass.AP(agu_out.tensor, s * (8 * E * 512), [[512, 8 * E], [1, 512]])
        p.op("g", lambda e, ins=ins, outs=outs: e.collective_compute(
            "AllGather", ALU.bypass, ins=[ins], outs=[outs],
            replica_groups=rg8), inc=agq_s.sem, inc_n=1)

    def emit_p_trigger(h):
        p.wait("g", qst_s[0].sem, qst_s[0].ready())
        p.wait("g", qst_s[1].sem, qst_s[1].ready())
        ins = bass.AP(ap_in.tensor, h * (HID // 2) * (E // 8),
                      [[E // 8, HID // 2], [1, E // 8]])
        outs = bass.AP(ap_out.tensor, h * 8 * (HID // 2) * (E // 8),
                       [[E // 8, 8 * (HID // 2)], [1, E // 8]])
        p.op("g", lambda e, ins=ins, outs=outs: e.collective_compute(
            "AllGather", ALU.bypass, ins=[ins], outs=[outs],
            replica_groups=rg8), inc=agq_s.sem, inc_n=1)

    sb.push()
    wkT = sb.alloc("wkT", [128, DC * (E // 2)], BF16)
    wvT = sb.alloc("wvT", [128, DC * (E // 2)], BF16)
    xb_off = sb.off
    xbuf = [sb.alloc(f"xbuf{i}", [128, E], F32) for i in range(2)]
    xb_end = sb.off
    sb.off = xb_off
    kvf = sb.alloc("kvf", [128, E], F32)
    sb.off = xb_end
    xn_tm = [sb.alloc(f"xn_tm{i}", [128, E], BF16) for i in range(2)]
    xn_st = sb.alloc("xn_st", [128, DC * STW], BF16)
    kbuf = sb.alloc("kbuf", [128, E // 2], BF16)
    vbuf = sb.alloc("vbuf", [128, E // 2], BF16)
    brow = sb.alloc("brow", [128, E // 2], BF16)
    blo = sb.alloc("blo", [128, E // 2], BF16)
    stats = sb.alloc("stats", [128, 24], F32)
    aggr = sb.alloc("aggr", [128, 2], F32)
    veps = sb.alloc("veps", [128, 1], F32)
    vrec = sb.alloc("vrec", [128, 1], F32)
    vinv = sb.alloc("vinv", [128, 1], F32)
    negm = sb.alloc("negm", [128, 1], F32)
    negmb = sb.alloc("negmb", [128, 1], F32)
    fb_off = sb.off
    fbias = sb.alloc("fbias", [128, E // 2], F32)
    natw = [sb.alloc(f"natw{i}", [128, E], BF16) for i in range(2)]
    fb_end = sb.off
    sb.off = fb_off
    wpf = [sb.alloc("wpf", [128, 2048], F32)]
    sb.off = fb_end
    natw_s = [Slot(p, dma=True), Slot(p, dma=True)]
    wpf_s = Slot(p, dma=True)
    xbuf_s = [Slot(p, dma=True), Slot(p, dma=True)]
    bias_s = Slot(p, dma=True)
    xnsp_s = Slot(p, dma=True)
    xn_tm_s = [Slot(p), Slot(p)]
    xn_st_s = Slot(p)
    kvbuf_s = Slot(p)

    phase_barrier([qnat_s, qst_s[0], qst_s[1]], engines=("v", "c", "sy", "t"))
    bias_s.dma("sy", lambda e: e.dma_start(out=fbias[0:1, :], in_=bk_d[:, :]))
    bias_s.dma("sy", lambda e: e.dma_start(out=fbias[1:2, :], in_=bv_d[:, :]))
    BIAS_ALL = bias_s.ready()

    p.wait("t", cst.sem, CST_ALL)
    transpose_weight_g(Wk_sh, wkT, E // 2, E, natw, natw_s)
    transpose_weight_g(Wv_sh, wvT, E // 2, E, natw, natw_s)
    NATW_T = p.val("t")

    p.op("v", lambda e: e.memset(kv_acc[:, :], 0.0))
    p.wait("v", bias_s.sem, BIAS_ALL)
    p.op("v", lambda e: e.memset(brow[:, :], 0.0))
    vbubble()
    p.op("v", lambda e: e.tensor_copy(brow[0:2, :], fbias[0:2, :]))
    vbubble()
    p.op("v", lambda e: e.tensor_tensor(blo[0:2, :], fbias[0:2, :], brow[0:2, :], ALU.subtract), inc="v")
    BIAS_V = p.val("v")
    p.wait("sy", "v", BIAS_V)
    p.wait("sy", "t", NATW_T)
    p.wait("sy", "v", BIAS_V)
    bias_s.dma("sy", lambda e: e.dma_start(out=brow[2:4, :], in_=blo[0:2, :]))
    p.wait("t", "v", BIAS_V)
    bias_s.wait_ready("t")

    apply_cval = {}
    negmb_cval = {}

    def emit_x_load(tcg):
        xsl = tcg % 2
        xbuf_s[xsl].acquire("sy")
        xbuf_s[xsl].dma("sy", lambda e, tcg=tcg, xsl=xsl: e.dma_start(
            out=xbuf[xsl][:, :], in_=x_all[tcg * 128:(tcg + 1) * 128, :]))

    def emit_ln(tcg):
        xsl = tcg % 2
        xbuf_s[xsl].wait_ready("v")
        for i in range(4):
            p.op("v", lambda e, i=i, xsl=xsl: e.bn_stats(
                stats[:, 6 * i:6 * (i + 1)], xbuf[xsl][:, 512 * i:512 * (i + 1)]))
        vbubble()
        p.op("v", lambda e: e.bn_aggr(aggr[:, :], stats[:, :]))
        vbubble()
        p.op("v", lambda e: e.tensor_scalar(veps[:, :], aggr[:, 1:2], 1.0, EPS_LN, ALU.mult, ALU.add))
        if tcg >= 1:
            p.wait("v", "c", negmb_cval[tcg - 1])
        vbubble()
        p.op("v", lambda e: e.tensor_scalar(negm[:, :], aggr[:, 0:1], -1.0, 0.0, ALU.mult, ALU.add))
        vbubble()
        p.op("v", lambda e: e.reciprocal(vrec[:, :], veps[:, :]), inc="v")
        p.wait("c", "v", p.val("v"))
        p.op("c", lambda e: e.activation(vinv[:, :], vrec[:, :], AF.Sqrt))
        p.wait("c", "c", p.val("c"))
        p.op("c", lambda e: e.activation(negmb[:, :], negm[:, :], AF.Identity,
                                         scale=vinv[:, 0:1]), inc="c")
        negmb_cval[tcg] = p.val("c")
        p.wait("c", "c", p.val("c"))
        xn_tm_s[xsl].acquire("c")
        xbuf_s[xsl].wait_ready("c")
        p.op("c", lambda e, xsl=xsl: e.activation(
            xn_tm[xsl][:, :], xbuf[xsl][:, :], AF.Identity,
            bias=negmb[:, 0:1], scale=vinv[:, 0:1]), inc="c")
        apply_cval[tcg] = p.val("c")
        xbuf_s[xsl].released("c")
        if tcg + 2 < 32:
            emit_x_load(tcg + 2)

    emit_x_load(0)
    emit_x_load(1)
    emit_ln(0)

    q_iter = iter(q_emitters)
    XN_EVAC_C = 0
    vdefer = []
    for st in range(16):
        xn_st_s.acquire("c")
        for tc in range(2):
            tcg = 2 * st + tc
            p.wait("t", "c", apply_cval[tcg])
            transpose_tiles(xn_tm[tcg % 2], DC, xn_st, tc * 128, STW)
            xn_tm_s[tcg % 2].released("t")
            if tc == 1:
                XN_EVAC_C = p.val("c")
            if tcg + 1 < 32:
                emit_ln(tcg + 1)
            if tc == 0:
                for fn in vdefer:
                    fn()
                vdefer = []
                for fn in tdefer:
                    fn()
                tdefer.clear()
            else:
                em = next(q_iter, None)
                if em is not None:
                    em()
                if st < 8:
                    em = next(q_iter, None)
                    if em is not None:
                        em()

        if st < 8:
            p.wait("sy", "c", XN_EVAC_C)
            xnsp_s.acquire("sy")
            for dcc in range(DC):
                xnsp_s.dma("sy", lambda e, dcc=dcc, st=st: e.dma_start(
                    out=xn_sp[dcc * 128:(dcc + 1) * 128, st * STW:(st + 1) * STW],
                    in_=xn_st[:, dcc * STW:(dcc + 1) * STW]))
            xn_st_s.released(xnsp_s.sem)

        p.wait("t", "c", XN_EVAC_C)
        for tc in range(2):
            if tc == 1:
                for fn in vdefer:
                    fn()
                vdefer = []
            kvbuf_s.acquire("c")
            for ft in range(2):
                for which in ("k", "v"):
                    si = (ft % 2) if which == "k" else (2 + ft % 2)
                    psl = pmm[si]
                    pmm_s[si].acquire("t")
                    wt = wkT if which == "k" else wvT
                    osel = ones_k_sb if which == "k" else ones_v_sb
                    for dcc in range(DC):
                        p.op("t", lambda e, dcc=dcc, tc=tc, ft=ft, psl=psl, wt=wt: e.matmul(
                            psl[:, :],
                            lhsT=xn_st[:, dcc * STW + tc * 128: dcc * STW + (tc + 1) * 128],
                            rhs=wt[:, dcc * (E // 2) + ft * 512: dcc * (E // 2) + (ft + 1) * 512],
                            start=(dcc == 0), stop=False))
                    p.op("t", lambda e, psl=psl, osel=osel, ft=ft: e.matmul(
                        psl[:, :], lhsT=osel[:, :],
                        rhs=brow[:, ft * 512:(ft + 1) * 512],
                        start=False, stop=True), inc="t")
                    p.wait("c", "t", p.val("t"))
                    dstb = kbuf if which == "k" else vbuf
                    p.op("c", lambda e, psl=psl, dstb=dstb, ft=ft: e.activation(
                        dstb[:, ft * 512:(ft + 1) * 512], psl[:, :], AF.Identity), inc="c")
                    pmm_s[si].released("c")
            p.wait("t", "c", p.val("c"))
            for hq in range(2):
                pkvh_s[hq].acquire("t")
                for i in range(4):
                    h = hq * 4 + i
                    p.op("t", lambda e, h=h, i=i, hq=hq: e.matmul(
                        pkv[:, hq * 512 + i * 128: hq * 512 + (i + 1) * 128],
                        lhsT=kbuf[:, h * 128:(h + 1) * 128],
                        rhs=vbuf[:, h * 128:(h + 1) * 128],
                        start=True, stop=True), inc=("t" if i == 3 else None))
                tval = p.val("t")

                def kv_add(hq=hq, tval=tval):
                    p.wait("v", "t", tval)
                    vbubble()
                    p.op("v", lambda e, hq=hq: e.tensor_tensor(
                        kv_acc[:, hq * 512:(hq + 1) * 512],
                        kv_acc[:, hq * 512:(hq + 1) * 512],
                        pkv[:, hq * 512:(hq + 1) * 512], ALU.add), inc="v")
                    pkvh_s[hq].released("v")
                vdefer.append(kv_add)
            kvbuf_s.released("t")
        xn_st_s.released("t")

    for fn in vdefer:
        fn()
    vdefer = []
    for em in q_iter:
        em()
    for fn in tdefer:
        fn()
    tdefer.clear()

    AB_V = p.val("v")
    AB_T = p.val("t")
    XNSP_ALL = xnsp_s.ready()

    dbg_s = Slot(p, dma=True) if dbg_aps else None
    if "dbg_xn" in dbg_aps:
        p.wait("sy", xnsp_s.sem, XNSP_ALL)
        dbg_s.dma("sy", lambda e: e.dma_start(out=dbg_aps["dbg_xn"][:, :], in_=xn_sp[:, :]))

    sb.pop()
    sb.pop()

    akvw_s = Slot(p, dma=True)
    akvr_s = Slot(p, dma=True)
    agkv_s = Slot(p, dma=True)
    p.wait("g", "v", AB_V)
    akvw_s.dma("g", lambda e: e.dma_start(out=akv_in[:, :], in_=kv_acc[:, :]))
    p.wait("g", akvw_s.sem, akvw_s.ready())
    rg2 = [[0, 1], [2, 3], [4, 5], [6, 7]]
    p.op("g", lambda e: e.collective_compute(
        "AllGather", ALU.bypass, ins=[akv_in[:, :]], outs=[akv_out[:, :]],
        replica_groups=rg2), inc=agkv_s.sem, inc_n=1)
    p.wait("g", agkv_s.sem, p.val(agkv_s.sem))
    for r in range(2):
        akvr_s.dma("g", lambda e, r=r: e.dma_start(
            out=kvf[:, r * 1024:(r + 1) * 1024],
            in_=akv_out[r * 128:(r + 1) * 128, :]))
    akvr_s.wait_ready("v")
    if "dbg_kv" in dbg_aps:
        p.wait("sy", akvr_s.sem, akvr_s.ready())
        dbg_s.dma("sy", lambda e: e.dma_start(out=dbg_aps["dbg_kv"][:, :], in_=kvf[:, :]))
    p.op("v", lambda e: e.tensor_scalar(kvb[:, :], kvf[:, :], ATT_SCALE, 0.0, ALU.mult, ALU.add), inc="v")
    KVB_V = p.val("v")

    sb.push()
    R1_off = sb.off
    wqT = sb.alloc("wqT", [128, DC * E], BF16)
    R2_off = sb.off
    q_own = sb.alloc("q_own", [128, DC * E], BF16)
    xnt = [sb.alloc(f"xnt{i}", [128, DC * 512], BF16) for i in range(2)]
    natw1 = [sb.alloc(f"natw1_{i}", [128, E], BF16) for i in range(2)]
    natw1_s = [Slot(p, dma=True) for _ in range(2)]
    xnt_s = [Slot(p, dma=True), Slot(p, dma=True)]

    p.wait("sy", "t", AB_T)
    p.wait("sy", "v", KVB_V)
    p.wait("sy", "c", p.val("c"))
    p.wait("sy", wqbf_s.sem, wqbf_s.ready())
    p.wait("c", "t", AB_T)
    p.wait("c", "v", AB_V)
    p.wait("c", cst.sem, CST_ALL)
    phase_barrier([qnat_s, qst_s[0], qst_s[1]])
    phase_barrier([xnsp_s, xbuf_s[0], xbuf_s[1], natw_s[0], natw_s[1], bias_s, wpf_s])

    transpose_weight(wq_bf, wqT, E, E, natw1, natw1_s, f32=False)
    p.wait("c", "v", KVB_V)
    p.wait("t", "c", p.val("c"))

    for tile in range(NTILE):
        slot = tile % 2
        xnt_s[slot].acquire("sy")
        p.wait("sy", xnsp_s.sem, XNSP_ALL)
        for dcc in range(DC):
            xnt_s[slot].dma("sy", lambda e, dcc=dcc, tile=tile, slot=slot: e.dma_start(
                out=xnt[slot][:, dcc * 512:(dcc + 1) * 512],
                in_=xn_sp[dcc * 128:(dcc + 1) * 128, tile * 512:(tile + 1) * 512]))
        xnt_s[slot].wait_ready("t")
        for oc in range(DC):
            si = oc % 4
            psl = pmm[si]
            pmm_s[si].acquire("t")
            for dcc in range(DC):
                p.op("t", lambda e, dcc=dcc, oc=oc, slot=slot, psl=psl: e.matmul(
                    psl[:, :],
                    lhsT=wqT[:, dcc * E + oc * 128: dcc * E + (oc + 1) * 128],
                    rhs=xnt[slot][:, dcc * 512:(dcc + 1) * 512],
                    start=(dcc == 0), stop=(dcc == DC - 1)),
                    inc=("t" if dcc == DC - 1 else None))
            p.wait("c", "t", p.val("t"))
            p.op("c", lambda e, psl=psl, oc=oc, tile=tile: e.activation(
                q_own[:, oc * E + tile * 512: oc * E + (tile + 1) * 512],
                psl[:, :], AF.Identity, bias=bq_sb[:, oc:oc + 1], scale=1.0), inc="c")
            pmm_s[si].released("c")
        xnt_s[slot].released("t")
    C1_C = p.val("c")
    C1_T = p.val("t")

    if "dbg_q" in dbg_aps:
        p.wait("sy", "c", C1_C)
        dbg_s.dma("sy", lambda e: e.dma_start(out=dbg_aps["dbg_q"][:, :], in_=q_own[:, :]))

    sb.pop()

    sb.push()
    assert sb.off == R1_off
    a_own = sb.alloc("a_own", [128, DC * E], BF16)
    assert sb.off == R2_off
    sb.off = R2_off + DC * E * 2
    q_own2 = q_own

    p.wait("t", "v", KVB_V)
    p.wait("t", "c", C1_C)
    for tile in range(NTILE):
        for h in range(H):
            si = (tile * H + h) % 4
            psl = pmm[si]
            pmm_s[si].acquire("t")
            p.op("t", lambda e, h=h, tile=tile, psl=psl: e.matmul(
                psl[:, :],
                lhsT=kvb[:, h * 128:(h + 1) * 128],
                rhs=q_own2[:, h * E + tile * 512: h * E + (tile + 1) * 512],
                start=True, stop=True), inc="t")
            p.wait("c", "t", p.val("t"))
            p.op("c", lambda e, psl=psl, h=h, tile=tile: e.copy(
                a_own[:, h * E + tile * 512: h * E + (tile + 1) * 512], psl[:, :]), inc="c")
            pmm_s[si].released("c")
    C2_V = p.val("c")

    if "dbg_attn" in dbg_aps:
        p.wait("sy", "c", C2_V)
        dbg_s.dma("sy", lambda e: e.dma_start(out=dbg_aps["dbg_attn"][:, :], in_=a_own[:, :]))

    sb.off = R2_off
    woT = sb.alloc("woT", [128, DC * E], BF16)
    bo_b = sb.alloc("bo_b", [128, E], F32)
    xchunk = [sb.alloc("xchunk0", [128, E], F32)]
    stg_a = [sb.alloc(f"stg_a{i}", [128, 512], F32) for i in range(4)]
    pstage = sb.alloc("pstage0", [128, E], F32)
    natw2 = [sb.alloc(f"natw2_{i}", [128, E], BF16) for i in range(2)]
    natw2_s = [Slot(p, dma=True), Slot(p, dma=True)]
    xchunk_s = [Slot(p, dma=True)]
    pstg_s = Slot(p, dma=True)

    p.wait("g", "t", p.val("t"))
    p.wait("sy", "t", C1_T)
    phase_barrier([xnt_s[0], xnt_s[1], natw1_s[0], natw1_s[1]])
    bias_s.dma("sy", lambda e: e.dma_start(out=bo_b[:, :], in_=bo_d[:, :].broadcast_to([128, E])))
    BO_ALL = bias_s.ready()
    p.wait("sy", wobf_s.sem, wobf_s.ready())
    transpose_weight(wo_bf, woT, E, E, natw2, natw2_s, f32=False)
    p.wait("g", natw2_s[0].sem, natw2_s[0].ready())
    p.wait("g", natw2_s[1].sem, natw2_s[1].ready())
    AGBASE = p.val(agq_s.sem)
    emit_gu_slice_trigger(0)
    p.wait("t", "c", p.val("c"))
    p.wait("v", bias_s.sem, BO_ALL)

    for tkc in range(NTKC):
        slot = 0
        xchunk_s[slot].acquire("sy")
        xchunk_s[slot].dma("sy", lambda e, tkc=tkc, slot=slot: e.dma_start(
            out=xchunk[slot][:, :], in_=x_all[tkc * 128:(tkc + 1) * 128, :]))
        for ot in range(4):
            pmm_s[ot].acquire("t")
            for dcc in range(DC):
                p.op("t", lambda e, dcc=dcc, tkc=tkc, ot=ot: e.matmul(
                    pmm[ot][:, :],
                    lhsT=a_own[:, dcc * E + tkc * 128: dcc * E + (tkc + 1) * 128],
                    rhs=woT[:, dcc * E + ot * 512: dcc * E + (ot + 1) * 512],
                    start=(dcc == 0), stop=(dcc == DC - 1)),
                    inc=("t" if dcc == DC - 1 else None))
        p.wait("v", "t", p.val("t"))
        xchunk_s[slot].wait_ready("v")
        pstg_s.acquire("v")
        for ot in range(4):
            p.op("v", lambda e, ot=ot: e.tensor_tensor(
                stg_a[ot][:, :], pmm[ot][:, :], bo_b[:, ot * 512:(ot + 1) * 512], ALU.add),
                inc=("v" if ot == 3 else None))
        for ot in range(4):
            pmm_s[ot].released("v")
        vbubble()
        for ot in range(4):
            p.op("v", lambda e, ot=ot, slot=slot: e.tensor_tensor(
                pstage[:, ot * 512:(ot + 1) * 512], stg_a[ot][:, :],
                xchunk[slot][:, ot * 512:(ot + 1) * 512], ALU.add),
                inc=("v" if ot == 3 else None))
        xchunk_s[slot].released("v")
        p.wait("sy", "v", p.val("v"))
        pstg_s.dma("sy", lambda e, tkc=tkc: e.dma_start(
            out=part_d[tkc * 128:(tkc + 1) * 128, :], in_=pstage[:, :]))
        pstg_s.released_dma()
    PSTG_ALL = pstg_s.ready()
    C3_T = p.val("t")

    if "dbg_part" in dbg_aps:
        p.wait("sy", pstg_s.sem, PSTG_ALL)
        dbg_s.dma("sy", lambda e: e.dma_start(out=dbg_aps["dbg_part"][:, :], in_=part_d[:, :]))

    sb.pop()

    sb.push()
    xn_own = sb.alloc("xn_own", [128, DC * TOK], BF16)
    wgT = [sb.alloc(f"wgT{i}", [128, E], BF16) for i in range(2)]
    wuT = [sb.alloc(f"wuT{i}", [128, E], BF16) for i in range(2)]
    sil = [sb.alloc(f"sil{i}", [128, 512], F32) for i in range(2)]
    h_stb = [sb.alloc(f"h_st{i}", [128, 512], BF16) for i in range(4)]
    hsq_stb = [sb.alloc(f"hsq_st{i}", [128, 512], BF16) for i in range(4)]

    gu_nat_s = [Slot(p, dma=True), Slot(p, dma=True)]
    xno_s = Slot(p, dma=True)
    hst_s = [Slot(p, dma=True) for _ in range(4)]
    wT_s = [Slot(p), Slot(p)]
    sil_s = [Slot(p), Slot(p)]
    hsq_s = [Slot(p) for _ in range(4)]

    p.wait("g", "t", C3_T)
    p.wait("g", "v", p.val("v"))
    phase_barrier([pstg_s, xchunk_s[0], natw2_s[0], natw2_s[1], bias_s])
    p.wait("sy", xnsp_s.sem, XNSP_ALL)
    for dcc in range(DC):
        xno_s.dma("sy", lambda e, dcc=dcc: e.dma_start(
            out=xn_own[:, dcc * TOK:(dcc + 1) * TOK],
            in_=xn_sp[dcc * 128:(dcc + 1) * 128, :]))
    xno_s.wait_ready("t")
    p.wait("t", "t", C3_T)

    ss_started = [False]
    ss_last_t = [0]
    WT_READY = {}

    SH = GUP // 8

    def _guT_src(h_global):
        core, ch = divmod(h_global, 16)
        return bass.AP(agu_out.tensor,
                       (ch // 4) * (8 * E * 512) + core * (E * 512) + (ch % 4) * 128,
                       [[512, 128], [128 * 512, DC], [1, 128]])

    def emit_gu_load_quant(pc):
        slot = pc % 2
        wT_s[slot].acquire("sy")
        gu_nat_s[slot].acquire("sy")
        gu_nat_s[slot].dma("sy", lambda e, pc=pc, slot=slot: e.dma_start(
            out=wgT[slot][:, :].rearrange("p (c j) -> p c j", j=128),
            in_=_guT_src(pc)))
        gu_nat_s[slot].dma("sy", lambda e, pc=pc, slot=slot: e.dma_start(
            out=wuT[slot][:, :].rearrange("p (c j) -> p c j", j=128),
            in_=_guT_src(64 + pc)))
        WT_READY[pc] = gu_nat_s[slot].ready()

    def emit_gu_mm(qc, is_last=False):
        slot = qc % 2
        hsq_counts = []
        p.wait("t", gu_nat_s[slot].sem, WT_READY[qc])
        for tile in range(4):
            sig, siu = tile % 2, 2 + tile % 2
            psg, psu = pmm[sig], pmm[siu]
            pmm_s[sig].acquire("t")
            pmm_s[siu].acquire("t")
            for dcc in range(DC):
                p.op("t", lambda e, dcc=dcc, tile=tile, psg=psg, slot=slot: e.matmul(
                    psg[:, :],
                    lhsT=wgT[slot][:, dcc * 128:(dcc + 1) * 128],
                    rhs=xn_own[:, dcc * TOK + tile * 512: dcc * TOK + (tile + 1) * 512],
                    start=(dcc == 0), stop=(dcc == DC - 1)),
                    inc=("t" if dcc == DC - 1 else None))
            TG = p.val("t")
            for dcc in range(DC):
                p.op("t", lambda e, dcc=dcc, tile=tile, psu=psu, slot=slot: e.matmul(
                    psu[:, :],
                    lhsT=wuT[slot][:, dcc * 128:(dcc + 1) * 128],
                    rhs=xn_own[:, dcc * TOK + tile * 512: dcc * TOK + (tile + 1) * 512],
                    start=(dcc == 0), stop=(dcc == DC - 1)),
                    inc=("t" if dcc == DC - 1 else None))
            TU = p.val("t")
            p.wait("c", "t", TG)
            sil_s[tile % 2].acquire("c")
            p.op("c", lambda e, psg=psg, tile=tile: e.activation(
                sil[tile % 2][:, :], psg[:, :], AF.Silu), inc="c")
            SIL_C = p.val("c")
            pmm_s[sig].released("c")
            hslot = tile
            p.wait("v", "t", TU)
            p.wait("v", "c", SIL_C)
            h_s_cur = hst_s[hslot]
            h_s_cur.acquire("v")
            p.op("v", lambda e, tile=tile, psu=psu, hslot=hslot: e.tensor_tensor(
                h_stb[hslot][:, :], sil[tile % 2][:, :], psu[:, :], ALU.mult), inc="v")
            HB_V = p.val("v")
            sil_s[tile % 2].released("v")
            pmm_s[siu].released("v")
            vbubble()
            hsq_s[tile].acquire("v")
            p.op("v", lambda e, hslot=hslot, tile=tile: e.tensor_tensor(
                hsq_stb[tile][:, :], h_stb[hslot][:, :], h_stb[hslot][:, :], ALU.mult), inc="v")
            hsq_counts.append(p.val("v"))
            p.wait("sy", "v", HB_V)
            h_s_cur.dma("sy", lambda e, qc=qc, tile=tile, hslot=hslot: e.dma_start(
                out=hsp[qc * 128:(qc + 1) * 128, tile * 512:(tile + 1) * 512],
                in_=h_stb[hslot][:, :]))
            h_s_cur.released_dma()
        for tile in range(4):
            p.wait("t", "v", hsq_counts[tile])
            first = not ss_started[0]
            ss_started[0] = True
            p.op("t", lambda e, tile=tile, first=first, last=(is_last and tile == 3): e.matmul(
                pkv[0:4, 512:1024],
                lhsT=sel4_sb[:, 4 * tile:4 * (tile + 1)],
                rhs=hsq_stb[tile][:, :],
                start=first, stop=last), inc="t")
            ss_last_t[0] = p.val("t")
            hsq_s[tile].released("t")
        wT_s[slot].released("t")

    qorder = [16 * bp + 4 * s + i for s in range(4) for bp in range(4) for i in range(4)]
    for k in range(65):
        if k < 64:
            if k % 16 == 0:
                p.wait("sy", agq_s.sem, AGBASE + k // 16 + 1)
            emit_gu_load_quant(qorder[k])
        if k in (6, 18, 34):
            p.wait("g", "t", p.val("t"))
            emit_gu_slice_trigger(k // 16 + 1)
        if k in (40, 52):
            p.wait("g", "t", p.val("t"))
            emit_p_trigger(0 if k == 40 else 1)
            AGW_DONE = p.val(agq_s.sem)
        if k >= 1:
            emit_gu_mm(qorder[k - 1], is_last=(k == 64))

    HST_ALL = [s.ready() for s in hst_s]

    if "dbg_h" in dbg_aps:
        for s, vtot in zip(hst_s, HST_ALL):
            p.wait("sy", s.sem, vtot)
        dbg_s.dma("sy", lambda e: e.dma_start(out=dbg_aps["dbg_h"][:, :], in_=hsp[:, :]))

    ssd_s = Slot(p, dma=True)
    invr_s = Slot(p, dma=True)
    p.wait("v", "t", ss_last_t[0])
    p.op("v", lambda e: e.tensor_copy(ssq[:, :], pkv[0:4, 512:1024]), inc="v")
    p.wait("sy", "v", p.val("v"))
    ssd_s.dma("sy", lambda e: e.dma_start(out=ss_d[:, :], in_=ssq[:, :]))
    ssd_s.wait_ready("sy")
    invr_s.dma("sy", lambda e: e.dma_start(
        out=inv_raw[:, :], in_=bass.AP(ss_d.tensor, 0, [[1, 128], [128, NTKC]]),
        allow_slow_non_contiguous=True))
    invr_s.wait_ready("v")
    p.op("v", lambda e: e.tensor_scalar(inv_raw[:, :], inv_raw[:, :], 1.0 / HID, EPS_RMS, ALU.mult, ALU.add))
    vbubble()
    p.op("v", lambda e: e.reciprocal(inv_raw[:, :], inv_raw[:, :]), inc="v")
    p.wait("c", "v", p.val("v"))
    p.op("c", lambda e: e.activation(inv_tm[:, :], inv_raw[:, :], AF.Sqrt), inc="c")
    INV_C = p.val("c")
    if "dbg_inv" in dbg_aps:
        p.wait("sy", "c", INV_C)
        dbg_s.dma("sy", lambda e: e.dma_start(out=dbg_aps["dbg_inv"][:, :], in_=inv_tm[:, :]))

    sb.pop()

    sb.push()
    wpT = [sb.alloc(f"wpT{i}", [128, 64 * 256], BF16) for i in range(2)]
    hpiece = [sb.alloc(f"hpiece{i}", [128, 64 * 256], BF16) for i in range(2)]
    pchunk = [sb.alloc(f"pchunk{i}", [128, 256], F32) for i in range(2)]
    stg1 = [sb.alloc(f"stg1{i}", [128, 256], F32) for i in range(2)]
    ostg = [sb.alloc(f"ostg{i}", [128, 256], F32) for i in range(2)]

    wpTl_s = [Slot(p, dma=True), Slot(p, dma=True)]
    hp_s = [Slot(p, dma=True), Slot(p, dma=True)]
    pch_s = [Slot(p, dma=True), Slot(p, dma=True)]
    ost_s = [Slot(p, dma=True), Slot(p, dma=True)]
    wpT_s = [Slot(p), Slot(p)]
    stg1_s = [Slot(p), Slot(p)]

    D_T_END = p.val("t")
    D_V_END = p.val("v")
    p.wait("g", "t", D_T_END)
    p.wait("g", "v", D_V_END)
    p.wait("c", "t", D_T_END)
    p.wait("c", "v", D_V_END)
    p.wait("sy", "t", D_T_END)
    p.wait("sy", "v", D_V_END)
    phase_barrier([gu_nat_s[0], gu_nat_s[1], hst_s[0], hst_s[1], hst_s[2],
                   hst_s[3], xno_s, ssd_s, invr_s])
    for s_, vtot in zip(hst_s, HST_ALL):
        p.wait("c", s_.sem, vtot)
    p.wait("sy", pstg_s.sem, PSTG_ALL)

    p.wait("sy", agq_s.sem, AGW_DONE)
    WPT_READY = {}

    def emit_wpT_load(dt_):
        dslot = dt_ % 2
        wpT_s[dslot].acquire("sy")
        wpTl_s[dslot].acquire("sy")
        for h in range(2):
            wpTl_s[dslot].dma("sy", lambda e, dt_=dt_, dslot=dslot, h=h: e.dma_start(
                out=wpT[dslot][:, h * 32 * 256:(h + 1) * 32 * 256].rearrange(
                    "p (c j) -> p c j", j=256),
                in_=bass.AP(ap_out.tensor,
                            h * 8 * (HID // 2) * (E // 8) + dt_ * (HID // 2) * (E // 8),
                            [[E // 8, 128], [128 * (E // 8), 32], [1, 256]])))
        WPT_READY[dt_] = wpTl_s[dslot].ready()

    hp_loaded = [-1, -1]

    def emit_hp_load(pi):
        hsl = pi % 2
        if hp_loaded[hsl] == pi:
            return
        hp_loaded[hsl] = pi
        hp_s[hsl].acquire("c")
        hp_s[hsl].dma("c", lambda e, pi=pi, hsl=hsl: e.dma_start(
            out=bass.AP(hpiece[hsl], 0, [[hpiece[hsl].shape[1], 128], [256, 64], [1, 256]]),
            in_=bass.AP(hsp.tensor, pi * 256, [[TOK, 128], [128 * TOK, 64], [1, 256]])))

    def emit_proj_pi(dt_, k):
        order = list(range(8)) if dt_ % 2 == 0 else list(range(7, -1, -1))
        pi = order[k]
        dslot = dt_ % 2
        if k == 0:
            emit_hp_load(pi)
            p.wait("t", wpTl_s[dslot].sem, WPT_READY[dt_])
        if k < 7:
            emit_hp_load(order[k + 1])
        hsl = pi % 2
        for sub in range(2):
            tkc = pi * 2 + sub
            si = tkc % 4
            pmm_s[si].acquire("t")
            hp_s[hsl].wait_ready("t")
            for hc in range(64):
                p.op("t", lambda e, hc=hc, sub=sub, si=si, hsl=hsl, dslot=dslot: e.matmul(
                    pmm[si][:, 0:256],
                    lhsT=hpiece[hsl][:, hc * 256 + sub * 128: hc * 256 + (sub + 1) * 128],
                    rhs=wpT[dslot][:, hc * 256:(hc + 1) * 256],
                    start=(hc == 0), stop=(hc == 63)),
                    inc=("t" if hc == 63 else None))
            T_PE = p.val("t")
            pch_s[sub].acquire("sy")
            pch_s[sub].dma("sy", lambda e, tkc=tkc, dt_=dt_, sub=sub: e.dma_start(
                out=pchunk[sub][:, :],
                in_=part_d[tkc * 128:(tkc + 1) * 128, dt_ * 256:(dt_ + 1) * 256]))
            p.wait("c", "t", T_PE)
            stg1_s[sub].acquire("c")
            p.op("c", lambda e, si=si, tkc=tkc, sub=sub: e.activation(
                stg1[sub][:, :], pmm[si][:, 0:256], AF.Identity,
                bias=0.0, scale=inv_tm[:, tkc:tkc + 1]), inc="c")
            STG1_C = p.val("c")
            pmm_s[si].released("c")
            p.wait("v", "c", STG1_C)
            pch_s[sub].wait_ready("v")
            ost_s[sub].acquire("v")
            p.op("v", lambda e, sub=sub: e.tensor_tensor(
                ostg[sub][:, :], stg1[sub][:, :], pchunk[sub][:, :], ALU.add), inc="v")
            stg1_s[sub].released("v")
            pch_s[sub].released("v")
            p.wait("sy", "v", p.val("v"))
            ost_s[sub].dma("sy", lambda e, tkc=tkc, dt_=dt_, sub=sub: e.dma_start(
                out=out_d[tkc * 128:(tkc + 1) * 128, dt_ * 256:(dt_ + 1) * 256],
                in_=ostg[sub][:, :]))
            ost_s[sub].released_dma()
        hp_s[hsl].released("t")
        if k == 7:
            wpT_s[dslot].released("t")
            wpTl_s[dslot].released("t")

    for dt_ in range(9):
        if dt_ < 8:
            emit_wpT_load(dt_)
        if dt_ >= 1:
            for k in range(8):
                emit_proj_pi(dt_ - 1, k)

    for s_ in ost_s:
        s_.wait_ready("sy")
    if dbg_s is not None:
        dbg_s.wait_ready("sy")
    sb.pop()

    p.finalize()
    return nc


_NC_CACHE = {}


def _get_nc(dbg=()):
    key = tuple(sorted(dbg))
    if key not in _NC_CACHE:
        _NC_CACHE[key] = build_program(dbg)
    return _NC_CACHE[key]


def make_in_maps(inputs):
    x = np.ascontiguousarray(np.asarray(inputs["L0_attn_x"], dtype=np.float32))
    ws = {k: np.ascontiguousarray(np.asarray(inputs[k], dtype=np.float32))
          for k in ("L0_attn_Wq", "L0_attn_Wk", "L0_attn_Wv", "L0_attn_Wo",
                    "L0_ffn_w_gate_up", "L0_ffn_w_proj")}
    bs = {k: np.ascontiguousarray(np.asarray(inputs[k], dtype=np.float32))
          for k in ("L0_attn_bq", "L0_attn_bk", "L0_attn_bv", "L0_attn_bo")}
    in_maps = []
    EH = E // 2
    for c in range(8):
        b, half = c // 2, c % 2
        own = x[b, half * TOK:(half + 1) * TOK, :]
        other = x[b, (1 - half) * TOK:(2 - half) * TOK, :]
        x_core = np.concatenate([own, other], axis=0)
        in_maps.append({
            "x_all": x_core,
            "Wq": ws["L0_attn_Wq"],
            "Wk_sh": np.ascontiguousarray(ws["L0_attn_Wk"][half * EH:(half + 1) * EH]),
            "Wv_sh": np.ascontiguousarray(ws["L0_attn_Wv"][half * EH:(half + 1) * EH]),
            "Wo": ws["L0_attn_Wo"],
            "bq": bs["L0_attn_bq"].reshape(E, 1),
            "bk": np.ascontiguousarray(bs["L0_attn_bk"].reshape(1, E)[:, half * EH:(half + 1) * EH]),
            "bv": np.ascontiguousarray(bs["L0_attn_bv"].reshape(1, E)[:, half * EH:(half + 1) * EH]),
            "bo": bs["L0_attn_bo"].reshape(1, E),
            "wgu_sh": np.ascontiguousarray(ws["L0_ffn_w_gate_up"][c * (GUP // 8):(c + 1) * (GUP // 8)]),
            "wp_sh": np.ascontiguousarray(ws["L0_ffn_w_proj"][c * (E // 8):(c + 1) * (E // 8)]),
        })
    return in_maps


def kernel(**inputs):
    nc = _get_nc()
    in_maps = make_in_maps(inputs)
    res = run_bass_kernel_spmd(nc, in_maps, core_ids=list(range(8)))
    out = np.empty((B, T, E), dtype=np.float32)
    for c in range(8):
        b, half = c // 2, c % 2
        out[b, half * TOK:(half + 1) * TOK, :] = np.asarray(res.results[c]["out"])
    return out
